# version 10
# baseline (speedup 1.0000x reference)
import numpy as np
import ml_dtypes
from contextlib import ExitStack

import concourse.bass as bass
import concourse.mybir as mybir
from concourse.bass_utils import run_bass_kernel_spmd

F32 = mybir.dt.float32
BF16 = mybir.dt.bfloat16
I32 = mybir.dt.int32
AF = mybir.ActivationFunctionType
ALU = mybir.AluOpType
AX = mybir.AxisListType.X
BF = ml_dtypes.bfloat16

D = 4096
TOK = 1024
NT = 8
KC = 32
SCL = float(1.0 / np.sqrt(128.0))
NEG = -1.0e30
KPP = 4
OPP = 256
NJ = KC // KPP

ENGINES = ("sync", "scalar", "vector", "tensor", "gpsimd")


class Sem:
    def __init__(self, h):
        self.h = h
        self.n = 0

    def inc(self, d=1):
        self.n += d
        return (self.h, d)

    def at(self):
        return (self.h, self.n)


class Prog:
    def __init__(self):
        self.ops = []
        self.last = {e: None for e in ENGINES}

    def op(self, eng, method, *args, wait=(), inc=(), **kwargs):
        rec = [eng, method, args, kwargs, list(wait), list(inc)]
        self.ops.append(rec)
        if method is not None:
            self.last[eng] = rec
        return rec

    def wait(self, eng, *wait):
        self.ops.append([eng, None, (), {}, list(wait), []])

    def fence(self, eng, sem):
        rec = self.last[eng]
        assert rec is not None
        rec[5].append(sem.inc())
        self.wait(eng, sem.at())

    def t(self, method, *a, **k):
        return self.op("tensor", method, *a, **k)

    def v(self, method, *a, **k):
        return self.op("vector", method, *a, **k)

    def s(self, method, *a, **k):
        return self.op("scalar", method, *a, **k)

    def g(self, method, *a, **k):
        return self.op("gpsimd", method, *a, **k)

    def y(self, method, *a, **k):
        return self.op("sync", method, *a, **k)

    def replay(self, block):
        def run(engobj, name):
            for eng, method, args, kwargs, waits, incs in self.ops:
                if eng != name:
                    continue
                for (h, val) in waits:
                    engobj.wait_ge(h, val)
                inst = None
                if method is not None:
                    inst = getattr(engobj, method)(*args, **kwargs)
                if incs:
                    if inst is None:
                        inst = engobj.nop()
                    inst.then_inc(*incs[0])
                    for (h, d) in incs[1:]:
                        engobj.nop().then_inc(h, d)

        block.sync(lambda e: run(e, "sync"))
        block.scalar(lambda e: run(e, "scalar"))
        block.vector(lambda e: run(e, "vector"))
        block.tensor(lambda e: run(e, "tensor"))
        block.gpsimd(lambda e: run(e, "gpsimd"))


class Ctx:
    def __init__(self, nc, es):
        self.nc = nc
        self.es = es
        self.p = Prog()
        self.sem = {}

    def sb(self, name, shape, dtype):
        return self.es.enter_context(self.nc.sbuf_tensor(name, shape, dtype))

    def psum(self, name, shape, dtype):
        return self.es.enter_context(self.nc.psum_tensor(name, shape, dtype))

    def mksem(self, *names):
        for n in names:
            h = self.es.enter_context(self.nc.semaphore(n))
            self.sem[n] = Sem(h)




def arrange_pieces(Wt, opp=OPP, kpp=KPP):
    K, O = Wt.shape
    a = Wt.reshape(K // (128 * kpp), kpp, 128, O // opp, opp)
    a = a.transpose(3, 0, 2, 1, 4)
    return np.ascontiguousarray(
        a.reshape(O // opp, K // (128 * kpp), 128, kpp * opp))


def cols128(v):
    n = v.shape[0] // 128
    return np.ascontiguousarray(v.reshape(n, 128).T)


def rope_tables(S=4096, hd=128, theta=10000.0):
    inv = 1.0 / theta ** (np.arange(0, hd, 2, dtype=np.float64) / hd)
    t = np.arange(S, dtype=np.float64)
    freqs = np.outer(t, inv)
    emb = np.concatenate([freqs, freqs], axis=1)
    cos = np.ascontiguousarray(np.cos(emb).astype(np.float32).T)
    sin = np.ascontiguousarray(np.sin(emb).astype(np.float32).T)
    return cos, sin


def tri_mask():
    m = np.where(np.tril(np.ones((128, 128), dtype=bool)), 0.0, NEG)
    return np.ascontiguousarray(m.astype(np.float32))


def e_matrix():
    E = np.zeros((8, 128), dtype=np.float32)
    for e in range(8):
        E[e, 16 * e:16 * (e + 1)] = 2.0
    return np.ascontiguousarray(E)




def common_tensors(ctx, stream=True, big=True):
    c = ctx
    if stream:
        c.sx = c.sb("sx", [128, NT * D], F32)
    if big:
        c.big = c.sb("big", [128, KC * TOK], BF16)
        c.xnT = c.big
    c.scr = c.sb("scr", [128, 4096], BF16)
    c.wbuf = c.sb("wbuf", [128, 2048], BF16)
    c.cst = c.sb("cst", [128, 512], F32)
    c.psA = c.psum("psA", [128, 4096], F32)
    c.mksem("sDW", "sDX", "sDS", "sT", "sV", "sS", "sG",
            "sW0", "sW1", "sB0", "sB1")
    c.stat = c.cst.ap()[:, 0:16]
    c.b05 = c.cst.ap()[:, 16:48]
    c.s05 = c.cst.ap()[:, 48:80]
    c.identf = c.cst.ap()[:, 80:208]
    c.l8 = c.cst.ap()[:, 208:216]
    c.r8 = c.cst.ap()[:, 216:232]
    c.msk = c.cst.ap()[:, 232:240]
    c.trimask = c.cst.ap()[:, 240:368]
    c.eps = c.cst.ap()[:, 368:369]
    cb = c.cst.ap().bitcast(BF16)
    c.zero128 = cb[:, 480:608]
    c.zs05 = c.cst.ap()[:, 304:336]
    c.rs8 = c.cst.ap()[:, 336:344]
    c.cb8 = cb[:, 744:752]
    c.sb_E = cb[0:8, 752:880]
    c.ident = cb[:, 880:1008]
    c.scrf = c.scr.ap().bitcast(F32)


def psb(ctx):
    return ctx.psA.ap().bitcast(BF16)


def pe_flush(ctx, region):
    return ctx.p.t("matmul", region, ctx.ident, ctx.zero128,
                   start=False, stop=True)


def emit_const_loads(ctx, consts, gp=()):
    p, s = ctx.p, ctx.sem
    for dst, src in consts:
        p.y("dma_start", out=dst, in_=src, inc=[s["sDX"].inc(16)])
    for dst, src in gp:
        p.g("dma_start", out=dst, in_=src, inc=[s["sDW"].inc(16)])


def emit_stream_load(ctx, x_ap):
    p, s = ctx.p, ctx.sem
    for t in range(NT):
        p.y("dma_start", out=ctx.sx[:, t * D:(t + 1) * D],
            in_=x_ap[t * 128:(t + 1) * 128, :], inc=[s["sDX"].inc(16)])


def emit_stream_store(ctx, out_ap):
    p, s = ctx.p, ctx.sem
    p.wait("sync", s["sV"].at(), s["sS"].at())
    for t in range(NT):
        p.y("dma_start", out=out_ap[t * 128:(t + 1) * 128, :],
            in_=ctx.sx[:, t * D:(t + 1) * D], inc=[s["sDS"].inc(16)])
    p.wait("sync", s["sDS"].at())


def emit_ln(ctx, rowsum=False):
    c, p, s = ctx, ctx.p, ctx.sem
    st = c.stat
    pb = psb(c)
    xn = c.scr.ap()[:, 0:4096]
    evac_done = {}
    for t in range(NT):
        xt = c.sx[:, t * D:(t + 1) * D]
        if t > 0:
            p.wait("vector", s["sT"].at())
        p.v("tensor_tensor", xn, xt, xt, ALU.mult)
        p.fence("vector", s["sV"])
        p.v("reduce_sum", st[:, 1:2], xn, axis=AX)
        p.v("reduce_sum", st[:, 0:1], xt, axis=AX)
        p.fence("vector", s["sV"])
        p.v("tensor_scalar", st[:, 2:3], st[:, 0:1], 1.0 / D, None, ALU.mult)
        p.fence("vector", s["sV"])
        p.v("tensor_tensor", st[:, 3:4], st[:, 2:3], st[:, 2:3], ALU.mult)
        p.fence("vector", s["sV"])
        p.v("tensor_scalar", st[:, 4:5], st[:, 1:2], 1.0 / D, st[:, 3:4],
            ALU.mult, ALU.subtract, inc=[s["sV"].inc()])
        p.wait("scalar", s["sV"].at())
        p.s("activation", st[:, 5:6], st[:, 4:5], AF.Ln, bias=c.eps, scale=1.0)
        p.fence("scalar", s["sS"])
        p.s("activation", st[:, 6:7], st[:, 5:6], AF.Exp, scale=-0.5,
            inc=[s["sS"].inc()])
        p.wait("vector", s["sS"].at())
        p.v("tensor_scalar", st[:, 7:8], st[:, 2:3], st[:, 6:7], -1.0,
            ALU.mult, ALU.mult)
        p.fence("vector", s["sV"])
        p.v("tensor_scalar", xn, xt, st[:, 6:7], st[:, 7:8],
            ALU.mult, ALU.add, inc=[s["sV"].inc()])
        if rowsum:
            p.fence("vector", s["sV"])
            p.v("reduce_sum", c.rs8[:, t:t + 1], xn, axis=AX,
                inc=[s["sV"].inc()])
        p.wait("tensor", s["sV"].at())
        for g in range(4):
            bank = 6 + (g % 2)
            bview = pb[:, bank * 1024:(bank + 1) * 1024]
            key = (t * 4 + g) - 2
            if key >= 0:
                p.wait("tensor", (s["sV"].h, evac_done[key]))
            for kk in range(8):
                kc = g * 8 + kk
                p.t("transpose", bview[:, kk * 128:(kk + 1) * 128],
                    xn[:, kc * 128:(kc + 1) * 128], c.ident,
                    inc=([s["sT"].inc()] if kk == 7 else []))
            p.wait("vector", s["sT"].at())
            dst = c.xnT[:].rearrange("p (k c) -> p k c", k=KC)[
                :, g * 8:(g + 1) * 8, t * 128:(t + 1) * 128]
            src = bview.rearrange("p (k c) -> p k c", k=8)
            p.v("tensor_copy", dst, src, inc=[s["sV"].inc()])
            evac_done[t * 4 + g] = s["sV"].n
    p.wait("tensor", s["sV"].at())


def emit_dense_fm(ctx, pieces_ap, n_out, tail, extra_mm=None, rank1=None):
    c, p, s = ctx, ctx.p, ctx.sem
    n_og = n_out // OPP
    acc = c.psA.ap()[:, 0:2048]
    wslot = [c.wbuf.ap()[:, 0:KPP * OPP],
             c.wbuf.ap()[:, 1024:1024 + KPP * OPP]]
    slot_free = [s["sT"].n, s["sT"].n]
    dma_done = {}

    wsem = [s["sW0"], s["sW1"]]

    def issue_piece(idx):
        og_, j_ = idx // NJ, idx % NJ
        sl_ = 0 if SERIAL_W else idx % 2
        p.wait("gpsimd", (s["sT"].h, slot_free[sl_]))
        p.g("dma_start", out=wslot[sl_], in_=pieces_ap(og_, j_),
            inc=[wsem[sl_].inc(16)])
        dma_done[idx] = wsem[sl_].at()

    if not SERIAL_W:
        issue_piece(0)
        issue_piece(1)
    for og in range(n_og):
        for kc in range(KC):
            j, ks = kc // KPP, kc % KPP
            idx = og * NJ + j
            sl = 0 if SERIAL_W else idx % 2
            if ks == 0:
                if SERIAL_W:
                    issue_piece(idx)
                p.wait("tensor", dma_done[idx])
            if kc == TRAIL_AT and getattr(c, "tail_trail", None) is not None:
                c.tail_trail()
                c.tail_trail = None
            for oc2 in range(2):
                lhsT = wslot[sl][:, ks * OPP + oc2 * 128: ks * OPP + (oc2 + 1) * 128]
                a = acc[:, oc2 * 1024:(oc2 + 1) * 1024]
                last = (kc == KC - 1) and (extra_mm is None)
                p.t("matmul", a[:, 0:512], lhsT,
                    c.xnT[:, kc * TOK: kc * TOK + 512],
                    start=(kc == 0), stop=False)
                rec = p.t("matmul", a[:, 512:1024], lhsT,
                          c.xnT[:, kc * TOK + 512: (kc + 1) * TOK],
                          start=(kc == 0), stop=(last and oc2 == 1))
                if ks == KPP - 1 and oc2 == 1:
                    rec[5].append(s["sT"].inc())
                    slot_free[sl] = s["sT"].n
                    if not SERIAL_W and idx + 2 < n_og * NJ:
                        issue_piece(idx + 2)
        if extra_mm is not None:
            extra_mm(og)
        if rank1 is not None:
            urow_fn, vrow_fn = rank1
            for oc2 in range(2):
                oc = og * 2 + oc2
                a = acc[:, oc2 * 1024:(oc2 + 1) * 1024]
                vr = vrow_fn(oc)
                p.t("matmul", a[:, 0:512], urow_fn(oc),
                    vr[:, 0:512], start=False, stop=False)
                p.t("matmul", a[:, 512:1024], urow_fn(oc),
                    vr[:, 512:1024], start=False, stop=True)
        fl = pe_flush(c, acc[:, 0:128])
        fl[5].append(s["sT"].inc())
        tail(og, acc, s["sT"].at())
        if getattr(c, "tail_evac_done", None) is not None:
            p.wait("tensor", *c.tail_evac_done)
            c.tail_evac_done = None
    if getattr(c, "tail_trail", None) is not None:
        c.tail_trail()
        c.tail_trail = None


def emit_ffn_tail(ctx, og, acc, acc_ready, s05, b05, dual_ev=True):
    c, p, s = ctx, ctx.p, ctx.sem
    pb = psb(c)
    ev0 = c.scr.ap()[:, 1024:2048]
    ev1 = c.scr.ap()[:, 3072:4096] if dual_ev else ev0
    evs = [ev0, ev1]
    if not hasattr(c, "tail_add_done"):
        c.tail_add_done = {}
        c.tail_n = 0
        c.tail_ev_free = {}
    ev_done = []

    def evac(oc2):
        oc = og * 2 + oc2
        eng = "scalar" if (dual_ev and oc2 == 0) else "vector"
        p.wait(eng, acc_ready)
        key = (dual_ev, oc2)
        if c.tail_ev_free.get(key) is not None:
            p.wait(eng, (s["sT"].h, c.tail_ev_free[key]))
        sc = s05 if isinstance(s05, float) else s05[:, oc:oc + 1]
        if eng == "scalar":
            p.s("activation", evs[oc2], acc[:, oc2 * 1024:(oc2 + 1) * 1024],
                AF.Identity, bias=b05[:, oc:oc + 1], scale=sc,
                inc=[s["sS"].inc()])
            ev_done.append(("s", s["sS"].n))
        else:
            p.v("tensor_scalar", evs[oc2], acc[:, oc2 * 1024:(oc2 + 1) * 1024],
                sc, b05[:, oc:oc + 1], ALU.mult, ALU.add, inc=[s["sV"].inc()])
            ev_done.append(("v", s["sV"].n))

    def trail(oc2):
        oc = og * 2 + oc2
        ev = evs[oc2]
        key = (dual_ev, oc2)
        eng, val = ev_done[oc2]
        p.wait("tensor", (s["sS" if eng == "s" else "sV"].h, val))
        for t in range(NT):
            bank = 4 + (c.tail_n % 2)
            bv = pb[:, bank * 1024: bank * 1024 + 128]
            k2 = c.tail_n - 2
            if k2 >= 0:
                p.wait("tensor", (s["sV"].h, c.tail_add_done[k2]))
            p.t("transpose", bv, ev[:, t * 128:(t + 1) * 128], c.ident,
                inc=[s["sT"].inc()])
            if t == NT - 1:
                c.tail_ev_free[key] = s["sT"].n
                if not dual_ev:
                    c.tail_ev_free[(dual_ev, 1 - oc2)] = s["sT"].n
            p.wait("vector", s["sT"].at())
            dst = c.sx[:, t * D + oc * 128: t * D + (oc + 1) * 128]
            p.v("scalar_tensor_tensor", out=dst, in0=bv, scalar=1.0,
                in1=dst, op0=ALU.mult, op1=ALU.add, inc=[s["sV"].inc()])
            c.tail_add_done[c.tail_n] = s["sV"].n
            c.tail_n += 1

    if dual_ev:
        evac(0)
        evac(1)
        c.tail_evac_done = [s["sV"].at(), s["sS"].at()]

        def trail_all():
            trail(0)
            trail(1)
        if DEFER_TRAIL:
            assert getattr(c, "tail_trail", None) is None, "trail dropped"
            c.tail_trail = trail_all
        else:
            trail_all()
    else:
        evac(0)
        trail(0)
        evac(1)
        c.tail_evac_done = [s["sV"].at()]
        if DEFER_TRAIL:
            assert getattr(c, "tail_trail", None) is None, "trail dropped"
            c.tail_trail = lambda: trail(1)
        else:
            trail(1)


def emit_router_lora(ctx, wr_ap, A_ap, B_ap):
    c, p, s = ctx, ctx.p, ctx.sem
    hs = c.scr.ap()[:, 0:1024]
    lg = c.scr.ap()[0:8, 2048:3072]
    cT = c.scr.ap()[0:8, 3072:4096]
    hf = c.scrf[:, 512:1024]
    l8, r8, msk = c.l8, c.r8, c.msk
    wrb = c.wbuf.ap()[:, 0:256]
    p.g("dma_start", out=wrb, in_=wr_ap, inc=[s["sW0"].inc(16)])
    wr_done = s["sW0"].at()

    lg_ps = c.psA.ap()[0:8, 0:1024]
    p.wait("tensor", wr_done)
    for kc in range(KC):
        lhsT = wrb[:, kc * 8:(kc + 1) * 8]
        p.t("matmul", lg_ps[:, 0:512], lhsT, c.xnT[:, kc * TOK:kc * TOK + 512],
            start=(kc == 0), stop=False)
        p.t("matmul", lg_ps[:, 512:1024], lhsT,
            c.xnT[:, kc * TOK + 512:(kc + 1) * TOK],
            start=(kc == 0), stop=(kc == KC - 1))
    fl = pe_flush(c, c.psA.ap()[:, 0:128])
    fl[5].append(s["sT"].inc())
    p.wait("vector", s["sT"].at())
    p.v("tensor_copy", lg, lg_ps, inc=[s["sV"].inc()])

    p.wait("gpsimd", s["sV"].at())
    aslot = [c.wbuf.ap()[:, 0:1024], c.wbuf.ap()[:, 1024:2048]]
    a_done = {}
    a_free = {}
    h_ps = c.psA.ap()[:, 0:1024]

    asem = [s["sW0"], s["sW1"]]

    def issue_a(ch_):
        sl_ = ch_ % 2
        if ch_ in (2, 3):
            p.wait("gpsimd", (s["sT"].h, a_free[ch_ - 2]))
        p.g("dma_start", out=aslot[sl_], in_=A_ap[ch_],
            inc=[asem[sl_].inc(16)])
        a_done[ch_] = asem[sl_].at()

    issue_a(0)
    issue_a(1)
    for kc in range(KC):
        ch = kc // 8
        sl = ch % 2
        if kc % 8 == 0:
            p.wait("tensor", a_done[ch])
        lhsT = aslot[sl][:, (kc % 8) * 128:(kc % 8 + 1) * 128]
        p.t("matmul", h_ps[:, 0:512], lhsT, c.xnT[:, kc * TOK:kc * TOK + 512],
            start=(kc == 0), stop=False)
        rec = p.t("matmul", h_ps[:, 512:1024], lhsT,
                  c.xnT[:, kc * TOK + 512:(kc + 1) * TOK],
                  start=(kc == 0), stop=(kc == KC - 1))
        if kc % 8 == 7:
            rec[5].append(s["sT"].inc())
            a_free[ch] = s["sT"].n
            if ch + 2 < 4:
                issue_a(ch + 2)
    fl = pe_flush(c, h_ps[:, 0:128])
    fl[5].append(s["sT"].inc())
    h_done = s["sT"].at()

    lgT_ps = psb(c)[:, 3072:3080]
    cbT_ps = psb(c)[0:8, 3584:3712]
    for t in range(NT):
        p.wait("tensor", s["sV"].at())
        p.t("transpose", lgT_ps, lg[:, t * 128:(t + 1) * 128],
            c.ident[0:8, 0:8], inc=[s["sT"].inc()])
        p.wait("vector", s["sT"].at())
        p.v("tensor_copy", l8, lgT_ps)
        p.fence("vector", s["sV"])
        p.v("reduce_max", r8[:, 0:1], l8, axis=AX)
        p.fence("vector", s["sV"])
        p.v("tensor_scalar", r8[:, 2:3], r8[:, 0:1], -1.0, None, ALU.mult)
        p.v("tensor_single_scalar", r8[:, 8:16], l8, r8[:, 0:1], ALU.is_equal)
        p.fence("vector", s["sV"])
        p.v("scalar_tensor_tensor", out=r8[:, 8:16], in0=r8[:, 8:16],
            scalar=NEG, in1=l8, op0=ALU.mult, op1=ALU.add)
        p.fence("vector", s["sV"])
        p.v("reduce_max", r8[:, 1:2], r8[:, 8:16], axis=AX)
        p.fence("vector", s["sV"])
        p.v("tensor_tensor", r8[:, 3:4], r8[:, 1:2], r8[:, 0:1], ALU.subtract,
            inc=[s["sV"].inc()])
        p.wait("scalar", s["sV"].at())
        p.s("activation", r8[:, 4:5], r8[:, 3:4], AF.Exp, inc=[s["sS"].inc()])
        p.wait("vector", s["sS"].at())
        p.v("tensor_scalar", r8[:, 5:6], r8[:, 4:5], 1.0, None, ALU.add)
        p.fence("vector", s["sV"])
        p.v("reciprocal", r8[:, 6:7], r8[:, 5:6], inc=[s["sV"].inc()])
        p.wait("scalar", s["sV"].at())
        p.s("activation", r8[:, 8:16], l8, AF.Exp, bias=r8[:, 2:3],
            scale=1.0, inc=[s["sS"].inc()])
        p.wait("vector", s["sS"].at())
        p.v("tensor_single_scalar", msk, l8, r8[:, 1:2], ALU.is_ge)
        p.fence("vector", s["sV"])
        p.v("tensor_scalar", r8[:, 8:16], r8[:, 8:16], r8[:, 6:7], None,
            ALU.mult)
        p.fence("vector", s["sV"])
        p.v("tensor_tensor", c.cb8, r8[:, 8:16], msk, ALU.mult,
            inc=[s["sV"].inc()])
        if t == 0 and getattr(c, "dbg_cst", None) is not None:
            p.wait("sync", s["sV"].at())
            p.y("dma_start", out=c.dbg_cst, in_=c.cst[:, :],
                inc=[s["sDS"].inc(16), s["sG"].inc()])
            p.wait("vector", s["sG"].at())
        p.wait("tensor", s["sV"].at())
        p.t("transpose", cbT_ps, c.cb8, c.ident, inc=[s["sT"].inc()])
        p.wait("vector", s["sT"].at())
        p.v("tensor_copy", cT[:, t * 128:(t + 1) * 128], cbT_ps,
            inc=[s["sV"].inc()])
    p.wait("tensor", s["sV"].at())
    srep_ps = c.psA.ap()[:, 1024:1536]
    for half in range(2):
        p.t("matmul", srep_ps, c.sb_E, cT[:, half * 512:(half + 1) * 512],
            start=True, stop=True)
        fl = pe_flush(c, srep_ps[:, 0:128])
        fl[5].append(s["sT"].inc())
        sr_done = s["sT"].at()
        p.wait("scalar", h_done)
        p.s("activation", hf, h_ps[:, half * 512:(half + 1) * 512], AF.Copy,
            inc=[s["sS"].inc()])
        p.wait("vector", sr_done, s["sS"].at())
        p.v("tensor_tensor", hs[:, half * 512:(half + 1) * 512], hf, srep_ps,
            ALU.mult, inc=[s["sV"].inc()])
        p.wait("tensor", s["sV"].at())
        p.wait("scalar", s["sV"].at())

    bslot = [c.scr.ap()[:, 2048:2304], c.scr.ap()[:, 2304:2560]]
    bfree = [None, None]
    bdone = {}

    bsem = [s["sB0"], s["sB1"]]

    def pre_og(og):
        sl = og % 2
        gate = bfree[sl] if bfree[sl] is not None else s["sT"].n
        p.wait("gpsimd", (s["sT"].h, gate))
        p.g("dma_start", out=bslot[sl], in_=B_ap[og], inc=[bsem[sl].inc(16)])
        bdone[og] = bsem[sl].at()

    def extra_mm(og):
        sl = og % 2
        p.wait("tensor", bdone[og])
        for oc2 in range(2):
            lhsT = bslot[sl][:, oc2 * 128:(oc2 + 1) * 128]
            a = c.psA.ap()[:, oc2 * 1024:(oc2 + 1) * 1024]
            p.t("matmul", a[:, 0:512], lhsT, hs[:, 0:512],
                start=False, stop=False)
            rec = p.t("matmul", a[:, 512:1024], lhsT, hs[:, 512:1024],
                      start=False, stop=(oc2 == 1))
            if oc2 == 1:
                rec[5].append(s["sT"].inc())
                bfree[sl] = s["sT"].n
        return rec
    return pre_og, extra_mm




DEBUG_FFN = False
NO_LORA = False
SERIAL_W = False
DEFER_TRAIL = True
TRAIL_AT = 4


def build_ffn_l0():
    es = ExitStack()
    nc = bass.Bass("TRN2", target_bir_lowering=False)
    x_in = nc.dram_tensor("x_in", [TOK, D], F32, kind="ExternalInput").ap()
    x_out = nc.dram_tensor("x_out", [TOK, D], F32, kind="ExternalOutput").ap()
    wb = nc.dram_tensor("wbase", [16, NJ, 128, KPP * OPP], F32,
                        kind="ExternalInput").ap()
    b_arr = nc.dram_tensor("b_arr", [128, 32], F32, kind="ExternalInput").ap()
    wr = nc.dram_tensor("wr", [128, 256], F32, kind="ExternalInput").ap()
    A_in = nc.dram_tensor("A_in", [4, 128, 1024], F32, kind="ExternalInput").ap()
    B_in = nc.dram_tensor("B_in", [16, 128, 256], F32, kind="ExternalInput").ap()
    E_in = nc.dram_tensor("E_in", [8, 128], F32, kind="ExternalInput").ap()
    id_in = nc.dram_tensor("id_in", [128, 128], F32, kind="ExternalInput").ap()

    ctx = Ctx(nc, es)
    common_tensors(ctx)
    c, p, s = ctx, ctx.p, ctx.sem
    emit_const_loads(ctx, [(c.b05, b_arr)],
                     gp=[(c.ident, id_in), (c.sb_E, E_in)])
    emit_stream_load(ctx, x_in)
    p.wait("vector", s["sDX"].at(), s["sDW"].at())
    p.v("memset", c.eps, 1e-5)
    p.v("memset", c.zero128, 0.0)
    p.v("tensor_scalar", c.b05, c.b05, 0.5, None, ALU.mult,
        inc=[s["sV"].inc()])
    if DEBUG_FFN:
        ctx.dbg_cst = nc.dram_tensor("dbg_cst", [128, 512], F32,
                                     kind="ExternalOutput").ap()
        dbg_xnT = nc.dram_tensor("dbg_xnT", [128, KC * TOK], BF16,
                                 kind="ExternalOutput").ap()
        dbg_lg = nc.dram_tensor("dbg_lg", [8, 1024], BF16,
                                kind="ExternalOutput").ap()
        dbg_cT = nc.dram_tensor("dbg_cT", [8, 1024], BF16,
                                kind="ExternalOutput").ap()
        dbg_hs = nc.dram_tensor("dbg_hs", [128, 1024], BF16,
                                kind="ExternalOutput").ap()
    emit_ln(ctx)
    if DEBUG_FFN:
        p.wait("sync", s["sV"].at())
        p.y("dma_start", out=dbg_xnT, in_=c.xnT[:, :], inc=[s["sDS"].inc(16)])
    pre_og, extra = emit_router_lora(ctx, wr, A_in, B_in)
    if DEBUG_FFN:
        p.wait("sync", s["sV"].at())
        p.y("dma_start", out=dbg_lg, in_=c.scr.ap()[0:8, 2048:3072], inc=[s["sDS"].inc(16)])
        p.y("dma_start", out=dbg_cT, in_=c.scr.ap()[0:8, 3072:4096], inc=[s["sDS"].inc(16)])
        p.y("dma_start", out=dbg_hs, in_=c.scr.ap()[:, 0:1024], inc=[s["sDS"].inc(16)])
        p.op("sync", None, inc=[s["sG"].inc()])
        p.wait("gpsimd", s["sG"].at())

    def tail(og, acc, rdy):
        emit_ffn_tail(ctx, og, acc, rdy, 0.5, c.b05)

    done_og = set()

    def pieces_ap(og, j):
        if og not in done_og:
            done_og.add(og)
            pre_og(og)
        return wb[og, j]

    emit_dense_fm(ctx, pieces_ap, D, tail,
                  extra_mm=(None if NO_LORA else extra))
    emit_stream_store(ctx, x_out)

    with nc.Block() as block:
        p.replay(block)
    es.close()
    return nc


def ffn_l0_inputs(inputs):
    wbT = np.ascontiguousarray(inputs["L0_ffn_W_base"].T)
    wb = arrange_pieces(wbT)
    b_arr = cols128(np.asarray(inputs["L0_ffn_b_base"]))
    wr = np.ascontiguousarray(
        np.asarray(inputs["L0_ffn_W_router"]).T.reshape(32, 128, 8)
        .transpose(1, 0, 2).reshape(128, 256))
    A = np.ascontiguousarray(
        np.asarray(inputs["L0_ffn_A_stack"]).transpose(2, 0, 1)
        .reshape(32, 128, 128).transpose(1, 0, 2).reshape(128, 4096))
    A = np.ascontiguousarray(A.reshape(128, 4, 1024).transpose(1, 0, 2))
    B = np.ascontiguousarray(
        np.asarray(inputs["L0_ffn_B_stack"]).transpose(0, 2, 1)
        .reshape(128, 4096).reshape(128, 16, 256).transpose(1, 0, 2))
    return {
        "wbase": wb, "b_arr": b_arr, "wr": wr, "A_in": A, "B_in": B,
        "E_in": e_matrix(), "id_in": np.eye(128, dtype=np.float32),
    }


def run_ffn_l0(xs, statics):
    nc = build_ffn_l0()
    in_maps = [dict(statics, x_in=xs[i]) for i in range(8)]
    res = run_bass_kernel_spmd(nc, in_maps, core_ids=list(range(8)))
    return [r["x_out"] for r in res.results]




def emit_qkv_tail(ctx, og, acc, acc_ready, out_ap, rope=False):
    c, p, s = ctx, ctx.p, ctx.sem
    evs = [c.scr.ap()[:, 1024:2048], c.scr.ap()[:, 2048:3072]]
    sems = [s["sB0"], s["sB1"]]
    for oc2 in range(2):
        oc = og * 2 + oc2
        sl = oc % 2
        ev = evs[sl]
        if rope:
            p.wait("vector", acc_ready, sems[sl].at())
            a = acc[:, oc2 * 1024:(oc2 + 1) * 1024]
            t1 = c.psA.ap()[:, 2048:2560]
            cosb = c.scr.ap()[:, 0:1024]
            sinb = c.scr.ap()[:, 3072:4096]
            for h in range(2):
                ah = a[:, h * 512:(h + 1) * 512]
                ch = cosb[:, h * 512:(h + 1) * 512]
                sh = sinb[:, h * 512:(h + 1) * 512]
                evh = ev[:, h * 512:(h + 1) * 512]
                p.v("scalar_tensor_tensor", out=t1[0:64, :], in0=ah[64:128, :],
                    scalar=-1.0, in1=sh[0:64, :], op0=ALU.mult, op1=ALU.mult)
                p.v("scalar_tensor_tensor", out=t1[64:128, :], in0=ah[0:64, :],
                    scalar=1.0, in1=sh[64:128, :], op0=ALU.mult, op1=ALU.mult)
                p.v("tensor_tensor", evh, ah, ch, ALU.mult)
                p.fence("vector", s["sV"])
                p.v("tensor_tensor", evh, evh, t1, ALU.add,
                    inc=[s["sV"].inc()])
        else:
            p.wait("scalar", acc_ready, sems[sl].at(), s["sV"].at())
            p.s("activation", ev, acc[:, oc2 * 1024:(oc2 + 1) * 1024],
                AF.Copy, inc=[s["sS"].inc()])
        if rope:
            p.wait("sync", s["sV"].at())
        else:
            p.wait("sync", s["sS"].at())
        p.y("dma_start", out=out_ap[oc * 128:(oc + 1) * 128, :], in_=ev,
            inc=[sems[sl].inc(16)])
    if rope:
        p.wait("tensor", s["sV"].at())
    else:
        p.wait("tensor", s["sS"].at())


def build_qkv(layer):
    es = ExitStack()
    nc = bass.Bass("TRN2", target_bir_lowering=False)
    nkv = 32 if layer == 0 else 8
    x_in = nc.dram_tensor("x_in", [TOK, D], F32, kind="ExternalInput").ap()
    wq = nc.dram_tensor("wq", [16, NJ, 128, KPP * OPP], F32, kind="ExternalInput").ap()
    wk = nc.dram_tensor("wk", [nkv // 2, NJ, 128, KPP * OPP], F32, kind="ExternalInput").ap()
    wv = nc.dram_tensor("wv", [nkv // 2, NJ, 128, KPP * OPP], F32, kind="ExternalInput").ap()
    id_in = nc.dram_tensor("id_in", [128, 128], F32, kind="ExternalInput").ap()
    qT = nc.dram_tensor("qT", [4096, TOK], BF16, kind="ExternalOutput").ap()
    kT = nc.dram_tensor("kT", [nkv * 128, TOK], BF16, kind="ExternalOutput").ap()
    vT = nc.dram_tensor("vT", [nkv * 128, TOK], BF16, kind="ExternalOutput").ap()
    if layer == 1:
        cos_in = nc.dram_tensor("cos_in", [128, TOK], F32, kind="ExternalInput").ap()
        sin_in = nc.dram_tensor("sin_in", [128, TOK], F32, kind="ExternalInput").ap()

    ctx = Ctx(nc, es)
    common_tensors(ctx)
    c, p, s = ctx, ctx.p, ctx.sem
    emit_const_loads(ctx, [], gp=[(c.ident, id_in)])
    emit_stream_load(ctx, x_in)
    p.wait("vector", s["sDX"].at())
    p.wait("vector", s["sDW"].at())
    p.v("memset", c.eps, 1e-5)
    p.v("memset", c.zero128, 0.0, inc=[s["sV"].inc()])
    p.wait("tensor", s["sV"].at())
    emit_ln(ctx)
    if layer == 1:
        p.wait("gpsimd", s["sT"].at(), s["sV"].at())
        p.g("dma_start", out=c.scr.ap()[:, 0:1024], in_=cos_in,
            inc=[s["sDW"].inc(16)])
        p.g("dma_start", out=c.scr.ap()[:, 3072:4096], in_=sin_in,
            inc=[s["sDW"].inc(16)])
        p.wait("vector", s["sDW"].at())
    rope = (layer == 1)
    emit_dense_fm(ctx, lambda og, j: wq[og, j], D,
                  lambda og, acc, rdy: emit_qkv_tail(ctx, og, acc, rdy, qT, rope=rope))
    emit_dense_fm(ctx, lambda og, j: wk[og, j], nkv * 128,
                  lambda og, acc, rdy: emit_qkv_tail(ctx, og, acc, rdy, kT, rope=rope))
    emit_dense_fm(ctx, lambda og, j: wv[og, j], nkv * 128,
                  lambda og, acc, rdy: emit_qkv_tail(ctx, og, acc, rdy, vT, rope=False))
    p.wait("sync", s["sB0"].at(), s["sB1"].at())
    with nc.Block() as block:
        p.replay(block)
    es.close()
    return nc


def emit_attn_core(ctx, qT_ap, kT_ap, vT_ap, aout_ap, nh, nkv, causal):
    c, p, s = ctx, ctx.p, ctx.sem
    B = c.big.ap()
    pb = psb(c)
    Kh = [B[:, 0:4096], B[:, 4096:8192]]
    VTh = B[:, 8192:12288]
    expTf = B[:, 8192:12288]
    Vh = [B[:, 12288:16384], B[:, 16384:20480]]
    Qs = [B[:, 20480:21504], B[:, 21504:22528]]
    expS = B[:, 22528:26624]
    outT = [B[:, 26624:27648], B[:, 27648:28672]]
    mwin = B[:, 28928:29440]
    rrow = B[0:1, 29952:30080]
    rbcb = B[:, 30080:30208]
    onesr = B[0:1, 30208:30336]
    l_ = c.stat[:, 8:9]
    r_ = c.stat[:, 9:10]
    negm = c.stat[:, 10:11]
    m_ = c.stat[:, 11:12]
    sK = [s["sK0"], s["sK1"]]
    sU = s["sU0"]
    sQ = [s["sQ0"], s["sQ1"]]
    sO = [s["sO0"], s["sO1"]]
    kslot_pe = [0, 0]
    qslot_pe = [0, 0]
    vt_pe = 0
    group = nh // nkv
    nrank = kT_ap.shape[0] // (nkv * 128)
    for kv in range(nkv):
        ksl = kv % 2
        p.wait("gpsimd", (s["sT"].h, kslot_pe[ksl]))
        for r in range(nrank):
            p.g("dma_start", out=Kh[ksl][:, r * 1024:(r + 1) * 1024],
                in_=kT_ap[r * nkv * 128 + kv * 128:
                          r * nkv * 128 + (kv + 1) * 128, :],
                inc=[sK[ksl].inc(16)])
        k_done = sK[ksl].at()
        p.wait("gpsimd", (s["sT"].h, vt_pe))
        for r in range(nrank):
            p.g("dma_start", out=VTh[:, r * 1024:(r + 1) * 1024],
                in_=vT_ap[r * nkv * 128 + kv * 128:
                          r * nkv * 128 + (kv + 1) * 128, :],
                inc=[sU.inc(16)])
        vt_done = sU.at()
        vsl = kv % 2
        p.wait("tensor", vt_done, s["sV"].at())
        ev_done = {}
        for g in range(8):
            bank = g % 2
            bv = pb[:, bank * 1024:bank * 1024 + 512]
            if g >= 2:
                p.wait("tensor", (s["sV"].h, ev_done[g - 2]))
            for kk in range(4):
                kc = g * 4 + kk
                p.t("transpose", bv[:, kk * 128:(kk + 1) * 128],
                    VTh[:, kc * 128:(kc + 1) * 128], c.ident,
                    inc=([s["sT"].inc()] if kk == 3 else []))
            p.wait("vector", s["sT"].at())
            p.v("tensor_copy", Vh[vsl][:, g * 512:(g + 1) * 512], bv,
                inc=[s["sV"].inc()])
            ev_done[g] = s["sV"].n
        p.wait("tensor", s["sV"].at())
        for qg in range(group):
            qh = kv * group + qg
            qsl = qh % 2
            p.wait("gpsimd", (s["sT"].h, qslot_pe[qsl]))
            p.g("dma_start", out=Qs[qsl], in_=qT_ap[qh * 128:(qh + 1) * 128, :],
                inc=[sQ[qsl].inc(16)])
            q_done = sQ[qsl].at()
            osl = qh % 2
            p.wait("vector", sO[osl].at())
            for t in range(NT):
                if causal:
                    w = (t + 1) * 128
                    ranges = [(r * 1024, w) for r in range(4)]
                else:
                    w = 4096
                    ranges = [(0, 4096)]
                kwidth = sum(rr[1] for rr in ranges)
                p.wait("tensor", k_done, q_done, s["sV"].at())
                sview = c.psA.ap().rearrange("p (r q) -> p r q", r=4)
                mview = mwin.rearrange("p (r c) -> p r c", r=4)
                m4 = c.stat[:, 12:16]
                mh = c.stat[:, 1:3]
                sh_done = []
                for hf in range(2):
                    for col0, wd in ranges[2 * hf:2 * hf + 2] if causal else                             [(hf * 2048, 2048)]:
                        i = 0
                        while i < wd:
                            n = min(512, wd - i)
                            p.t("matmul", c.psA.ap()[:, col0 + i:col0 + i + n],
                                Qs[qsl][:, t * 128:(t + 1) * 128],
                                Kh[ksl][:, col0 + i:col0 + i + n],
                                start=True, stop=True)
                            i += n
                    fl = pe_flush(c, c.psA.ap()[:, hf * 2048:hf * 2048 + 128])
                    fl[5].append(s["sT"].inc())
                    sh_done.append(s["sT"].at())
                s_done = s["sT"].at()
                kslot_pe[ksl] = s["sT"].n
                qslot_pe[qsl] = s["sT"].n
                for hf in range(2):
                    p.wait("vector", sh_done[hf])
                    if causal:
                        dv = sview[:, 2 * hf:2 * hf + 2, t * 128:(t + 1) * 128]
                        mv = mview[:, 2 * hf:2 * hf + 2, :]
                        p.v("tensor_tensor", dv, dv, mv, ALU.add)
                        p.fence("vector", s["sV"])
                        ssub = c.psA.ap().rearrange(
                            "p (r a z) -> p r a z", r=4, z=8)[
                            :, 2 * hf:2 * hf + 2, 0:w // 8, 0]
                        p.v("reduce_max", m4[:, 2 * hf:2 * hf + 2], ssub,
                            axis=AX)
                        p.fence("vector", s["sV"])
                        p.v("reduce_max", mh[:, hf:hf + 1],
                            m4[:, 2 * hf:2 * hf + 2], axis=AX)
                    else:
                        ssub = c.psA.ap().rearrange(
                            "p (a z) -> p a z", z=8)[
                            :, hf * 256:(hf + 1) * 256, 0]
                        p.v("reduce_max", mh[:, hf:hf + 1], ssub, axis=AX)
                    p.fence("vector", s["sV"])
                p.v("reduce_max", m_, mh, axis=AX)
                p.fence("vector", s["sV"])
                p.v("tensor_scalar", negm, m_, -SCL, None, ALU.mult,
                    inc=[s["sV"].inc()])
                p.wait("scalar", s["sV"].at())
                l_b = c.stat[:, 0:1]
                if causal:
                    eviewA = expS[:, 0:2 * w].rearrange(
                        "p (r q) -> p r q", r=2)
                    eviewB = expS[:, 2 * w:4 * w].rearrange(
                        "p (r q) -> p r q", r=2)
                    p.s("activation", eviewA, sview[:, 0:2, 0:w],
                        AF.Exp, bias=negm, scale=SCL, accum_out=l_,
                        inc=[s["sS"].inc()])
                    p.s("activation", eviewB, sview[:, 2:4, 0:w],
                        AF.Exp, bias=negm, scale=SCL, accum_out=l_b,
                        inc=[s["sS"].inc()])
                else:
                    p.s("activation", expS[:, 0:2048], c.psA.ap()[:, 0:2048],
                        AF.Exp, bias=negm, scale=SCL, accum_out=l_,
                        inc=[s["sS"].inc()])
                    p.s("activation", expS[:, 2048:4096],
                        c.psA.ap()[:, 2048:4096],
                        AF.Exp, bias=negm, scale=SCL, accum_out=l_b,
                        inc=[s["sS"].inc()])
                exp_done = s["sS"].n
                p.wait("vector", s["sS"].at())
                p.v("tensor_tensor", l_, l_, l_b, ALU.add)
                p.fence("vector", s["sV"])
                p.v("reciprocal", r_, l_)
                p.fence("vector", s["sV"])
                p.v("tensor_copy", c.stat.bitcast(BF16)[:, 24:25], r_,
                    inc=[s["sV"].inc()])
                r_bf = c.stat.bitcast(BF16)[:, 24:25]
                rv_done = s["sV"].at()
                nkc = kwidth // 128
                out_ps = c.psA.ap()[:, 2048:2176]

                def vcol(cc):
                    if not causal:
                        return cc * 128
                    rr_, jj = cc // (w // 128), cc % (w // 128)
                    return rr_ * 1024 + jj * 128

                half = 8 * ((nkc // 2 + 7) // 8)
                if half >= nkc:
                    half = nkc
                p.wait("tensor", (s["sS"].h, exp_done - 1))
                ev_t = []
                for cc in range(nkc):
                    if cc == nkc // 2:
                        p.wait("tensor", (s["sS"].h, exp_done))
                    tslot = pb[:, cc * 128:(cc + 1) * 128]
                    rec = p.t("transpose", tslot,
                              expS[:, cc * 128:(cc + 1) * 128], c.ident)
                    if cc == half - 1 or cc == nkc - 1:
                        fl = p.t("transpose", tslot,
                                 expS[:, cc * 128:(cc + 1) * 128], c.ident)
                        fl[5].append(s["sT"].inc())
                        ev_t.append(s["sT"].n)
                p.wait("tensor", rv_done)
                rrow_ps = pb[0:1, 5 * 1024 + 512:5 * 1024 + 640]
                p.t("transpose", rrow_ps, r_bf, c.ident,
                    inc=[s["sT"].inc()])
                p.wait("vector", (s["sT"].h, ev_t[0]))
                p.v("tensor_copy", expTf[:, 0:half * 128],
                    pb[:, 0:half * 128], inc=[s["sV"].inc()])
                evA = s["sV"].n
                p.wait("vector", s["sT"].at())
                p.v("tensor_copy", rrow, rrow_ps)
                if nkc > half:
                    p.wait("vector", (s["sT"].h, ev_t[1]))
                    p.v("tensor_copy", expTf[:, half * 128:nkc * 128],
                        pb[:, half * 128:nkc * 128], inc=[s["sV"].inc()])
                evB = s["sV"].n
                p.op("vector", None, inc=[s["sV"].inc()])
                p.wait("tensor", (s["sV"].h, evA), (s["sS"].h, exp_done))
                for cc in range(half):
                    vc = vcol(cc)
                    p.t("matmul", out_ps, Vh[vsl][:, vc:vc + 128],
                        expTf[:, cc * 128:(cc + 1) * 128],
                        start=(cc == 0), stop=False)
                if nkc > half:
                    p.wait("tensor", (s["sV"].h, evB))
                for cc in range(half, nkc):
                    vc = vcol(cc)
                    p.t("matmul", out_ps, Vh[vsl][:, vc:vc + 128],
                        expTf[:, cc * 128:(cc + 1) * 128],
                        start=False, stop=(cc == nkc - 1))
                rbc_ps = c.psA.ap()[:, 2560:2688]
                p.wait("tensor", s["sV"].at())
                p.t("matmul", rbc_ps, onesr, rrow, start=True, stop=True)
                fl = pe_flush(c, rbc_ps)
                fl[5].append(s["sT"].inc())
                p.wait("vector", s["sT"].at())
                p.v("tensor_copy", rbcb, rbc_ps)
                p.fence("vector", s["sV"])
                p.v("tensor_tensor", outT[osl][:, t * 128:(t + 1) * 128],
                    out_ps, rbcb, ALU.mult, inc=[s["sV"].inc()])
            p.wait("sync", s["sV"].at())
            p.y("dma_start", out=aout_ap[qh * 128:(qh + 1) * 128, :],
                in_=outT[osl], inc=[sO[osl].inc(16)])
        vt_pe = s["sT"].n
    p.wait("tensor", s["sV"].at())


def emit_dense_tm(ctx, pieces_ap, attnT, n_out, opp_o=512, kpp_e=2):
    c, p, s = ctx, ctx.p, ctx.sem
    n_og = n_out // opp_o
    n_j = KC // kpp_e
    wslot = [c.wbuf.ap()[:, 0:kpp_e * opp_o],
             c.wbuf.ap()[:, 1024:1024 + kpp_e * opp_o]]
    wsem = [s["sW0"], s["sW1"]]
    slot_free = [s["sT"].n, s["sT"].n]
    dma_done = {}

    def issue_piece(idx):
        og_, j_ = idx // n_j, idx % n_j
        sl_ = idx % 2
        p.wait("gpsimd", (s["sT"].h, slot_free[sl_]))
        p.g("dma_start", out=wslot[sl_], in_=pieces_ap(og_, j_),
            inc=[wsem[sl_].inc(16)])
        dma_done[idx] = wsem[sl_].at()

    issue_piece(0)
    issue_piece(1)
    for og in range(n_og):
        for ec in range(KC):
            j, ks = ec // kpp_e, ec % kpp_e
            idx = og * n_j + j
            sl = idx % 2
            if ks == 0:
                p.wait("tensor", dma_done[idx])
            for t in range(NT):
                rec = p.t("matmul", c.psA.ap()[:, t * 512:(t + 1) * 512],
                          attnT[:, ec * TOK + t * 128: ec * TOK + (t + 1) * 128],
                          wslot[sl][:, ks * opp_o:(ks + 1) * opp_o],
                          start=(ec == 0), stop=(ec == KC - 1))
                if ks == kpp_e - 1 and t == NT - 1:
                    rec[5].append(s["sT"].inc())
                    slot_free[sl] = s["sT"].n
                    if idx + 2 < n_og * n_j:
                        issue_piece(idx + 2)
        fl = pe_flush(c, c.psA.ap()[:, 0:128])
        fl[5].append(s["sT"].inc())
        acc_ready = s["sT"].at()
        for t in range(NT):
            p.wait("vector", acc_ready)
            dst = c.sx[:, t * D + og * opp_o: t * D + (og + 1) * opp_o]
            p.v("tensor_tensor", dst, dst, c.psA.ap()[:, t * 512:(t + 1) * 512],
                ALU.add, inc=[s["sV"].inc()])
        p.wait("tensor", s["sV"].at())


def build_attnB(layer):
    es = ExitStack()
    nc = bass.Bass("TRN2", target_bir_lowering=False)
    nh = 32
    nkv = 32 if layer == 0 else 8
    causal = (layer == 1)
    x_in = nc.dram_tensor("x_in", [TOK, D], F32, kind="ExternalInput").ap()
    x_out = nc.dram_tensor("x_out", [TOK, D], F32, kind="ExternalOutput").ap()
    qT = nc.dram_tensor("qT", [4096, TOK], BF16, kind="ExternalInput").ap()
    kT = nc.dram_tensor("kT", [4 * nkv * 128, TOK], BF16, kind="ExternalInput").ap()
    vT = nc.dram_tensor("vT", [4 * nkv * 128, TOK], BF16, kind="ExternalInput").ap()
    wo = nc.dram_tensor("wo", [8, 16, 128, 1024], F32, kind="ExternalInput").ap()
    id_in = nc.dram_tensor("id_in", [128, 128], F32, kind="ExternalInput").ap()
    mw_in = nc.dram_tensor("mw_in", [128, 512], F32, kind="ExternalInput").ap()
    aout = nc.dram_tensor("aout", [4096, TOK], BF16, kind="Internal").ap()

    ctx = Ctx(nc, es)
    common_tensors(ctx)
    c, p, s = ctx, ctx.p, ctx.sem
    c.mksem("sK0", "sK1", "sU0", "sQ0", "sQ1", "sO0", "sO1")
    gp = [(c.ident, id_in)]
    if causal:
        gp.append((c.big.ap()[:, 28928:29440], mw_in))
    emit_const_loads(ctx, [], gp=gp)
    emit_stream_load(ctx, x_in)
    p.wait("vector", s["sDX"].at(), s["sDW"].at())
    p.v("memset", c.big.ap()[0:1, 30208:30336], 1.0)
    p.v("memset", c.zero128, 0.0, inc=[s["sV"].inc()])
    p.wait("tensor", s["sV"].at())
    emit_attn_core(ctx, qT, kT, vT, aout, nh, nkv, causal)
    p.wait("sync", s["sV"].at(), s["sO0"].at(), s["sO1"].at())
    for g4 in range(8):
        src = aout[g4 * 512:(g4 + 1) * 512, :].rearrange(
            "(e p) t -> p e t", p=128)
        dst = c.big.ap().rearrange("p (e t) -> p e t", e=KC)[
            :, g4 * 4:(g4 + 1) * 4, :]
        p.y("dma_start", out=dst, in_=src, inc=[s["sDX"].inc(16)])
    p.wait("tensor", s["sDX"].at())
    emit_dense_tm(ctx, lambda og, j: wo[og, j], c.big.ap(), D)
    emit_stream_store(ctx, x_out)
    with nc.Block() as block:
        p.replay(block)
    es.close()
    return nc


def emit_dequant(ctx, qw_ap, scratch_ap):
    c, p, s = ctx, ctx.p, ctx.sem
    B = c.big.ap()
    bi = c.big.ap().bitcast(I32)
    slab = B[:, 0:8192]
    u_int = bi[:, 4096:8192]
    u = B[:, 28672:32768]
    qsl = [bi[:, 8192:11264], bi[:, 11264:14336]]
    qsem = [s["sK0"], s["sK1"]]
    pb = psb(c)
    q_pe = [0, 0]
    for nt in range(32):
        sl = nt % 2
        p.wait("sync", (s["sV"].h, q_pe[sl]))
        p.y("dma_start", out=qsl[sl], in_=qw_ap[nt * 128:(nt + 1) * 128, :],
            inc=[qsem[sl].inc(16)])
        q_done = qsem[sl].at()
        qv = qsl[sl].rearrange("p (g r) -> p g r", r=3)
        uv = u_int.rearrange("p (g r) -> p g r", r=4)
        p.wait("vector", q_done, s["sT"].at())
        p.v("tensor_scalar", uv[:, :, 0], qv[:, :, 0], 63, None, ALU.bitwise_and)
        p.v("tensor_scalar", uv[:, :, 1], qv[:, :, 0], 6, 63,
            ALU.logical_shift_right, ALU.bitwise_and)
        p.v("tensor_scalar", uv[:, :, 2], qv[:, :, 1], 63, None, ALU.bitwise_and)
        p.v("tensor_scalar", uv[:, :, 3], qv[:, :, 2], 63, None, ALU.bitwise_and,
            inc=[s["sV"].inc()])
        q_pe[sl] = s["sV"].n
        p.fence("vector", s["sV"])
        p.v("tensor_copy", u, u_int, inc=[s["sV"].inc()])
        p.wait("tensor", s["sV"].at())
        half = nt % 2
        ev_done = {}
        for g in range(8):
            bank = 6 + g % 2
            bv = pb[:, bank * 1024:bank * 1024 + 512]
            if g >= 2:
                p.wait("tensor", (s["sV"].h, ev_done[g - 2]))
            for kk in range(4):
                kc = g * 4 + kk
                p.t("transpose", bv[:, kk * 128:(kk + 1) * 128],
                    u[:, kc * 128:(kc + 1) * 128], c.ident,
                    inc=([s["sT"].inc()] if kk == 3 else []))
            p.wait("vector", s["sT"].at())
            dst = slab.rearrange("p (j k h c) -> p j k h c", j=8, k=4, h=2)[
                :, g, :, half, :]
            src = bv.rearrange("p (k c) -> p k c", k=4)
            p.v("tensor_copy", dst, src, inc=[s["sV"].inc()])
            ev_done[g] = s["sV"].n
        if half == 1:
            og = nt // 2
            p.wait("sync", s["sV"].at())
            for j in range(8):
                p.y("dma_start", out=scratch_ap[og, j],
                    in_=slab[:, j * 1024:(j + 1) * 1024],
                    inc=[s["sDS"].inc(16)])
            p.wait("vector", s["sDS"].at())
    p.wait("tensor", s["sV"].at())
    p.wait("gpsimd", s["sDS"].at())


def build_ffn_l1():
    es = ExitStack()
    nc = bass.Bass("TRN2", target_bir_lowering=False)
    x_in = nc.dram_tensor("x_in", [TOK, D], F32, kind="ExternalInput").ap()
    x_out = nc.dram_tensor("x_out", [TOK, D], F32, kind="ExternalOutput").ap()
    qw = nc.dram_tensor("qw", [4096, 3072], I32, kind="ExternalInput").ap()
    s_arr = nc.dram_tensor("s_arr", [128, 32], F32, kind="ExternalInput").ap()
    b_arr = nc.dram_tensor("b_arr", [128, 32], F32, kind="ExternalInput").ap()
    z_arr = nc.dram_tensor("z_arr", [128, 32], F32, kind="ExternalInput").ap()
    id_in = nc.dram_tensor("id_in", [128, 128], F32, kind="ExternalInput").ap()
    scratch = nc.dram_tensor("wscr", [16, NJ, 128, KPP * OPP], BF16,
                             kind=("ExternalOutput" if DEBUG_FFN else "Internal")).ap()
    rowscr = nc.dram_tensor("rowscr", [40, 128], BF16, kind="Internal").ap()
    ctx = Ctx(nc, es)
    common_tensors(ctx)
    c, p, s = ctx, ctx.p, ctx.sem
    c.mksem("sK0", "sK1")
    emit_const_loads(ctx, [(c.s05, s_arr), (c.b05, b_arr),
                           (c.scrf[:, 0:32], z_arr)],
                     gp=[(c.ident, id_in)])
    p.wait("vector", s["sDX"].at(), s["sDW"].at())
    p.v("memset", c.eps, 1e-5)
    p.v("memset", c.zero128, 0.0)
    p.v("tensor_scalar", c.s05, c.s05, 0.5, None, ALU.mult)
    p.v("tensor_scalar", c.b05, c.b05, 0.5, None, ALU.mult)
    p.fence("vector", s["sV"])
    p.v("tensor_tensor", c.zs05, c.scrf[:, 0:32], c.s05, ALU.mult)
    p.fence("vector", s["sV"])
    p.v("tensor_scalar", c.zs05, c.zs05, -1.0, None, ALU.mult,
        inc=[s["sV"].inc()])
    emit_dequant(ctx, qw, scratch)
    emit_stream_load(ctx, x_in)
    p.wait("vector", s["sDX"].at())
    emit_ln(ctx, rowsum=True)
    zT_ps = psb(c)[0:32, 0:128]
    rT_ps = psb(c)[0:8, 1024:1152]
    zs_bf = c.scr.ap()[:, 256:288]
    rs_bf = c.scr.ap()[:, 288:296]
    p.v("tensor_copy", zs_bf, c.zs05)
    p.v("tensor_copy", rs_bf, c.rs8, inc=[s["sV"].inc()])
    p.wait("tensor", s["sV"].at())
    p.t("transpose", zT_ps, zs_bf, c.ident, inc=[s["sT"].inc()])
    p.t("transpose", rT_ps, rs_bf, c.ident)
    fl = pe_flush(c, c.psA.ap()[:, 1536:1664])
    fl[5].append(s["sT"].inc())
    p.wait("vector", s["sT"].at())
    p.v("tensor_copy", c.scr.ap()[0:32, 0:128], zT_ps)
    p.v("tensor_copy", c.scr.ap()[0:8, 128:256], rT_ps, inc=[s["sV"].inc()])
    p.wait("sync", s["sV"].at())
    p.y("dma_start", out=rowscr[0:32, :], in_=c.scr.ap()[0:32, 0:128],
        inc=[s["sDS"].inc(16)])
    p.y("dma_start", out=rowscr[32:40, :], in_=c.scr.ap()[0:8, 128:256],
        inc=[s["sDS"].inc(16)])
    p.wait("sync", s["sDS"].at())
    p.y("dma_start", out=c.scr.ap()[0:1, 2048:4096],
        in_=rowscr[0:16, :].rearrange("a b -> (a b)").unsqueeze(0),
        inc=[s["sDX"].inc(16)])
    p.y("dma_start", out=c.scr.ap()[32:33, 2048:4096],
        in_=rowscr[16:32, :].rearrange("a b -> (a b)").unsqueeze(0),
        inc=[s["sDX"].inc(16)])
    p.y("dma_start", out=c.scr.ap()[0:1, 0:1024],
        in_=rowscr[32:40, :].rearrange("a b -> (a b)").unsqueeze(0),
        inc=[s["sDX"].inc(16)])
    p.y("dma_start", out=c.scr.ap()[32:33, 0:1024],
        in_=rowscr[32:40, :].rearrange("a b -> (a b)").unsqueeze(0),
        inc=[s["sDX"].inc(16)])
    p.wait("tensor", s["sDX"].at())

    def zrow_fn(oc):
        part = 0 if oc < 16 else 32
        return c.scr.ap()[part:part + 1, 2048 + (oc % 16) * 128:
                          2048 + (oc % 16 + 1) * 128]

    def rsrow_fn(oc):
        part = 0 if oc < 16 else 32
        return c.scr.ap()[part:part + 1, 0:1024]

    def tail(og, acc, rdy):
        emit_ffn_tail(ctx, og, acc, rdy, c.s05, c.b05)
    if DEBUG_FFN:
        dbg_xnT = nc.dram_tensor("dbg_xnT", [128, KC * TOK], BF16,
                                 kind="ExternalOutput").ap()
        p.wait("sync", s["sV"].at())
        p.y("dma_start", out=dbg_xnT, in_=c.xnT[:, :], inc=[s["sDS"].inc(16)])
        p.wait("vector", s["sDS"].at())

    def tail(og, acc, rdy):
        emit_ffn_tail(ctx, og, acc, rdy, c.s05, c.b05, dbg=dbg_ev,
                      dbg2=dbg_ev2)

    emit_dense_fm(ctx, lambda og, j: scratch[og, j], D, tail,
                  rank1=(zrow_fn, rsrow_fn))
    emit_stream_store(ctx, x_out)
    with nc.Block() as block:
        p.replay(block)
    es.close()
    return nc



_NC_CACHE = {}


def _get_nc(name, builder):
    if name not in _NC_CACHE:
        _NC_CACHE[name] = builder()
    return _NC_CACHE[name]


def _run(name, builder, per_core):
    nc = _get_nc(name, builder)
    res = run_bass_kernel_spmd(nc, per_core, core_ids=list(range(8)))
    return res.results


def core_rows(c):
    g = c % 4
    idx = []
    for t in range(NT):
        base = (4 * t + g) * 128
        idx.extend(range(base, base + 128))
    return np.array(idx)


def maskwin_host(cg):
    m = np.full((128, 512), NEG, dtype=np.float32)
    for rr in range(128):
        m[rr, 0:cg * 128] = 0.0
        m[rr, cg * 128:cg * 128 + rr + 1] = 0.0
    return m


def _OLD_kernel(**inputs):
    inputs = {k: np.asarray(v) for k, v in inputs.items()}
    x = inputs["L0_attn_x"]
    rows = [core_rows(c) for c in range(8)]
    xs = [np.ascontiguousarray(x[c // 4][rows[c]]) for c in range(8)]
    ident = np.eye(128, dtype=np.float32)

    st_ffn0 = ffn_l0_inputs(inputs)
    st_qkv0 = {
        "wq": arrange_pieces(np.ascontiguousarray(inputs["L0_attn_w_q"].T)),
        "wk": arrange_pieces(np.ascontiguousarray(inputs["L0_attn_w_k"].T)),
        "wv": arrange_pieces(np.ascontiguousarray(inputs["L0_attn_w_v"].T)),
        "id_in": ident,
    }
    wo0 = arrange_pieces(np.ascontiguousarray(inputs["L0_attn_w_o"].T),
                         opp=512, kpp=2)
    st_qkv1 = {
        "wq": arrange_pieces(np.ascontiguousarray(inputs["L1_attn_Wq"])),
        "wk": arrange_pieces(np.ascontiguousarray(inputs["L1_attn_Wk"])),
        "wv": arrange_pieces(np.ascontiguousarray(inputs["L1_attn_Wv"])),
        "id_in": ident,
    }
    wo1 = arrange_pieces(np.ascontiguousarray(inputs["L1_attn_Wo"]),
                         opp=512, kpp=2)
    st_ffn1 = {
        "qw": np.ascontiguousarray(inputs["L1_ffn_qweight"]),
        "s_arr": cols128(np.ascontiguousarray(inputs["L1_ffn_weight_scale"][:, 0])),
        "b_arr": cols128(np.ascontiguousarray(inputs["L1_ffn_bias"])),
        "z_arr": cols128(np.ascontiguousarray(inputs["L1_ffn_weight_zero_point"][:, 0])),
        "id_in": ident,
    }
    cosf, sinf = rope_tables()
    cos_c = [np.ascontiguousarray(cosf[:, rows[c]]) for c in range(8)]
    sin_c = [np.ascontiguousarray(sinf[:, rows[c]]) for c in range(8)]

    def gather_kv(shards):
        full = [np.ascontiguousarray(np.concatenate(shards[g * 4:(g + 1) * 4], axis=0))
                for g in range(2)]
        return [full[c // 4] for c in range(8)]

    r = _run("ffn0", build_ffn_l0, [dict(st_ffn0, x_in=xs[i]) for i in range(8)])
    x1 = [r[i]["x_out"] for i in range(8)]
    r = _run("qkv0", lambda: build_qkv(0),
             [dict(st_qkv0, x_in=x1[i]) for i in range(8)])
    q0 = [r[i]["qT"] for i in range(8)]
    k0 = gather_kv([r[i]["kT"] for i in range(8)])
    v0 = gather_kv([r[i]["vT"] for i in range(8)])
    r = _run("attnB0", lambda: build_attnB(0),
             [dict(x_in=x1[i], qT=q0[i], kT=k0[i], vT=v0[i], wo=wo0,
                   id_in=ident, mw_in=np.zeros((128, 512), dtype=np.float32))
              for i in range(8)])
    x2 = [r[i]["x_out"] for i in range(8)]
    r = _run("ffn0", build_ffn_l0, [dict(st_ffn0, x_in=x2[i]) for i in range(8)])
    x3 = [r[i]["x_out"] for i in range(8)]
    r = _run("ffn1", build_ffn_l1, [dict(st_ffn1, x_in=x3[i]) for i in range(8)])
    x4 = [r[i]["x_out"] for i in range(8)]
    r = _run("qkv1", lambda: build_qkv(1),
             [dict(st_qkv1, x_in=x4[i], cos_in=cos_c[i], sin_in=sin_c[i])
              for i in range(8)])
    q1 = [r[i]["qT"] for i in range(8)]
    k1 = gather_kv([r[i]["kT"] for i in range(8)])
    v1 = gather_kv([r[i]["vT"] for i in range(8)])
    mws = [maskwin_host(c % 4) for c in range(8)]
    r = _run("attnB1", lambda: build_attnB(1),
             [dict(x_in=x4[i], qT=q1[i], kT=k1[i], vT=v1[i], wo=wo1,
                   id_in=ident, mw_in=mws[i]) for i in range(8)])
    x5 = [r[i]["x_out"] for i in range(8)]
    r = _run("ffn1", build_ffn_l1, [dict(st_ffn1, x_in=x5[i]) for i in range(8)])
    xo = [r[i]["x_out"] for i in range(8)]

    out = np.empty((2, 4096, 4096), dtype=np.float32)
    for c in range(8):
        out[c // 4][rows[c]] = xo[c]
    return out


def emit_barrier(ctx):
    p, s = ctx.p, ctx.sem
    tgt = [s["sT"].at(), s["sV"].at(), s["sS"].at()]
    for eng in ENGINES:
        p.wait(eng, *tgt)


def emit_zrow_staging(ctx, rowscr):
    c, p, s = ctx, ctx.p, ctx.sem
    zT_ps = psb(c)[0:32, 0:128]
    rT_ps = psb(c)[0:8, 1024:1152]
    zs_bf = c.scr.ap()[:, 256:288]
    rs_bf = c.scr.ap()[:, 288:296]
    p.wait("vector", s["sT"].at())
    p.v("tensor_copy", zs_bf, c.zs05)
    p.v("tensor_copy", rs_bf, c.rs8, inc=[s["sV"].inc()])
    p.wait("tensor", s["sV"].at())
    p.t("transpose", zT_ps, zs_bf, c.ident, inc=[s["sT"].inc()])
    p.t("transpose", rT_ps, rs_bf, c.ident)
    fl = pe_flush(c, c.psA.ap()[:, 1536:1664])
    fl[5].append(s["sT"].inc())
    p.wait("vector", s["sT"].at())
    p.v("tensor_copy", c.scr.ap()[0:32, 0:128], zT_ps)
    p.v("tensor_copy", c.scr.ap()[0:8, 128:256], rT_ps, inc=[s["sV"].inc()])
    p.wait("sync", s["sV"].at())
    p.y("dma_start", out=rowscr[0:32, :], in_=c.scr.ap()[0:32, 0:128],
        inc=[s["sDS"].inc(16)])
    p.y("dma_start", out=rowscr[32:40, :], in_=c.scr.ap()[0:8, 128:256],
        inc=[s["sDS"].inc(16)])
    p.wait("sync", s["sDS"].at())
    p.y("dma_start", out=c.scr.ap()[0:1, 2048:4096],
        in_=rowscr[0:16, :].rearrange("a b -> (a b)").unsqueeze(0),
        inc=[s["sDX"].inc(16)])
    p.y("dma_start", out=c.scr.ap()[32:33, 2048:4096],
        in_=rowscr[16:32, :].rearrange("a b -> (a b)").unsqueeze(0),
        inc=[s["sDX"].inc(16)])
    p.y("dma_start", out=c.scr.ap()[0:1, 0:1024],
        in_=rowscr[32:40, :].rearrange("a b -> (a b)").unsqueeze(0),
        inc=[s["sDX"].inc(16)])
    p.y("dma_start", out=c.scr.ap()[32:33, 0:1024],
        in_=rowscr[32:40, :].rearrange("a b -> (a b)").unsqueeze(0),
        inc=[s["sDX"].inc(16)])
    p.wait("tensor", s["sDX"].at())

    def zrow_fn(oc):
        part = 0 if oc < 16 else 32
        return c.scr.ap()[part:part + 1, 2048 + (oc % 16) * 128:
                          2048 + (oc % 16 + 1) * 128]

    def rsrow_fn(oc):
        part = 0 if oc < 16 else 32
        return c.scr.ap()[part:part + 1, 0:1024]
    return zrow_fn, rsrow_fn


def build_fused():
    es = ExitStack()
    nc = bass.Bass("TRN2", target_bir_lowering=False)
    dt_ = nc.dram_tensor
    x_in = dt_("x_in", [TOK, D], F32, kind="ExternalInput").ap()
    x_out = dt_("x_out", [TOK, D], F32, kind="ExternalOutput").ap()
    wb = dt_("wbase", [16, NJ, 128, KPP * OPP], F32, kind="ExternalInput").ap()
    b_arr0 = dt_("b_arr0", [128, 32], F32, kind="ExternalInput").ap()
    wr = dt_("wr", [128, 256], F32, kind="ExternalInput").ap()
    A_in = dt_("A_in", [4, 128, 1024], F32, kind="ExternalInput").ap()
    B_in = dt_("B_in", [16, 128, 256], F32, kind="ExternalInput").ap()
    E_in = dt_("E_in", [8, 128], F32, kind="ExternalInput").ap()
    id_in = dt_("id_in", [128, 128], F32, kind="ExternalInput").ap()
    wq0 = dt_("wq0", [16, NJ, 128, KPP * OPP], F32, kind="ExternalInput").ap()
    wk0 = dt_("wk0", [16, NJ, 128, KPP * OPP], F32, kind="ExternalInput").ap()
    wv0 = dt_("wv0", [16, NJ, 128, KPP * OPP], F32, kind="ExternalInput").ap()
    wo0 = dt_("wo0", [8, 16, 128, 1024], F32, kind="ExternalInput").ap()
    wq1 = dt_("wq1", [16, NJ, 128, KPP * OPP], F32, kind="ExternalInput").ap()
    wk1 = dt_("wk1", [4, NJ, 128, KPP * OPP], F32, kind="ExternalInput").ap()
    wv1 = dt_("wv1", [4, NJ, 128, KPP * OPP], F32, kind="ExternalInput").ap()
    wo1 = dt_("wo1", [8, 16, 128, 1024], F32, kind="ExternalInput").ap()
    qw = dt_("qw", [4096, 3072], I32, kind="ExternalInput").ap()
    s_arr = dt_("s_arr", [128, 32], F32, kind="ExternalInput").ap()
    b_arr1 = dt_("b_arr1", [128, 32], F32, kind="ExternalInput").ap()
    z_arr = dt_("z_arr", [128, 32], F32, kind="ExternalInput").ap()
    cos_in = dt_("cos_in", [128, TOK], F32, kind="ExternalInput").ap()
    sin_in = dt_("sin_in", [128, TOK], F32, kind="ExternalInput").ap()
    mw_in = dt_("mw_in", [128, 512], F32, kind="ExternalInput").ap()
    wscr = dt_("wscr", [16, NJ, 128, KPP * OPP], BF16, kind="Internal").ap()
    rowscr = dt_("rowscr", [40, 128], BF16, kind="Internal").ap()
    qb = dt_("qb", [4096, TOK], BF16, kind="Internal").ap()
    aout = dt_("aout", [4096, TOK], BF16, kind="Internal").ap()
    k0i = dt_("k0i", [4096, TOK], BF16, kind="Internal").ap()
    v0i = dt_("v0i", [4096, TOK], BF16, kind="Internal").ap()
    k0o = dt_("k0o", [16384, TOK], BF16, kind="Internal").ap()
    v0o = dt_("v0o", [16384, TOK], BF16, kind="Internal").ap()
    k1i = dt_("k1i", [1024, TOK], BF16, kind="Internal").ap()
    v1i = dt_("v1i", [1024, TOK], BF16, kind="Internal").ap()
    k1o = dt_("k1o", [4096, TOK], BF16, kind="Internal").ap()
    v1o = dt_("v1o", [4096, TOK], BF16, kind="Internal").ap()

    ctx = Ctx(nc, es)
    common_tensors(ctx)
    c, p, s = ctx, ctx.p, ctx.sem
    c.mksem("sK0", "sK1", "sU0", "sQ0", "sQ1", "sO0", "sO1")
    c.b05L1 = c.cst.ap()[:, 80:112]
    rg = [[0, 1, 2, 3], [4, 5, 6, 7]]

    emit_const_loads(ctx, [(c.b05, b_arr0), (c.s05, s_arr),
                           (c.b05L1, b_arr1), (c.scrf[:, 0:32], z_arr)],
                     gp=[(c.ident, id_in), (c.sb_E, E_in)])
    emit_stream_load(ctx, x_in)
    p.wait("vector", s["sDX"].at(), s["sDW"].at())
    p.v("memset", c.eps, 1e-5)
    p.v("memset", c.zero128, 0.0)
    p.v("tensor_scalar", c.b05, c.b05, 0.5, None, ALU.mult)
    p.v("tensor_scalar", c.s05, c.s05, 0.5, None, ALU.mult)
    p.v("tensor_scalar", c.b05L1, c.b05L1, 0.5, None, ALU.mult)
    p.fence("vector", s["sV"])
    p.v("tensor_tensor", c.zs05, c.scrf[:, 0:32], c.s05, ALU.mult)
    p.fence("vector", s["sV"])
    p.v("tensor_scalar", c.zs05, c.zs05, -1.0, None, ALU.mult,
        inc=[s["sV"].inc()])
    p.wait("tensor", s["sV"].at())
    emit_dequant(ctx, qw, wscr)
    emit_barrier(ctx)

    def ffn0():
        emit_ln(ctx)
        pre_og, extra = emit_router_lora(ctx, wr, A_in, B_in)
        done_og = set()

        def pieces_ap(og, j):
            if og not in done_og:
                done_og.add(og)
                pre_og(og)
            return wb[og, j]

        def tail(og, acc, rdy):
            emit_ffn_tail(ctx, og, acc, rdy, 0.5, c.b05)
        emit_dense_fm(ctx, pieces_ap, D, tail, extra_mm=extra)
        emit_barrier(ctx)

    def ffn1():
        emit_ln(ctx, rowsum=True)
        zrow_fn, rsrow_fn = emit_zrow_staging(ctx, rowscr)

        def tail(og, acc, rdy):
            emit_ffn_tail(ctx, og, acc, rdy, c.s05, c.b05L1, dual_ev=False)
        emit_dense_fm(ctx, lambda og, j: wscr[og, j], D, tail,
                      rank1=(zrow_fn, rsrow_fn))
        emit_barrier(ctx)

    def attn(layer):
        nkv = 32 if layer == 0 else 8
        causal = (layer == 1)
        wqx = wq0 if layer == 0 else wq1
        wkx = wk0 if layer == 0 else wk1
        wvx = wv0 if layer == 0 else wv1
        wox = wo0 if layer == 0 else wo1
        ki = k0i if layer == 0 else k1i
        ko = k0o if layer == 0 else k1o
        vi = v0i if layer == 0 else v1i
        vo = v0o if layer == 0 else v1o
        emit_ln(ctx)
        if layer == 1:
            p.wait("gpsimd", s["sT"].at(), s["sV"].at())
            p.g("dma_start", out=c.scr.ap()[:, 0:1024], in_=cos_in,
                inc=[s["sDW"].inc(16)])
            p.g("dma_start", out=c.scr.ap()[:, 3072:4096], in_=sin_in,
                inc=[s["sDW"].inc(16)])
            p.wait("vector", s["sDW"].at())
        rope = (layer == 1)
        emit_dense_fm(ctx, lambda og, j: wkx[og, j], nkv * 128,
                      lambda og, acc, rdy: emit_qkv_tail(ctx, og, acc, rdy,
                                                         ki, rope=rope))
        p.wait("gpsimd", s["sB0"].at(), s["sB1"].at())
        p.g("collective_compute", "AllGather", ALU.bypass,
            replica_groups=rg, ins=[ki], outs=[ko], inc=[s["sG"].inc()])
        kg_done = s["sG"].at()
        emit_dense_fm(ctx, lambda og, j: wvx[og, j], nkv * 128,
                      lambda og, acc, rdy: emit_qkv_tail(ctx, og, acc, rdy,
                                                         vi, rope=False))
        p.wait("gpsimd", s["sB0"].at(), s["sB1"].at())
        p.g("collective_compute", "AllGather", ALU.bypass,
            replica_groups=rg, ins=[vi], outs=[vo], inc=[s["sG"].inc()])
        vg_done = s["sG"].at()
        emit_dense_fm(ctx, lambda og, j: wqx[og, j], D,
                      lambda og, acc, rdy: emit_qkv_tail(ctx, og, acc, rdy,
                                                         qb, rope=rope))
        p.wait("gpsimd", (s["sT"].h, s["sT"].n))
        if causal:
            p.g("dma_start", out=c.big.ap()[:, 28928:29440], in_=mw_in,
                inc=[s["sDW"].inc(16)])
        p.wait("gpsimd", kg_done, vg_done, s["sB0"].at(), s["sB1"].at())
        p.wait("vector", s["sDW"].at(), (s["sT"].h, s["sT"].n))
        p.v("memset", c.big.ap()[0:1, 30208:30336], 1.0, inc=[s["sV"].inc()])
        p.wait("tensor", s["sV"].at())
        emit_attn_core(ctx, qb, ko, vo, aout, 32, nkv, causal)
        emit_barrier(ctx)
        p.wait("sync", s["sO0"].at(), s["sO1"].at(), s["sV"].at())
        for g4 in range(8):
            sr = aout[g4 * 512:(g4 + 1) * 512, :].rearrange(
                "(e p) t -> p e t", p=128)
            dsx = c.big.ap().rearrange("p (e t) -> p e t", e=KC)[
                :, g4 * 4:(g4 + 1) * 4, :]
            p.y("dma_start", out=dsx, in_=sr, inc=[s["sDX"].inc(16)])
        p.wait("tensor", s["sDX"].at())
        emit_dense_tm(ctx, lambda og, j: wox[og, j], c.big.ap(), D)
        emit_barrier(ctx)

    ffn0()
    attn(0)
    ffn0()
    ffn1()
    attn(1)
    ffn1()
    emit_stream_store(ctx, x_out)
    with nc.Block() as block:
        p.replay(block)
    es.close()
    return nc


def fused_inputs(inputs):
    ident = np.eye(128, dtype=np.float32)
    st = ffn_l0_inputs(inputs)
    st = {"wbase": st["wbase"], "b_arr0": st["b_arr"], "wr": st["wr"],
          "A_in": st["A_in"], "B_in": st["B_in"], "E_in": st["E_in"],
          "id_in": ident}
    st["wq0"] = arrange_pieces(np.ascontiguousarray(inputs["L0_attn_w_q"].T))
    st["wk0"] = arrange_pieces(np.ascontiguousarray(inputs["L0_attn_w_k"].T))
    st["wv0"] = arrange_pieces(np.ascontiguousarray(inputs["L0_attn_w_v"].T))
    st["wo0"] = arrange_pieces(np.ascontiguousarray(inputs["L0_attn_w_o"].T),
                               opp=512, kpp=2)
    st["wq1"] = arrange_pieces(np.ascontiguousarray(inputs["L1_attn_Wq"]))
    st["wk1"] = arrange_pieces(np.ascontiguousarray(inputs["L1_attn_Wk"]))
    st["wv1"] = arrange_pieces(np.ascontiguousarray(inputs["L1_attn_Wv"]))
    st["wo1"] = arrange_pieces(np.ascontiguousarray(inputs["L1_attn_Wo"]),
                               opp=512, kpp=2)
    st["qw"] = np.ascontiguousarray(inputs["L1_ffn_qweight"])
    st["s_arr"] = cols128(np.ascontiguousarray(inputs["L1_ffn_weight_scale"][:, 0]))
    st["b_arr1"] = cols128(np.ascontiguousarray(inputs["L1_ffn_bias"]))
    st["z_arr"] = cols128(np.ascontiguousarray(inputs["L1_ffn_weight_zero_point"][:, 0]))
    return st


def kernel(**inputs):
    inputs = {k: np.asarray(v) for k, v in inputs.items()}
    x = inputs["L0_attn_x"]
    rows = [core_rows(c) for c in range(8)]
    xs = [np.ascontiguousarray(x[c // 4][rows[c]]) for c in range(8)]
    st = fused_inputs(inputs)
    cosf, sinf = rope_tables()
    nc = _NC_CACHE.get("fused")
    if nc is None:
        nc = build_fused()
        _NC_CACHE["fused"] = nc
    in_maps = [dict(st, x_in=xs[i],
                    cos_in=np.ascontiguousarray(cosf[:, rows[i]]),
                    sin_in=np.ascontiguousarray(sinf[:, rows[i]]),
                    mw_in=maskwin_host(i % 4)) for i in range(8)]
    res = run_bass_kernel_spmd(nc, in_maps, core_ids=list(range(8)))
    out = np.empty((2, 4096, 4096), dtype=np.float32)
    for cc in range(8):
        out[cc // 4][rows[cc]] = res.results[cc]["x_out"]
    return out




# version 11
# speedup vs baseline: 1.0069x; 1.0069x over previous
import numpy as np
import ml_dtypes
from contextlib import ExitStack

import concourse.bass as bass
import concourse.mybir as mybir
from concourse.bass_utils import run_bass_kernel_spmd

F32 = mybir.dt.float32
BF16 = mybir.dt.bfloat16
I32 = mybir.dt.int32
AF = mybir.ActivationFunctionType
ALU = mybir.AluOpType
AX = mybir.AxisListType.X
BF = ml_dtypes.bfloat16

D = 4096
TOK = 1024
NT = 8
KC = 32
SCL = float(1.0 / np.sqrt(128.0))
NEG = -1.0e30
KPP = 4
OPP = 256
NJ = KC // KPP

ENGINES = ("sync", "scalar", "vector", "tensor", "gpsimd")


class Sem:
    def __init__(self, h):
        self.h = h
        self.n = 0

    def inc(self, d=1):
        self.n += d
        return (self.h, d)

    def at(self):
        return (self.h, self.n)


class Prog:
    def __init__(self):
        self.ops = []
        self.last = {e: None for e in ENGINES}

    def op(self, eng, method, *args, wait=(), inc=(), **kwargs):
        rec = [eng, method, args, kwargs, list(wait), list(inc)]
        self.ops.append(rec)
        if method is not None:
            self.last[eng] = rec
        return rec

    def wait(self, eng, *wait):
        self.ops.append([eng, None, (), {}, list(wait), []])

    def fence(self, eng, sem):
        rec = self.last[eng]
        assert rec is not None
        rec[5].append(sem.inc())
        self.wait(eng, sem.at())

    def t(self, method, *a, **k):
        return self.op("tensor", method, *a, **k)

    def v(self, method, *a, **k):
        return self.op("vector", method, *a, **k)

    def s(self, method, *a, **k):
        return self.op("scalar", method, *a, **k)

    def g(self, method, *a, **k):
        return self.op("gpsimd", method, *a, **k)

    def y(self, method, *a, **k):
        return self.op("sync", method, *a, **k)

    def replay(self, block):
        def run(engobj, name):
            for eng, method, args, kwargs, waits, incs in self.ops:
                if eng != name:
                    continue
                for (h, val) in waits:
                    engobj.wait_ge(h, val)
                inst = None
                if method is not None:
                    inst = getattr(engobj, method)(*args, **kwargs)
                if incs:
                    if inst is None:
                        inst = engobj.nop()
                    inst.then_inc(*incs[0])
                    for (h, d) in incs[1:]:
                        engobj.nop().then_inc(h, d)

        block.sync(lambda e: run(e, "sync"))
        block.scalar(lambda e: run(e, "scalar"))
        block.vector(lambda e: run(e, "vector"))
        block.tensor(lambda e: run(e, "tensor"))
        block.gpsimd(lambda e: run(e, "gpsimd"))


class Ctx:
    def __init__(self, nc, es):
        self.nc = nc
        self.es = es
        self.p = Prog()
        self.sem = {}

    def sb(self, name, shape, dtype):
        return self.es.enter_context(self.nc.sbuf_tensor(name, shape, dtype))

    def psum(self, name, shape, dtype):
        return self.es.enter_context(self.nc.psum_tensor(name, shape, dtype))

    def mksem(self, *names):
        for n in names:
            h = self.es.enter_context(self.nc.semaphore(n))
            self.sem[n] = Sem(h)




def arrange_pieces(Wt, opp=OPP, kpp=KPP):
    K, O = Wt.shape
    a = Wt.reshape(K // (128 * kpp), kpp, 128, O // opp, opp)
    a = a.transpose(3, 0, 2, 1, 4)
    return np.ascontiguousarray(
        a.reshape(O // opp, K // (128 * kpp), 128, kpp * opp))


def cols128(v):
    n = v.shape[0] // 128
    return np.ascontiguousarray(v.reshape(n, 128).T)


def rope_tables(S=4096, hd=128, theta=10000.0):
    inv = 1.0 / theta ** (np.arange(0, hd, 2, dtype=np.float64) / hd)
    t = np.arange(S, dtype=np.float64)
    freqs = np.outer(t, inv)
    emb = np.concatenate([freqs, freqs], axis=1)
    cos = np.ascontiguousarray(np.cos(emb).astype(np.float32).T)
    sin = np.ascontiguousarray(np.sin(emb).astype(np.float32).T)
    return cos, sin


def tri_mask():
    m = np.where(np.tril(np.ones((128, 128), dtype=bool)), 0.0, NEG)
    return np.ascontiguousarray(m.astype(np.float32))


def e_matrix():
    E = np.zeros((8, 128), dtype=np.float32)
    for e in range(8):
        E[e, 16 * e:16 * (e + 1)] = 2.0
    return np.ascontiguousarray(E)




def common_tensors(ctx, stream=True, big=True):
    c = ctx
    if stream:
        c.sx = c.sb("sx", [128, NT * D], F32)
    if big:
        c.big = c.sb("big", [128, KC * TOK], BF16)
        c.xnT = c.big
    c.scr = c.sb("scr", [128, 4096], BF16)
    c.wbuf = c.sb("wbuf", [128, 2048], BF16)
    c.cst = c.sb("cst", [128, 512], F32)
    c.psA = c.psum("psA", [128, 4096], F32)
    c.mksem("sDW", "sDX", "sDS", "sT", "sV", "sS", "sG",
            "sW0", "sW1", "sB0", "sB1")
    c.stat = c.cst.ap()[:, 0:16]
    c.b05 = c.cst.ap()[:, 16:48]
    c.s05 = c.cst.ap()[:, 48:80]
    c.identf = c.cst.ap()[:, 80:208]
    c.l8 = c.cst.ap()[:, 208:216]
    c.r8 = c.cst.ap()[:, 216:232]
    c.msk = c.cst.ap()[:, 232:240]
    c.trimask = c.cst.ap()[:, 240:368]
    c.eps = c.cst.ap()[:, 368:369]
    cb = c.cst.ap().bitcast(BF16)
    c.zero128 = cb[:, 480:608]
    c.zs05 = c.cst.ap()[:, 304:336]
    c.rs8 = c.cst.ap()[:, 336:344]
    c.cb8 = cb[:, 744:752]
    c.sb_E = cb[0:8, 752:880]
    c.ident = cb[:, 880:1008]
    c.scrf = c.scr.ap().bitcast(F32)


def psb(ctx):
    return ctx.psA.ap().bitcast(BF16)


def pe_flush(ctx, region):
    return ctx.p.t("matmul", region, ctx.ident, ctx.zero128,
                   start=False, stop=True)


def emit_const_loads(ctx, consts, gp=()):
    p, s = ctx.p, ctx.sem
    for dst, src in consts:
        p.y("dma_start", out=dst, in_=src, inc=[s["sDX"].inc(16)])
    for dst, src in gp:
        p.g("dma_start", out=dst, in_=src, inc=[s["sDW"].inc(16)])


def emit_stream_load(ctx, x_ap):
    p, s = ctx.p, ctx.sem
    for t in range(NT):
        p.y("dma_start", out=ctx.sx[:, t * D:(t + 1) * D],
            in_=x_ap[t * 128:(t + 1) * 128, :], inc=[s["sDX"].inc(16)])


def emit_stream_store(ctx, out_ap):
    p, s = ctx.p, ctx.sem
    p.wait("sync", s["sV"].at(), s["sS"].at())
    for t in range(NT):
        p.y("dma_start", out=out_ap[t * 128:(t + 1) * 128, :],
            in_=ctx.sx[:, t * D:(t + 1) * D], inc=[s["sDS"].inc(16)])
    p.wait("sync", s["sDS"].at())


def emit_ln(ctx, rowsum=False):
    c, p, s = ctx, ctx.p, ctx.sem
    st = c.stat
    pb = psb(c)
    xn = c.scr.ap()[:, 0:4096]
    evac_done = {}
    for t in range(NT):
        xt = c.sx[:, t * D:(t + 1) * D]
        if t > 0:
            p.wait("scalar", s["sT"].at())
        else:
            p.wait("scalar", s["sV"].at())
        p.s("activation", xn, xt, AF.Square, inc=[s["sS"].inc()])
        p.v("reduce_sum", st[:, 0:1], xt, axis=AX)
        p.wait("vector", s["sS"].at())
        p.v("reduce_sum", st[:, 1:2], xn, axis=AX)
        p.fence("vector", s["sV"])
        p.v("tensor_scalar", st[:, 2:3], st[:, 0:1], 1.0 / D, None, ALU.mult)
        p.fence("vector", s["sV"])
        p.v("tensor_tensor", st[:, 3:4], st[:, 2:3], st[:, 2:3], ALU.mult)
        p.fence("vector", s["sV"])
        p.v("tensor_scalar", st[:, 4:5], st[:, 1:2], 1.0 / D, st[:, 3:4],
            ALU.mult, ALU.subtract, inc=[s["sV"].inc()])
        p.wait("scalar", s["sV"].at())
        p.s("activation", st[:, 5:6], st[:, 4:5], AF.Ln, bias=c.eps, scale=1.0)
        p.fence("scalar", s["sS"])
        p.s("activation", st[:, 6:7], st[:, 5:6], AF.Exp, scale=-0.5,
            inc=[s["sS"].inc()])
        p.wait("vector", s["sS"].at())
        p.v("tensor_scalar", st[:, 7:8], st[:, 2:3], st[:, 6:7], -1.0,
            ALU.mult, ALU.mult)
        p.fence("vector", s["sV"])
        p.v("tensor_scalar", xn, xt, st[:, 6:7], st[:, 7:8],
            ALU.mult, ALU.add, inc=[s["sV"].inc()])
        if rowsum:
            p.fence("vector", s["sV"])
            p.v("reduce_sum", c.rs8[:, t:t + 1], xn, axis=AX,
                inc=[s["sV"].inc()])
        p.wait("tensor", s["sV"].at())
        for g in range(4):
            bank = 6 + (g % 2)
            bview = pb[:, bank * 1024:(bank + 1) * 1024]
            key = (t * 4 + g) - 2
            if key >= 0:
                p.wait("tensor", (s["sV"].h, evac_done[key]))
            for kk in range(8):
                kc = g * 8 + kk
                p.t("transpose", bview[:, kk * 128:(kk + 1) * 128],
                    xn[:, kc * 128:(kc + 1) * 128], c.ident,
                    inc=([s["sT"].inc()] if kk == 7 else []))
            p.wait("vector", s["sT"].at())
            dst = c.xnT[:].rearrange("p (k c) -> p k c", k=KC)[
                :, g * 8:(g + 1) * 8, t * 128:(t + 1) * 128]
            src = bview.rearrange("p (k c) -> p k c", k=8)
            p.v("tensor_copy", dst, src, inc=[s["sV"].inc()])
            evac_done[t * 4 + g] = s["sV"].n
    p.wait("tensor", s["sV"].at())


def emit_dense_fm(ctx, pieces_ap, n_out, tail, extra_mm=None, rank1=None):
    c, p, s = ctx, ctx.p, ctx.sem
    n_og = n_out // OPP
    acc = c.psA.ap()[:, 0:2048]
    wslot = [c.wbuf.ap()[:, 0:KPP * OPP],
             c.wbuf.ap()[:, 1024:1024 + KPP * OPP]]
    slot_free = [s["sT"].n, s["sT"].n]
    dma_done = {}

    wsem = [s["sW0"], s["sW1"]]

    def issue_piece(idx):
        og_, j_ = idx // NJ, idx % NJ
        sl_ = 0 if SERIAL_W else idx % 2
        p.wait("gpsimd", (s["sT"].h, slot_free[sl_]))
        p.g("dma_start", out=wslot[sl_], in_=pieces_ap(og_, j_),
            inc=[wsem[sl_].inc(16)])
        dma_done[idx] = wsem[sl_].at()

    if not SERIAL_W:
        issue_piece(0)
        issue_piece(1)
    for og in range(n_og):
        for kc in range(KC):
            j, ks = kc // KPP, kc % KPP
            idx = og * NJ + j
            sl = 0 if SERIAL_W else idx % 2
            if ks == 0:
                if SERIAL_W:
                    issue_piece(idx)
                p.wait("tensor", dma_done[idx])
            if kc == TRAIL_AT and getattr(c, "tail_trail", None) is not None:
                c.tail_trail()
                c.tail_trail = None
            for oc2 in range(2):
                lhsT = wslot[sl][:, ks * OPP + oc2 * 128: ks * OPP + (oc2 + 1) * 128]
                a = acc[:, oc2 * 1024:(oc2 + 1) * 1024]
                last = (kc == KC - 1) and (extra_mm is None)
                p.t("matmul", a[:, 0:512], lhsT,
                    c.xnT[:, kc * TOK: kc * TOK + 512],
                    start=(kc == 0), stop=False)
                rec = p.t("matmul", a[:, 512:1024], lhsT,
                          c.xnT[:, kc * TOK + 512: (kc + 1) * TOK],
                          start=(kc == 0), stop=(last and oc2 == 1))
                if ks == KPP - 1 and oc2 == 1:
                    rec[5].append(s["sT"].inc())
                    slot_free[sl] = s["sT"].n
                    if not SERIAL_W and idx + 2 < n_og * NJ:
                        issue_piece(idx + 2)
        if extra_mm is not None:
            extra_mm(og)
        if rank1 is not None:
            urow_fn, vrow_fn = rank1
            for oc2 in range(2):
                oc = og * 2 + oc2
                a = acc[:, oc2 * 1024:(oc2 + 1) * 1024]
                vr = vrow_fn(oc)
                p.t("matmul", a[:, 0:512], urow_fn(oc),
                    vr[:, 0:512], start=False, stop=False)
                p.t("matmul", a[:, 512:1024], urow_fn(oc),
                    vr[:, 512:1024], start=False, stop=True)
        fl = pe_flush(c, acc[:, 0:128])
        fl[5].append(s["sT"].inc())
        tail(og, acc, s["sT"].at())
        if getattr(c, "tail_evac_done", None) is not None:
            p.wait("tensor", *c.tail_evac_done)
            c.tail_evac_done = None
    if getattr(c, "tail_trail", None) is not None:
        c.tail_trail()
        c.tail_trail = None


def emit_ffn_tail(ctx, og, acc, acc_ready, s05, b05, dual_ev=True):
    c, p, s = ctx, ctx.p, ctx.sem
    pb = psb(c)
    ev0 = c.scr.ap()[:, 1024:2048]
    ev1 = c.scr.ap()[:, 3072:4096] if dual_ev else ev0
    evs = [ev0, ev1]
    if not hasattr(c, "tail_add_done"):
        c.tail_add_done = {}
        c.tail_n = 0
        c.tail_ev_free = {}
    ev_done = []

    def evac(oc2):
        oc = og * 2 + oc2
        eng = "scalar" if (dual_ev and oc2 == 0) else "vector"
        p.wait(eng, acc_ready)
        key = (dual_ev, oc2)
        if c.tail_ev_free.get(key) is not None:
            p.wait(eng, (s["sT"].h, c.tail_ev_free[key]))
        sc = s05 if isinstance(s05, float) else s05[:, oc:oc + 1]
        if eng == "scalar":
            p.s("activation", evs[oc2], acc[:, oc2 * 1024:(oc2 + 1) * 1024],
                AF.Identity, bias=b05[:, oc:oc + 1], scale=sc,
                inc=[s["sS"].inc()])
            ev_done.append(("s", s["sS"].n))
        else:
            p.v("tensor_scalar", evs[oc2], acc[:, oc2 * 1024:(oc2 + 1) * 1024],
                sc, b05[:, oc:oc + 1], ALU.mult, ALU.add, inc=[s["sV"].inc()])
            ev_done.append(("v", s["sV"].n))

    def trail(oc2):
        oc = og * 2 + oc2
        ev = evs[oc2]
        key = (dual_ev, oc2)
        eng, val = ev_done[oc2]
        p.wait("tensor", (s["sS" if eng == "s" else "sV"].h, val))
        for t in range(NT):
            bank = 4 + (c.tail_n % 2)
            bv = pb[:, bank * 1024: bank * 1024 + 128]
            k2 = c.tail_n - 2
            if k2 >= 0:
                p.wait("tensor", (s["sV"].h, c.tail_add_done[k2]))
            p.t("transpose", bv, ev[:, t * 128:(t + 1) * 128], c.ident,
                inc=[s["sT"].inc()])
            if t == NT - 1:
                c.tail_ev_free[key] = s["sT"].n
                if not dual_ev:
                    c.tail_ev_free[(dual_ev, 1 - oc2)] = s["sT"].n
            p.wait("vector", s["sT"].at())
            dst = c.sx[:, t * D + oc * 128: t * D + (oc + 1) * 128]
            p.v("scalar_tensor_tensor", out=dst, in0=bv, scalar=1.0,
                in1=dst, op0=ALU.mult, op1=ALU.add, inc=[s["sV"].inc()])
            c.tail_add_done[c.tail_n] = s["sV"].n
            c.tail_n += 1

    if dual_ev:
        evac(0)
        evac(1)
        c.tail_evac_done = [s["sV"].at(), s["sS"].at()]

        def trail_all():
            trail(0)
            trail(1)
        if DEFER_TRAIL:
            assert getattr(c, "tail_trail", None) is None, "trail dropped"
            c.tail_trail = trail_all
        else:
            trail_all()
    else:
        evac(0)
        trail(0)
        evac(1)
        c.tail_evac_done = [s["sV"].at()]
        if DEFER_TRAIL:
            assert getattr(c, "tail_trail", None) is None, "trail dropped"
            c.tail_trail = lambda: trail(1)
        else:
            trail(1)


def emit_router_lora(ctx, wr_ap, A_ap, B_ap):
    c, p, s = ctx, ctx.p, ctx.sem
    hs = c.scr.ap()[:, 0:1024]
    lg = c.scr.ap()[0:8, 2048:3072]
    cT = c.scr.ap()[0:8, 3072:4096]
    hf = c.scrf[:, 512:1024]
    l8, r8, msk = c.l8, c.r8, c.msk
    wrb = c.wbuf.ap()[:, 0:256]
    p.g("dma_start", out=wrb, in_=wr_ap, inc=[s["sW0"].inc(16)])
    wr_done = s["sW0"].at()

    lg_ps = c.psA.ap()[0:8, 0:1024]
    p.wait("tensor", wr_done)
    for kc in range(KC):
        lhsT = wrb[:, kc * 8:(kc + 1) * 8]
        p.t("matmul", lg_ps[:, 0:512], lhsT, c.xnT[:, kc * TOK:kc * TOK + 512],
            start=(kc == 0), stop=False)
        p.t("matmul", lg_ps[:, 512:1024], lhsT,
            c.xnT[:, kc * TOK + 512:(kc + 1) * TOK],
            start=(kc == 0), stop=(kc == KC - 1))
    fl = pe_flush(c, c.psA.ap()[:, 0:128])
    fl[5].append(s["sT"].inc())
    p.wait("vector", s["sT"].at())
    p.v("tensor_copy", lg, lg_ps, inc=[s["sV"].inc()])

    p.wait("gpsimd", s["sV"].at())
    aslot = [c.wbuf.ap()[:, 0:1024], c.wbuf.ap()[:, 1024:2048]]
    a_done = {}
    a_free = {}
    h_ps = c.psA.ap()[:, 0:1024]

    asem = [s["sW0"], s["sW1"]]

    def issue_a(ch_):
        sl_ = ch_ % 2
        if ch_ in (2, 3):
            p.wait("gpsimd", (s["sT"].h, a_free[ch_ - 2]))
        p.g("dma_start", out=aslot[sl_], in_=A_ap[ch_],
            inc=[asem[sl_].inc(16)])
        a_done[ch_] = asem[sl_].at()

    issue_a(0)
    issue_a(1)
    for kc in range(KC):
        ch = kc // 8
        sl = ch % 2
        if kc % 8 == 0:
            p.wait("tensor", a_done[ch])
        lhsT = aslot[sl][:, (kc % 8) * 128:(kc % 8 + 1) * 128]
        p.t("matmul", h_ps[:, 0:512], lhsT, c.xnT[:, kc * TOK:kc * TOK + 512],
            start=(kc == 0), stop=False)
        rec = p.t("matmul", h_ps[:, 512:1024], lhsT,
                  c.xnT[:, kc * TOK + 512:(kc + 1) * TOK],
                  start=(kc == 0), stop=(kc == KC - 1))
        if kc % 8 == 7:
            rec[5].append(s["sT"].inc())
            a_free[ch] = s["sT"].n
            if ch + 2 < 4:
                issue_a(ch + 2)
    fl = pe_flush(c, h_ps[:, 0:128])
    fl[5].append(s["sT"].inc())
    h_done = s["sT"].at()

    lgT_ps = psb(c)[:, 3072:3080]
    cbT_ps = psb(c)[0:8, 3584:3712]
    for t in range(NT):
        p.wait("tensor", s["sV"].at())
        p.t("transpose", lgT_ps, lg[:, t * 128:(t + 1) * 128],
            c.ident[0:8, 0:8], inc=[s["sT"].inc()])
        p.wait("vector", s["sT"].at())
        p.v("tensor_copy", l8, lgT_ps)
        p.fence("vector", s["sV"])
        p.v("reduce_max", r8[:, 0:1], l8, axis=AX)
        p.fence("vector", s["sV"])
        p.v("tensor_scalar", r8[:, 2:3], r8[:, 0:1], -1.0, None, ALU.mult)
        p.v("tensor_single_scalar", r8[:, 8:16], l8, r8[:, 0:1], ALU.is_equal)
        p.fence("vector", s["sV"])
        p.v("scalar_tensor_tensor", out=r8[:, 8:16], in0=r8[:, 8:16],
            scalar=NEG, in1=l8, op0=ALU.mult, op1=ALU.add)
        p.fence("vector", s["sV"])
        p.v("reduce_max", r8[:, 1:2], r8[:, 8:16], axis=AX)
        p.fence("vector", s["sV"])
        p.v("tensor_tensor", r8[:, 3:4], r8[:, 1:2], r8[:, 0:1], ALU.subtract,
            inc=[s["sV"].inc()])
        p.wait("scalar", s["sV"].at())
        p.s("activation", r8[:, 4:5], r8[:, 3:4], AF.Exp, inc=[s["sS"].inc()])
        p.wait("vector", s["sS"].at())
        p.v("tensor_scalar", r8[:, 5:6], r8[:, 4:5], 1.0, None, ALU.add)
        p.fence("vector", s["sV"])
        p.v("reciprocal", r8[:, 6:7], r8[:, 5:6], inc=[s["sV"].inc()])
        p.wait("scalar", s["sV"].at())
        p.s("activation", r8[:, 8:16], l8, AF.Exp, bias=r8[:, 2:3],
            scale=1.0, inc=[s["sS"].inc()])
        p.wait("vector", s["sS"].at())
        p.v("tensor_single_scalar", msk, l8, r8[:, 1:2], ALU.is_ge)
        p.fence("vector", s["sV"])
        p.v("tensor_scalar", r8[:, 8:16], r8[:, 8:16], r8[:, 6:7], None,
            ALU.mult)
        p.fence("vector", s["sV"])
        p.v("tensor_tensor", c.cb8, r8[:, 8:16], msk, ALU.mult,
            inc=[s["sV"].inc()])
        if t == 0 and getattr(c, "dbg_cst", None) is not None:
            p.wait("sync", s["sV"].at())
            p.y("dma_start", out=c.dbg_cst, in_=c.cst[:, :],
                inc=[s["sDS"].inc(16), s["sG"].inc()])
            p.wait("vector", s["sG"].at())
        p.wait("tensor", s["sV"].at())
        p.t("transpose", cbT_ps, c.cb8, c.ident, inc=[s["sT"].inc()])
        p.wait("vector", s["sT"].at())
        p.v("tensor_copy", cT[:, t * 128:(t + 1) * 128], cbT_ps,
            inc=[s["sV"].inc()])
    p.wait("tensor", s["sV"].at())
    srep_ps = c.psA.ap()[:, 1024:1536]
    for half in range(2):
        p.t("matmul", srep_ps, c.sb_E, cT[:, half * 512:(half + 1) * 512],
            start=True, stop=True)
        fl = pe_flush(c, srep_ps[:, 0:128])
        fl[5].append(s["sT"].inc())
        sr_done = s["sT"].at()
        p.wait("scalar", h_done)
        p.s("activation", hf, h_ps[:, half * 512:(half + 1) * 512], AF.Copy,
            inc=[s["sS"].inc()])
        p.wait("vector", sr_done, s["sS"].at())
        p.v("tensor_tensor", hs[:, half * 512:(half + 1) * 512], hf, srep_ps,
            ALU.mult, inc=[s["sV"].inc()])
        p.wait("tensor", s["sV"].at())
        p.wait("scalar", s["sV"].at())

    bslot = [c.scr.ap()[:, 2048:2304], c.scr.ap()[:, 2304:2560]]
    bfree = [None, None]
    bdone = {}

    bsem = [s["sB0"], s["sB1"]]

    def pre_og(og):
        sl = og % 2
        gate = bfree[sl] if bfree[sl] is not None else s["sT"].n
        p.wait("gpsimd", (s["sT"].h, gate))
        p.g("dma_start", out=bslot[sl], in_=B_ap[og], inc=[bsem[sl].inc(16)])
        bdone[og] = bsem[sl].at()

    def extra_mm(og):
        sl = og % 2
        p.wait("tensor", bdone[og])
        for oc2 in range(2):
            lhsT = bslot[sl][:, oc2 * 128:(oc2 + 1) * 128]
            a = c.psA.ap()[:, oc2 * 1024:(oc2 + 1) * 1024]
            p.t("matmul", a[:, 0:512], lhsT, hs[:, 0:512],
                start=False, stop=False)
            rec = p.t("matmul", a[:, 512:1024], lhsT, hs[:, 512:1024],
                      start=False, stop=(oc2 == 1))
            if oc2 == 1:
                rec[5].append(s["sT"].inc())
                bfree[sl] = s["sT"].n
        return rec
    return pre_og, extra_mm




DEBUG_FFN = False
NO_LORA = False
SERIAL_W = False
DEFER_TRAIL = True
TRAIL_AT = 4


def build_ffn_l0():
    es = ExitStack()
    nc = bass.Bass("TRN2", target_bir_lowering=False)
    x_in = nc.dram_tensor("x_in", [TOK, D], F32, kind="ExternalInput").ap()
    x_out = nc.dram_tensor("x_out", [TOK, D], F32, kind="ExternalOutput").ap()
    wb = nc.dram_tensor("wbase", [16, NJ, 128, KPP * OPP], F32,
                        kind="ExternalInput").ap()
    b_arr = nc.dram_tensor("b_arr", [128, 32], F32, kind="ExternalInput").ap()
    wr = nc.dram_tensor("wr", [128, 256], F32, kind="ExternalInput").ap()
    A_in = nc.dram_tensor("A_in", [4, 128, 1024], F32, kind="ExternalInput").ap()
    B_in = nc.dram_tensor("B_in", [16, 128, 256], F32, kind="ExternalInput").ap()
    E_in = nc.dram_tensor("E_in", [8, 128], F32, kind="ExternalInput").ap()
    id_in = nc.dram_tensor("id_in", [128, 128], F32, kind="ExternalInput").ap()

    ctx = Ctx(nc, es)
    common_tensors(ctx)
    c, p, s = ctx, ctx.p, ctx.sem
    emit_const_loads(ctx, [(c.b05, b_arr)],
                     gp=[(c.ident, id_in), (c.sb_E, E_in)])
    emit_stream_load(ctx, x_in)
    p.wait("vector", s["sDX"].at(), s["sDW"].at())
    p.v("memset", c.eps, 1e-5)
    p.v("memset", c.zero128, 0.0)
    p.v("tensor_scalar", c.b05, c.b05, 0.5, None, ALU.mult,
        inc=[s["sV"].inc()])
    if DEBUG_FFN:
        ctx.dbg_cst = nc.dram_tensor("dbg_cst", [128, 512], F32,
                                     kind="ExternalOutput").ap()
        dbg_xnT = nc.dram_tensor("dbg_xnT", [128, KC * TOK], BF16,
                                 kind="ExternalOutput").ap()
        dbg_lg = nc.dram_tensor("dbg_lg", [8, 1024], BF16,
                                kind="ExternalOutput").ap()
        dbg_cT = nc.dram_tensor("dbg_cT", [8, 1024], BF16,
                                kind="ExternalOutput").ap()
        dbg_hs = nc.dram_tensor("dbg_hs", [128, 1024], BF16,
                                kind="ExternalOutput").ap()
    emit_ln(ctx)
    if DEBUG_FFN:
        p.wait("sync", s["sV"].at())
        p.y("dma_start", out=dbg_xnT, in_=c.xnT[:, :], inc=[s["sDS"].inc(16)])
    pre_og, extra = emit_router_lora(ctx, wr, A_in, B_in)
    if DEBUG_FFN:
        p.wait("sync", s["sV"].at())
        p.y("dma_start", out=dbg_lg, in_=c.scr.ap()[0:8, 2048:3072], inc=[s["sDS"].inc(16)])
        p.y("dma_start", out=dbg_cT, in_=c.scr.ap()[0:8, 3072:4096], inc=[s["sDS"].inc(16)])
        p.y("dma_start", out=dbg_hs, in_=c.scr.ap()[:, 0:1024], inc=[s["sDS"].inc(16)])
        p.op("sync", None, inc=[s["sG"].inc()])
        p.wait("gpsimd", s["sG"].at())

    def tail(og, acc, rdy):
        emit_ffn_tail(ctx, og, acc, rdy, 0.5, c.b05)

    done_og = set()

    def pieces_ap(og, j):
        if og not in done_og:
            done_og.add(og)
            pre_og(og)
        return wb[og, j]

    emit_dense_fm(ctx, pieces_ap, D, tail,
                  extra_mm=(None if NO_LORA else extra))
    emit_stream_store(ctx, x_out)

    with nc.Block() as block:
        p.replay(block)
    es.close()
    return nc


def ffn_l0_inputs(inputs):
    wbT = np.ascontiguousarray(inputs["L0_ffn_W_base"].T)
    wb = arrange_pieces(wbT)
    b_arr = cols128(np.asarray(inputs["L0_ffn_b_base"]))
    wr = np.ascontiguousarray(
        np.asarray(inputs["L0_ffn_W_router"]).T.reshape(32, 128, 8)
        .transpose(1, 0, 2).reshape(128, 256))
    A = np.ascontiguousarray(
        np.asarray(inputs["L0_ffn_A_stack"]).transpose(2, 0, 1)
        .reshape(32, 128, 128).transpose(1, 0, 2).reshape(128, 4096))
    A = np.ascontiguousarray(A.reshape(128, 4, 1024).transpose(1, 0, 2))
    B = np.ascontiguousarray(
        np.asarray(inputs["L0_ffn_B_stack"]).transpose(0, 2, 1)
        .reshape(128, 4096).reshape(128, 16, 256).transpose(1, 0, 2))
    return {
        "wbase": wb, "b_arr": b_arr, "wr": wr, "A_in": A, "B_in": B,
        "E_in": e_matrix(), "id_in": np.eye(128, dtype=np.float32),
    }


def run_ffn_l0(xs, statics):
    nc = build_ffn_l0()
    in_maps = [dict(statics, x_in=xs[i]) for i in range(8)]
    res = run_bass_kernel_spmd(nc, in_maps, core_ids=list(range(8)))
    return [r["x_out"] for r in res.results]




def emit_qkv_tail(ctx, og, acc, acc_ready, out_ap, rope=False):
    c, p, s = ctx, ctx.p, ctx.sem
    evs = [c.scr.ap()[:, 1024:2048], c.scr.ap()[:, 2048:3072]]
    sems = [s["sB0"], s["sB1"]]
    for oc2 in range(2):
        oc = og * 2 + oc2
        sl = oc % 2
        ev = evs[sl]
        if rope:
            p.wait("vector", acc_ready, sems[sl].at())
            a = acc[:, oc2 * 1024:(oc2 + 1) * 1024]
            t1 = c.psA.ap()[:, 2048:2560]
            cosb = c.scr.ap()[:, 0:1024]
            sinb = c.scr.ap()[:, 3072:4096]
            for h in range(2):
                ah = a[:, h * 512:(h + 1) * 512]
                ch = cosb[:, h * 512:(h + 1) * 512]
                sh = sinb[:, h * 512:(h + 1) * 512]
                evh = ev[:, h * 512:(h + 1) * 512]
                p.v("scalar_tensor_tensor", out=t1[0:64, :], in0=ah[64:128, :],
                    scalar=-1.0, in1=sh[0:64, :], op0=ALU.mult, op1=ALU.mult)
                p.v("scalar_tensor_tensor", out=t1[64:128, :], in0=ah[0:64, :],
                    scalar=1.0, in1=sh[64:128, :], op0=ALU.mult, op1=ALU.mult)
                p.v("tensor_tensor", evh, ah, ch, ALU.mult)
                p.fence("vector", s["sV"])
                p.v("tensor_tensor", evh, evh, t1, ALU.add,
                    inc=[s["sV"].inc()])
        else:
            p.wait("scalar", acc_ready, sems[sl].at(), s["sV"].at())
            p.s("activation", ev, acc[:, oc2 * 1024:(oc2 + 1) * 1024],
                AF.Copy, inc=[s["sS"].inc()])
        if rope:
            p.wait("sync", s["sV"].at())
        else:
            p.wait("sync", s["sS"].at())
        p.y("dma_start", out=out_ap[oc * 128:(oc + 1) * 128, :], in_=ev,
            inc=[sems[sl].inc(16)])
    if rope:
        p.wait("tensor", s["sV"].at())
    else:
        p.wait("tensor", s["sS"].at())


def build_qkv(layer):
    es = ExitStack()
    nc = bass.Bass("TRN2", target_bir_lowering=False)
    nkv = 32 if layer == 0 else 8
    x_in = nc.dram_tensor("x_in", [TOK, D], F32, kind="ExternalInput").ap()
    wq = nc.dram_tensor("wq", [16, NJ, 128, KPP * OPP], F32, kind="ExternalInput").ap()
    wk = nc.dram_tensor("wk", [nkv // 2, NJ, 128, KPP * OPP], F32, kind="ExternalInput").ap()
    wv = nc.dram_tensor("wv", [nkv // 2, NJ, 128, KPP * OPP], F32, kind="ExternalInput").ap()
    id_in = nc.dram_tensor("id_in", [128, 128], F32, kind="ExternalInput").ap()
    qT = nc.dram_tensor("qT", [4096, TOK], BF16, kind="ExternalOutput").ap()
    kT = nc.dram_tensor("kT", [nkv * 128, TOK], BF16, kind="ExternalOutput").ap()
    vT = nc.dram_tensor("vT", [nkv * 128, TOK], BF16, kind="ExternalOutput").ap()
    if layer == 1:
        cos_in = nc.dram_tensor("cos_in", [128, TOK], F32, kind="ExternalInput").ap()
        sin_in = nc.dram_tensor("sin_in", [128, TOK], F32, kind="ExternalInput").ap()

    ctx = Ctx(nc, es)
    common_tensors(ctx)
    c, p, s = ctx, ctx.p, ctx.sem
    emit_const_loads(ctx, [], gp=[(c.ident, id_in)])
    emit_stream_load(ctx, x_in)
    p.wait("vector", s["sDX"].at())
    p.wait("vector", s["sDW"].at())
    p.v("memset", c.eps, 1e-5)
    p.v("memset", c.zero128, 0.0, inc=[s["sV"].inc()])
    p.wait("tensor", s["sV"].at())
    emit_ln(ctx)
    if layer == 1:
        p.wait("gpsimd", s["sT"].at(), s["sV"].at())
        p.g("dma_start", out=c.scr.ap()[:, 0:1024], in_=cos_in,
            inc=[s["sDW"].inc(16)])
        p.g("dma_start", out=c.scr.ap()[:, 3072:4096], in_=sin_in,
            inc=[s["sDW"].inc(16)])
        p.wait("vector", s["sDW"].at())
    rope = (layer == 1)
    emit_dense_fm(ctx, lambda og, j: wq[og, j], D,
                  lambda og, acc, rdy: emit_qkv_tail(ctx, og, acc, rdy, qT, rope=rope))
    emit_dense_fm(ctx, lambda og, j: wk[og, j], nkv * 128,
                  lambda og, acc, rdy: emit_qkv_tail(ctx, og, acc, rdy, kT, rope=rope))
    emit_dense_fm(ctx, lambda og, j: wv[og, j], nkv * 128,
                  lambda og, acc, rdy: emit_qkv_tail(ctx, og, acc, rdy, vT, rope=False))
    p.wait("sync", s["sB0"].at(), s["sB1"].at())
    with nc.Block() as block:
        p.replay(block)
    es.close()
    return nc


def emit_attn_core(ctx, qT_ap, kT_ap, vT_ap, aout_ap, nh, nkv, causal):
    c, p, s = ctx, ctx.p, ctx.sem
    B = c.big.ap()
    pb = psb(c)
    Kh = [B[:, 0:4096], B[:, 4096:8192]]
    VTh = B[:, 8192:12288]
    expTf = B[:, 8192:12288]
    Vh = [B[:, 12288:16384], B[:, 16384:20480]]
    Qs = [B[:, 20480:21504], B[:, 21504:22528]]
    expS = B[:, 22528:26624]
    outT = [B[:, 26624:27648], B[:, 27648:28672]]
    mwin = B[:, 28928:29440]
    rrow = B[0:1, 29952:30080]
    rbcb = B[:, 30080:30208]
    onesr = B[0:1, 30208:30336]
    l_ = c.stat[:, 8:9]
    r_ = c.stat[:, 9:10]
    negm = c.stat[:, 10:11]
    m_ = c.stat[:, 11:12]
    sK = [s["sK0"], s["sK1"]]
    sU = s["sU0"]
    sQ = [s["sQ0"], s["sQ1"]]
    sO = [s["sO0"], s["sO1"]]
    kslot_pe = [0, 0]
    qslot_pe = [0, 0]
    vt_pe = 0
    group = nh // nkv
    nrank = kT_ap.shape[0] // (nkv * 128)
    for kv in range(nkv):
        ksl = kv % 2
        p.wait("gpsimd", (s["sT"].h, kslot_pe[ksl]))
        for r in range(nrank):
            p.g("dma_start", out=Kh[ksl][:, r * 1024:(r + 1) * 1024],
                in_=kT_ap[r * nkv * 128 + kv * 128:
                          r * nkv * 128 + (kv + 1) * 128, :],
                inc=[sK[ksl].inc(16)])
        k_done = sK[ksl].at()
        p.wait("gpsimd", (s["sT"].h, vt_pe))
        for r in range(nrank):
            p.g("dma_start", out=VTh[:, r * 1024:(r + 1) * 1024],
                in_=vT_ap[r * nkv * 128 + kv * 128:
                          r * nkv * 128 + (kv + 1) * 128, :],
                inc=[sU.inc(16)])
        vt_done = sU.at()
        vsl = kv % 2
        p.wait("tensor", vt_done, s["sV"].at())
        ev_done = {}
        for g in range(8):
            bank = g % 2
            bv = pb[:, bank * 1024:bank * 1024 + 512]
            if g >= 2:
                p.wait("tensor", (s["sV"].h, ev_done[g - 2]))
            for kk in range(4):
                kc = g * 4 + kk
                p.t("transpose", bv[:, kk * 128:(kk + 1) * 128],
                    VTh[:, kc * 128:(kc + 1) * 128], c.ident,
                    inc=([s["sT"].inc()] if kk == 3 else []))
            p.wait("vector", s["sT"].at())
            p.v("tensor_copy", Vh[vsl][:, g * 512:(g + 1) * 512], bv,
                inc=[s["sV"].inc()])
            ev_done[g] = s["sV"].n
        p.wait("tensor", s["sV"].at())
        for qg in range(group):
            qh = kv * group + qg
            qsl = qh % 2
            p.wait("gpsimd", (s["sT"].h, qslot_pe[qsl]))
            p.g("dma_start", out=Qs[qsl], in_=qT_ap[qh * 128:(qh + 1) * 128, :],
                inc=[sQ[qsl].inc(16)])
            q_done = sQ[qsl].at()
            osl = qh % 2
            p.wait("vector", sO[osl].at())
            for t in range(NT):
                if causal:
                    w = (t + 1) * 128
                    ranges = [(r * 1024, w) for r in range(4)]
                else:
                    w = 4096
                    ranges = [(0, 4096)]
                kwidth = sum(rr[1] for rr in ranges)
                p.wait("tensor", k_done, q_done, s["sV"].at())
                sview = c.psA.ap().rearrange("p (r q) -> p r q", r=4)
                mview = mwin.rearrange("p (r c) -> p r c", r=4)
                m4 = c.stat[:, 12:16]
                mh = c.stat[:, 1:3]
                sh_done = []
                for hf in range(2):
                    for col0, wd in ranges[2 * hf:2 * hf + 2] if causal else                             [(hf * 2048, 2048)]:
                        i = 0
                        while i < wd:
                            n = min(512, wd - i)
                            p.t("matmul", c.psA.ap()[:, col0 + i:col0 + i + n],
                                Qs[qsl][:, t * 128:(t + 1) * 128],
                                Kh[ksl][:, col0 + i:col0 + i + n],
                                start=True, stop=True)
                            i += n
                    fl = pe_flush(c, c.psA.ap()[:, hf * 2048:hf * 2048 + 128])
                    fl[5].append(s["sT"].inc())
                    sh_done.append(s["sT"].at())
                s_done = s["sT"].at()
                kslot_pe[ksl] = s["sT"].n
                qslot_pe[qsl] = s["sT"].n
                for hf in range(2):
                    p.wait("vector", sh_done[hf])
                    if causal:
                        dv = sview[:, 2 * hf:2 * hf + 2, t * 128:(t + 1) * 128]
                        mv = mview[:, 2 * hf:2 * hf + 2, :]
                        p.v("tensor_tensor", dv, dv, mv, ALU.add)
                        p.fence("vector", s["sV"])
                        ssub = c.psA.ap().rearrange(
                            "p (r a z) -> p r a z", r=4, z=8)[
                            :, 2 * hf:2 * hf + 2, 0:w // 8, 0]
                        p.v("reduce_max", m4[:, 2 * hf:2 * hf + 2], ssub,
                            axis=AX)
                        p.fence("vector", s["sV"])
                        p.v("reduce_max", mh[:, hf:hf + 1],
                            m4[:, 2 * hf:2 * hf + 2], axis=AX)
                    else:
                        ssub = c.psA.ap().rearrange(
                            "p (a z) -> p a z", z=8)[
                            :, hf * 256:(hf + 1) * 256, 0]
                        p.v("reduce_max", mh[:, hf:hf + 1], ssub, axis=AX)
                    p.fence("vector", s["sV"])
                p.v("reduce_max", m_, mh, axis=AX)
                p.fence("vector", s["sV"])
                p.v("tensor_scalar", negm, m_, -SCL, None, ALU.mult,
                    inc=[s["sV"].inc()])
                p.wait("scalar", s["sV"].at())
                l_b = c.stat[:, 0:1]
                if causal:
                    eviewA = expS[:, 0:2 * w].rearrange(
                        "p (r q) -> p r q", r=2)
                    eviewB = expS[:, 2 * w:4 * w].rearrange(
                        "p (r q) -> p r q", r=2)
                    p.s("activation", eviewA, sview[:, 0:2, 0:w],
                        AF.Exp, bias=negm, scale=SCL, accum_out=l_,
                        inc=[s["sS"].inc()])
                    p.s("activation", eviewB, sview[:, 2:4, 0:w],
                        AF.Exp, bias=negm, scale=SCL, accum_out=l_b,
                        inc=[s["sS"].inc()])
                else:
                    p.s("activation", expS[:, 0:2048], c.psA.ap()[:, 0:2048],
                        AF.Exp, bias=negm, scale=SCL, accum_out=l_,
                        inc=[s["sS"].inc()])
                    p.s("activation", expS[:, 2048:4096],
                        c.psA.ap()[:, 2048:4096],
                        AF.Exp, bias=negm, scale=SCL, accum_out=l_b,
                        inc=[s["sS"].inc()])
                exp_done = s["sS"].n
                p.wait("vector", s["sS"].at())
                p.v("tensor_tensor", l_, l_, l_b, ALU.add)
                p.fence("vector", s["sV"])
                p.v("reciprocal", r_, l_)
                p.fence("vector", s["sV"])
                p.v("tensor_copy", c.stat.bitcast(BF16)[:, 24:25], r_,
                    inc=[s["sV"].inc()])
                r_bf = c.stat.bitcast(BF16)[:, 24:25]
                rv_done = s["sV"].at()
                nkc = kwidth // 128
                out_ps = c.psA.ap()[:, 2048:2176]

                def vcol(cc):
                    if not causal:
                        return cc * 128
                    rr_, jj = cc // (w // 128), cc % (w // 128)
                    return rr_ * 1024 + jj * 128

                half = 8 * ((nkc // 2 + 7) // 8)
                if half >= nkc:
                    half = nkc
                p.wait("tensor", (s["sS"].h, exp_done - 1))
                ev_t = []
                for cc in range(nkc):
                    if cc == nkc // 2:
                        p.wait("tensor", (s["sS"].h, exp_done))
                    tslot = pb[:, cc * 128:(cc + 1) * 128]
                    rec = p.t("transpose", tslot,
                              expS[:, cc * 128:(cc + 1) * 128], c.ident)
                    if cc == half - 1 or cc == nkc - 1:
                        fl = p.t("transpose", tslot,
                                 expS[:, cc * 128:(cc + 1) * 128], c.ident)
                        fl[5].append(s["sT"].inc())
                        ev_t.append(s["sT"].n)
                p.wait("tensor", rv_done)
                rrow_ps = pb[0:1, 5 * 1024 + 512:5 * 1024 + 640]
                p.t("transpose", rrow_ps, r_bf, c.ident,
                    inc=[s["sT"].inc()])
                p.wait("vector", (s["sT"].h, ev_t[0]))
                p.v("tensor_copy", expTf[:, 0:half * 128],
                    pb[:, 0:half * 128], inc=[s["sV"].inc()])
                evA = s["sV"].n
                p.wait("vector", s["sT"].at())
                p.v("tensor_copy", rrow, rrow_ps)
                if nkc > half:
                    p.wait("vector", (s["sT"].h, ev_t[1]))
                    p.v("tensor_copy", expTf[:, half * 128:nkc * 128],
                        pb[:, half * 128:nkc * 128], inc=[s["sV"].inc()])
                evB = s["sV"].n
                p.op("vector", None, inc=[s["sV"].inc()])
                p.wait("tensor", (s["sV"].h, evA), (s["sS"].h, exp_done))
                for cc in range(half):
                    vc = vcol(cc)
                    p.t("matmul", out_ps, Vh[vsl][:, vc:vc + 128],
                        expTf[:, cc * 128:(cc + 1) * 128],
                        start=(cc == 0), stop=False)
                if nkc > half:
                    p.wait("tensor", (s["sV"].h, evB))
                for cc in range(half, nkc):
                    vc = vcol(cc)
                    p.t("matmul", out_ps, Vh[vsl][:, vc:vc + 128],
                        expTf[:, cc * 128:(cc + 1) * 128],
                        start=False, stop=(cc == nkc - 1))
                rbc_ps = c.psA.ap()[:, 2560:2688]
                p.wait("tensor", s["sV"].at())
                p.t("matmul", rbc_ps, onesr, rrow, start=True, stop=True)
                fl = pe_flush(c, rbc_ps)
                fl[5].append(s["sT"].inc())
                p.wait("vector", s["sT"].at())
                p.v("tensor_copy", rbcb, rbc_ps)
                p.fence("vector", s["sV"])
                p.v("tensor_tensor", outT[osl][:, t * 128:(t + 1) * 128],
                    out_ps, rbcb, ALU.mult, inc=[s["sV"].inc()])
            p.wait("sync", s["sV"].at())
            p.y("dma_start", out=aout_ap[qh * 128:(qh + 1) * 128, :],
                in_=outT[osl], inc=[sO[osl].inc(16)])
        vt_pe = s["sT"].n
    p.wait("tensor", s["sV"].at())


def emit_dense_tm(ctx, pieces_ap, attnT, n_out, opp_o=512, kpp_e=2):
    c, p, s = ctx, ctx.p, ctx.sem
    n_og = n_out // opp_o
    n_j = KC // kpp_e
    wslot = [c.wbuf.ap()[:, 0:kpp_e * opp_o],
             c.wbuf.ap()[:, 1024:1024 + kpp_e * opp_o]]
    wsem = [s["sW0"], s["sW1"]]
    slot_free = [s["sT"].n, s["sT"].n]
    dma_done = {}

    def issue_piece(idx):
        og_, j_ = idx // n_j, idx % n_j
        sl_ = idx % 2
        p.wait("gpsimd", (s["sT"].h, slot_free[sl_]))
        p.g("dma_start", out=wslot[sl_], in_=pieces_ap(og_, j_),
            inc=[wsem[sl_].inc(16)])
        dma_done[idx] = wsem[sl_].at()

    issue_piece(0)
    issue_piece(1)
    for og in range(n_og):
        for ec in range(KC):
            j, ks = ec // kpp_e, ec % kpp_e
            idx = og * n_j + j
            sl = idx % 2
            if ks == 0:
                p.wait("tensor", dma_done[idx])
            for t in range(NT):
                rec = p.t("matmul", c.psA.ap()[:, t * 512:(t + 1) * 512],
                          attnT[:, ec * TOK + t * 128: ec * TOK + (t + 1) * 128],
                          wslot[sl][:, ks * opp_o:(ks + 1) * opp_o],
                          start=(ec == 0), stop=(ec == KC - 1))
                if ks == kpp_e - 1 and t == NT - 1:
                    rec[5].append(s["sT"].inc())
                    slot_free[sl] = s["sT"].n
                    if idx + 2 < n_og * n_j:
                        issue_piece(idx + 2)
        fl = pe_flush(c, c.psA.ap()[:, 0:128])
        fl[5].append(s["sT"].inc())
        acc_ready = s["sT"].at()
        for t in range(NT):
            p.wait("vector", acc_ready)
            dst = c.sx[:, t * D + og * opp_o: t * D + (og + 1) * opp_o]
            p.v("tensor_tensor", dst, dst, c.psA.ap()[:, t * 512:(t + 1) * 512],
                ALU.add, inc=[s["sV"].inc()])
        p.wait("tensor", s["sV"].at())


def build_attnB(layer):
    es = ExitStack()
    nc = bass.Bass("TRN2", target_bir_lowering=False)
    nh = 32
    nkv = 32 if layer == 0 else 8
    causal = (layer == 1)
    x_in = nc.dram_tensor("x_in", [TOK, D], F32, kind="ExternalInput").ap()
    x_out = nc.dram_tensor("x_out", [TOK, D], F32, kind="ExternalOutput").ap()
    qT = nc.dram_tensor("qT", [4096, TOK], BF16, kind="ExternalInput").ap()
    kT = nc.dram_tensor("kT", [4 * nkv * 128, TOK], BF16, kind="ExternalInput").ap()
    vT = nc.dram_tensor("vT", [4 * nkv * 128, TOK], BF16, kind="ExternalInput").ap()
    wo = nc.dram_tensor("wo", [8, 16, 128, 1024], F32, kind="ExternalInput").ap()
    id_in = nc.dram_tensor("id_in", [128, 128], F32, kind="ExternalInput").ap()
    mw_in = nc.dram_tensor("mw_in", [128, 512], F32, kind="ExternalInput").ap()
    aout = nc.dram_tensor("aout", [4096, TOK], BF16, kind="Internal").ap()

    ctx = Ctx(nc, es)
    common_tensors(ctx)
    c, p, s = ctx, ctx.p, ctx.sem
    c.mksem("sK0", "sK1", "sU0", "sQ0", "sQ1", "sO0", "sO1")
    gp = [(c.ident, id_in)]
    if causal:
        gp.append((c.big.ap()[:, 28928:29440], mw_in))
    emit_const_loads(ctx, [], gp=gp)
    emit_stream_load(ctx, x_in)
    p.wait("vector", s["sDX"].at(), s["sDW"].at())
    p.v("memset", c.big.ap()[0:1, 30208:30336], 1.0)
    p.v("memset", c.zero128, 0.0, inc=[s["sV"].inc()])
    p.wait("tensor", s["sV"].at())
    emit_attn_core(ctx, qT, kT, vT, aout, nh, nkv, causal)
    p.wait("sync", s["sV"].at(), s["sO0"].at(), s["sO1"].at())
    for g4 in range(8):
        src = aout[g4 * 512:(g4 + 1) * 512, :].rearrange(
            "(e p) t -> p e t", p=128)
        dst = c.big.ap().rearrange("p (e t) -> p e t", e=KC)[
            :, g4 * 4:(g4 + 1) * 4, :]
        p.y("dma_start", out=dst, in_=src, inc=[s["sDX"].inc(16)])
    p.wait("tensor", s["sDX"].at())
    emit_dense_tm(ctx, lambda og, j: wo[og, j], c.big.ap(), D)
    emit_stream_store(ctx, x_out)
    with nc.Block() as block:
        p.replay(block)
    es.close()
    return nc


def emit_dequant(ctx, qw_ap, scratch_ap):
    c, p, s = ctx, ctx.p, ctx.sem
    B = c.big.ap()
    bi = c.big.ap().bitcast(I32)
    slab = B[:, 0:8192]
    u_int = bi[:, 4096:8192]
    u = B[:, 28672:32768]
    qsl = [bi[:, 8192:11264], bi[:, 11264:14336]]
    qsem = [s["sK0"], s["sK1"]]
    pb = psb(c)
    q_pe = [0, 0]
    for nt in range(32):
        sl = nt % 2
        p.wait("sync", (s["sV"].h, q_pe[sl]))
        p.y("dma_start", out=qsl[sl], in_=qw_ap[nt * 128:(nt + 1) * 128, :],
            inc=[qsem[sl].inc(16)])
        q_done = qsem[sl].at()
        qv = qsl[sl].rearrange("p (g r) -> p g r", r=3)
        uv = u_int.rearrange("p (g r) -> p g r", r=4)
        p.wait("vector", q_done, s["sT"].at())
        p.v("tensor_scalar", uv[:, :, 0], qv[:, :, 0], 63, None, ALU.bitwise_and)
        p.v("tensor_scalar", uv[:, :, 1], qv[:, :, 0], 6, 63,
            ALU.logical_shift_right, ALU.bitwise_and)
        p.v("tensor_scalar", uv[:, :, 2], qv[:, :, 1], 63, None, ALU.bitwise_and)
        p.v("tensor_scalar", uv[:, :, 3], qv[:, :, 2], 63, None, ALU.bitwise_and,
            inc=[s["sV"].inc()])
        q_pe[sl] = s["sV"].n
        p.fence("vector", s["sV"])
        p.v("tensor_copy", u, u_int, inc=[s["sV"].inc()])
        p.wait("tensor", s["sV"].at())
        half = nt % 2
        ev_done = {}
        for g in range(8):
            bank = 6 + g % 2
            bv = pb[:, bank * 1024:bank * 1024 + 512]
            if g >= 2:
                p.wait("tensor", (s["sV"].h, ev_done[g - 2]))
            for kk in range(4):
                kc = g * 4 + kk
                p.t("transpose", bv[:, kk * 128:(kk + 1) * 128],
                    u[:, kc * 128:(kc + 1) * 128], c.ident,
                    inc=([s["sT"].inc()] if kk == 3 else []))
            p.wait("vector", s["sT"].at())
            dst = slab.rearrange("p (j k h c) -> p j k h c", j=8, k=4, h=2)[
                :, g, :, half, :]
            src = bv.rearrange("p (k c) -> p k c", k=4)
            p.v("tensor_copy", dst, src, inc=[s["sV"].inc()])
            ev_done[g] = s["sV"].n
        if half == 1:
            og = nt // 2
            p.wait("sync", s["sV"].at())
            for j in range(8):
                p.y("dma_start", out=scratch_ap[og, j],
                    in_=slab[:, j * 1024:(j + 1) * 1024],
                    inc=[s["sDS"].inc(16)])
            p.wait("vector", s["sDS"].at())
    p.wait("tensor", s["sV"].at())
    p.wait("gpsimd", s["sDS"].at())


def build_ffn_l1():
    es = ExitStack()
    nc = bass.Bass("TRN2", target_bir_lowering=False)
    x_in = nc.dram_tensor("x_in", [TOK, D], F32, kind="ExternalInput").ap()
    x_out = nc.dram_tensor("x_out", [TOK, D], F32, kind="ExternalOutput").ap()
    qw = nc.dram_tensor("qw", [4096, 3072], I32, kind="ExternalInput").ap()
    s_arr = nc.dram_tensor("s_arr", [128, 32], F32, kind="ExternalInput").ap()
    b_arr = nc.dram_tensor("b_arr", [128, 32], F32, kind="ExternalInput").ap()
    z_arr = nc.dram_tensor("z_arr", [128, 32], F32, kind="ExternalInput").ap()
    id_in = nc.dram_tensor("id_in", [128, 128], F32, kind="ExternalInput").ap()
    scratch = nc.dram_tensor("wscr", [16, NJ, 128, KPP * OPP], BF16,
                             kind=("ExternalOutput" if DEBUG_FFN else "Internal")).ap()
    rowscr = nc.dram_tensor("rowscr", [40, 128], BF16, kind="Internal").ap()
    ctx = Ctx(nc, es)
    common_tensors(ctx)
    c, p, s = ctx, ctx.p, ctx.sem
    c.mksem("sK0", "sK1")
    emit_const_loads(ctx, [(c.s05, s_arr), (c.b05, b_arr),
                           (c.scrf[:, 0:32], z_arr)],
                     gp=[(c.ident, id_in)])
    p.wait("vector", s["sDX"].at(), s["sDW"].at())
    p.v("memset", c.eps, 1e-5)
    p.v("memset", c.zero128, 0.0)
    p.v("tensor_scalar", c.s05, c.s05, 0.5, None, ALU.mult)
    p.v("tensor_scalar", c.b05, c.b05, 0.5, None, ALU.mult)
    p.fence("vector", s["sV"])
    p.v("tensor_tensor", c.zs05, c.scrf[:, 0:32], c.s05, ALU.mult)
    p.fence("vector", s["sV"])
    p.v("tensor_scalar", c.zs05, c.zs05, -1.0, None, ALU.mult,
        inc=[s["sV"].inc()])
    emit_dequant(ctx, qw, scratch)
    emit_stream_load(ctx, x_in)
    p.wait("vector", s["sDX"].at())
    emit_ln(ctx, rowsum=True)
    zT_ps = psb(c)[0:32, 0:128]
    rT_ps = psb(c)[0:8, 1024:1152]
    zs_bf = c.scr.ap()[:, 256:288]
    rs_bf = c.scr.ap()[:, 288:296]
    p.v("tensor_copy", zs_bf, c.zs05)
    p.v("tensor_copy", rs_bf, c.rs8, inc=[s["sV"].inc()])
    p.wait("tensor", s["sV"].at())
    p.t("transpose", zT_ps, zs_bf, c.ident, inc=[s["sT"].inc()])
    p.t("transpose", rT_ps, rs_bf, c.ident)
    fl = pe_flush(c, c.psA.ap()[:, 1536:1664])
    fl[5].append(s["sT"].inc())
    p.wait("vector", s["sT"].at())
    p.v("tensor_copy", c.scr.ap()[0:32, 0:128], zT_ps)
    p.v("tensor_copy", c.scr.ap()[0:8, 128:256], rT_ps, inc=[s["sV"].inc()])
    p.wait("sync", s["sV"].at())
    p.y("dma_start", out=rowscr[0:32, :], in_=c.scr.ap()[0:32, 0:128],
        inc=[s["sDS"].inc(16)])
    p.y("dma_start", out=rowscr[32:40, :], in_=c.scr.ap()[0:8, 128:256],
        inc=[s["sDS"].inc(16)])
    p.wait("sync", s["sDS"].at())
    p.y("dma_start", out=c.scr.ap()[0:1, 2048:4096],
        in_=rowscr[0:16, :].rearrange("a b -> (a b)").unsqueeze(0),
        inc=[s["sDX"].inc(16)])
    p.y("dma_start", out=c.scr.ap()[32:33, 2048:4096],
        in_=rowscr[16:32, :].rearrange("a b -> (a b)").unsqueeze(0),
        inc=[s["sDX"].inc(16)])
    p.y("dma_start", out=c.scr.ap()[0:1, 0:1024],
        in_=rowscr[32:40, :].rearrange("a b -> (a b)").unsqueeze(0),
        inc=[s["sDX"].inc(16)])
    p.y("dma_start", out=c.scr.ap()[32:33, 0:1024],
        in_=rowscr[32:40, :].rearrange("a b -> (a b)").unsqueeze(0),
        inc=[s["sDX"].inc(16)])
    p.wait("tensor", s["sDX"].at())

    def zrow_fn(oc):
        part = 0 if oc < 16 else 32
        return c.scr.ap()[part:part + 1, 2048 + (oc % 16) * 128:
                          2048 + (oc % 16 + 1) * 128]

    def rsrow_fn(oc):
        part = 0 if oc < 16 else 32
        return c.scr.ap()[part:part + 1, 0:1024]

    def tail(og, acc, rdy):
        emit_ffn_tail(ctx, og, acc, rdy, c.s05, c.b05)
    if DEBUG_FFN:
        dbg_xnT = nc.dram_tensor("dbg_xnT", [128, KC * TOK], BF16,
                                 kind="ExternalOutput").ap()
        p.wait("sync", s["sV"].at())
        p.y("dma_start", out=dbg_xnT, in_=c.xnT[:, :], inc=[s["sDS"].inc(16)])
        p.wait("vector", s["sDS"].at())

    def tail(og, acc, rdy):
        emit_ffn_tail(ctx, og, acc, rdy, c.s05, c.b05, dbg=dbg_ev,
                      dbg2=dbg_ev2)

    emit_dense_fm(ctx, lambda og, j: scratch[og, j], D, tail,
                  rank1=(zrow_fn, rsrow_fn))
    emit_stream_store(ctx, x_out)
    with nc.Block() as block:
        p.replay(block)
    es.close()
    return nc



_NC_CACHE = {}


def _get_nc(name, builder):
    if name not in _NC_CACHE:
        _NC_CACHE[name] = builder()
    return _NC_CACHE[name]


def _run(name, builder, per_core):
    nc = _get_nc(name, builder)
    res = run_bass_kernel_spmd(nc, per_core, core_ids=list(range(8)))
    return res.results


def core_rows(c):
    g = c % 4
    idx = []
    for t in range(NT):
        base = (4 * t + g) * 128
        idx.extend(range(base, base + 128))
    return np.array(idx)


def maskwin_host(cg):
    m = np.full((128, 512), NEG, dtype=np.float32)
    for rr in range(128):
        m[rr, 0:cg * 128] = 0.0
        m[rr, cg * 128:cg * 128 + rr + 1] = 0.0
    return m


def _OLD_kernel(**inputs):
    inputs = {k: np.asarray(v) for k, v in inputs.items()}
    x = inputs["L0_attn_x"]
    rows = [core_rows(c) for c in range(8)]
    xs = [np.ascontiguousarray(x[c // 4][rows[c]]) for c in range(8)]
    ident = np.eye(128, dtype=np.float32)

    st_ffn0 = ffn_l0_inputs(inputs)
    st_qkv0 = {
        "wq": arrange_pieces(np.ascontiguousarray(inputs["L0_attn_w_q"].T)),
        "wk": arrange_pieces(np.ascontiguousarray(inputs["L0_attn_w_k"].T)),
        "wv": arrange_pieces(np.ascontiguousarray(inputs["L0_attn_w_v"].T)),
        "id_in": ident,
    }
    wo0 = arrange_pieces(np.ascontiguousarray(inputs["L0_attn_w_o"].T),
                         opp=512, kpp=2)
    st_qkv1 = {
        "wq": arrange_pieces(np.ascontiguousarray(inputs["L1_attn_Wq"])),
        "wk": arrange_pieces(np.ascontiguousarray(inputs["L1_attn_Wk"])),
        "wv": arrange_pieces(np.ascontiguousarray(inputs["L1_attn_Wv"])),
        "id_in": ident,
    }
    wo1 = arrange_pieces(np.ascontiguousarray(inputs["L1_attn_Wo"]),
                         opp=512, kpp=2)
    st_ffn1 = {
        "qw": np.ascontiguousarray(inputs["L1_ffn_qweight"]),
        "s_arr": cols128(np.ascontiguousarray(inputs["L1_ffn_weight_scale"][:, 0])),
        "b_arr": cols128(np.ascontiguousarray(inputs["L1_ffn_bias"])),
        "z_arr": cols128(np.ascontiguousarray(inputs["L1_ffn_weight_zero_point"][:, 0])),
        "id_in": ident,
    }
    cosf, sinf = rope_tables()
    cos_c = [np.ascontiguousarray(cosf[:, rows[c]]) for c in range(8)]
    sin_c = [np.ascontiguousarray(sinf[:, rows[c]]) for c in range(8)]

    def gather_kv(shards):
        full = [np.ascontiguousarray(np.concatenate(shards[g * 4:(g + 1) * 4], axis=0))
                for g in range(2)]
        return [full[c // 4] for c in range(8)]

    r = _run("ffn0", build_ffn_l0, [dict(st_ffn0, x_in=xs[i]) for i in range(8)])
    x1 = [r[i]["x_out"] for i in range(8)]
    r = _run("qkv0", lambda: build_qkv(0),
             [dict(st_qkv0, x_in=x1[i]) for i in range(8)])
    q0 = [r[i]["qT"] for i in range(8)]
    k0 = gather_kv([r[i]["kT"] for i in range(8)])
    v0 = gather_kv([r[i]["vT"] for i in range(8)])
    r = _run("attnB0", lambda: build_attnB(0),
             [dict(x_in=x1[i], qT=q0[i], kT=k0[i], vT=v0[i], wo=wo0,
                   id_in=ident, mw_in=np.zeros((128, 512), dtype=np.float32))
              for i in range(8)])
    x2 = [r[i]["x_out"] for i in range(8)]
    r = _run("ffn0", build_ffn_l0, [dict(st_ffn0, x_in=x2[i]) for i in range(8)])
    x3 = [r[i]["x_out"] for i in range(8)]
    r = _run("ffn1", build_ffn_l1, [dict(st_ffn1, x_in=x3[i]) for i in range(8)])
    x4 = [r[i]["x_out"] for i in range(8)]
    r = _run("qkv1", lambda: build_qkv(1),
             [dict(st_qkv1, x_in=x4[i], cos_in=cos_c[i], sin_in=sin_c[i])
              for i in range(8)])
    q1 = [r[i]["qT"] for i in range(8)]
    k1 = gather_kv([r[i]["kT"] for i in range(8)])
    v1 = gather_kv([r[i]["vT"] for i in range(8)])
    mws = [maskwin_host(c % 4) for c in range(8)]
    r = _run("attnB1", lambda: build_attnB(1),
             [dict(x_in=x4[i], qT=q1[i], kT=k1[i], vT=v1[i], wo=wo1,
                   id_in=ident, mw_in=mws[i]) for i in range(8)])
    x5 = [r[i]["x_out"] for i in range(8)]
    r = _run("ffn1", build_ffn_l1, [dict(st_ffn1, x_in=x5[i]) for i in range(8)])
    xo = [r[i]["x_out"] for i in range(8)]

    out = np.empty((2, 4096, 4096), dtype=np.float32)
    for c in range(8):
        out[c // 4][rows[c]] = xo[c]
    return out


def emit_barrier(ctx):
    p, s = ctx.p, ctx.sem
    tgt = [s["sT"].at(), s["sV"].at(), s["sS"].at()]
    for eng in ENGINES:
        p.wait(eng, *tgt)


def emit_zrow_staging(ctx, rowscr):
    c, p, s = ctx, ctx.p, ctx.sem
    zT_ps = psb(c)[0:32, 0:128]
    rT_ps = psb(c)[0:8, 1024:1152]
    zs_bf = c.scr.ap()[:, 256:288]
    rs_bf = c.scr.ap()[:, 288:296]
    p.wait("vector", s["sT"].at())
    p.v("tensor_copy", zs_bf, c.zs05)
    p.v("tensor_copy", rs_bf, c.rs8, inc=[s["sV"].inc()])
    p.wait("tensor", s["sV"].at())
    p.t("transpose", zT_ps, zs_bf, c.ident, inc=[s["sT"].inc()])
    p.t("transpose", rT_ps, rs_bf, c.ident)
    fl = pe_flush(c, c.psA.ap()[:, 1536:1664])
    fl[5].append(s["sT"].inc())
    p.wait("vector", s["sT"].at())
    p.v("tensor_copy", c.scr.ap()[0:32, 0:128], zT_ps)
    p.v("tensor_copy", c.scr.ap()[0:8, 128:256], rT_ps, inc=[s["sV"].inc()])
    p.wait("sync", s["sV"].at())
    p.y("dma_start", out=rowscr[0:32, :], in_=c.scr.ap()[0:32, 0:128],
        inc=[s["sDS"].inc(16)])
    p.y("dma_start", out=rowscr[32:40, :], in_=c.scr.ap()[0:8, 128:256],
        inc=[s["sDS"].inc(16)])
    p.wait("sync", s["sDS"].at())
    p.y("dma_start", out=c.scr.ap()[0:1, 2048:4096],
        in_=rowscr[0:16, :].rearrange("a b -> (a b)").unsqueeze(0),
        inc=[s["sDX"].inc(16)])
    p.y("dma_start", out=c.scr.ap()[32:33, 2048:4096],
        in_=rowscr[16:32, :].rearrange("a b -> (a b)").unsqueeze(0),
        inc=[s["sDX"].inc(16)])
    p.y("dma_start", out=c.scr.ap()[0:1, 0:1024],
        in_=rowscr[32:40, :].rearrange("a b -> (a b)").unsqueeze(0),
        inc=[s["sDX"].inc(16)])
    p.y("dma_start", out=c.scr.ap()[32:33, 0:1024],
        in_=rowscr[32:40, :].rearrange("a b -> (a b)").unsqueeze(0),
        inc=[s["sDX"].inc(16)])
    p.wait("tensor", s["sDX"].at())

    def zrow_fn(oc):
        part = 0 if oc < 16 else 32
        return c.scr.ap()[part:part + 1, 2048 + (oc % 16) * 128:
                          2048 + (oc % 16 + 1) * 128]

    def rsrow_fn(oc):
        part = 0 if oc < 16 else 32
        return c.scr.ap()[part:part + 1, 0:1024]
    return zrow_fn, rsrow_fn


def build_fused():
    es = ExitStack()
    nc = bass.Bass("TRN2", target_bir_lowering=False)
    dt_ = nc.dram_tensor
    x_in = dt_("x_in", [TOK, D], F32, kind="ExternalInput").ap()
    x_out = dt_("x_out", [TOK, D], F32, kind="ExternalOutput").ap()
    wb = dt_("wbase", [16, NJ, 128, KPP * OPP], F32, kind="ExternalInput").ap()
    b_arr0 = dt_("b_arr0", [128, 32], F32, kind="ExternalInput").ap()
    wr = dt_("wr", [128, 256], F32, kind="ExternalInput").ap()
    A_in = dt_("A_in", [4, 128, 1024], F32, kind="ExternalInput").ap()
    B_in = dt_("B_in", [16, 128, 256], F32, kind="ExternalInput").ap()
    E_in = dt_("E_in", [8, 128], F32, kind="ExternalInput").ap()
    id_in = dt_("id_in", [128, 128], F32, kind="ExternalInput").ap()
    wq0 = dt_("wq0", [16, NJ, 128, KPP * OPP], F32, kind="ExternalInput").ap()
    wk0 = dt_("wk0", [16, NJ, 128, KPP * OPP], F32, kind="ExternalInput").ap()
    wv0 = dt_("wv0", [16, NJ, 128, KPP * OPP], F32, kind="ExternalInput").ap()
    wo0 = dt_("wo0", [8, 16, 128, 1024], F32, kind="ExternalInput").ap()
    wq1 = dt_("wq1", [16, NJ, 128, KPP * OPP], F32, kind="ExternalInput").ap()
    wk1 = dt_("wk1", [4, NJ, 128, KPP * OPP], F32, kind="ExternalInput").ap()
    wv1 = dt_("wv1", [4, NJ, 128, KPP * OPP], F32, kind="ExternalInput").ap()
    wo1 = dt_("wo1", [8, 16, 128, 1024], F32, kind="ExternalInput").ap()
    qw = dt_("qw", [4096, 3072], I32, kind="ExternalInput").ap()
    s_arr = dt_("s_arr", [128, 32], F32, kind="ExternalInput").ap()
    b_arr1 = dt_("b_arr1", [128, 32], F32, kind="ExternalInput").ap()
    z_arr = dt_("z_arr", [128, 32], F32, kind="ExternalInput").ap()
    cos_in = dt_("cos_in", [128, TOK], F32, kind="ExternalInput").ap()
    sin_in = dt_("sin_in", [128, TOK], F32, kind="ExternalInput").ap()
    mw_in = dt_("mw_in", [128, 512], F32, kind="ExternalInput").ap()
    wscr = dt_("wscr", [16, NJ, 128, KPP * OPP], BF16, kind="Internal").ap()
    rowscr = dt_("rowscr", [40, 128], BF16, kind="Internal").ap()
    qb = dt_("qb", [4096, TOK], BF16, kind="Internal").ap()
    aout = dt_("aout", [4096, TOK], BF16, kind="Internal").ap()
    k0i = dt_("k0i", [4096, TOK], BF16, kind="Internal").ap()
    v0i = dt_("v0i", [4096, TOK], BF16, kind="Internal").ap()
    k0o = dt_("k0o", [16384, TOK], BF16, kind="Internal").ap()
    v0o = dt_("v0o", [16384, TOK], BF16, kind="Internal").ap()
    k1i = dt_("k1i", [1024, TOK], BF16, kind="Internal").ap()
    v1i = dt_("v1i", [1024, TOK], BF16, kind="Internal").ap()
    k1o = dt_("k1o", [4096, TOK], BF16, kind="Internal").ap()
    v1o = dt_("v1o", [4096, TOK], BF16, kind="Internal").ap()

    ctx = Ctx(nc, es)
    common_tensors(ctx)
    c, p, s = ctx, ctx.p, ctx.sem
    c.mksem("sK0", "sK1", "sU0", "sQ0", "sQ1", "sO0", "sO1")
    c.b05L1 = c.cst.ap()[:, 80:112]
    rg = [[0, 1, 2, 3], [4, 5, 6, 7]]

    emit_const_loads(ctx, [(c.b05, b_arr0), (c.s05, s_arr),
                           (c.b05L1, b_arr1), (c.scrf[:, 0:32], z_arr)],
                     gp=[(c.ident, id_in), (c.sb_E, E_in)])
    emit_stream_load(ctx, x_in)
    p.wait("vector", s["sDX"].at(), s["sDW"].at())
    p.v("memset", c.eps, 1e-5)
    p.v("memset", c.zero128, 0.0)
    p.v("tensor_scalar", c.b05, c.b05, 0.5, None, ALU.mult)
    p.v("tensor_scalar", c.s05, c.s05, 0.5, None, ALU.mult)
    p.v("tensor_scalar", c.b05L1, c.b05L1, 0.5, None, ALU.mult)
    p.fence("vector", s["sV"])
    p.v("tensor_tensor", c.zs05, c.scrf[:, 0:32], c.s05, ALU.mult)
    p.fence("vector", s["sV"])
    p.v("tensor_scalar", c.zs05, c.zs05, -1.0, None, ALU.mult,
        inc=[s["sV"].inc()])
    p.wait("tensor", s["sV"].at())
    emit_dequant(ctx, qw, wscr)
    emit_barrier(ctx)

    def ffn0():
        emit_ln(ctx)
        pre_og, extra = emit_router_lora(ctx, wr, A_in, B_in)
        done_og = set()

        def pieces_ap(og, j):
            if og not in done_og:
                done_og.add(og)
                pre_og(og)
            return wb[og, j]

        def tail(og, acc, rdy):
            emit_ffn_tail(ctx, og, acc, rdy, 0.5, c.b05)
        emit_dense_fm(ctx, pieces_ap, D, tail, extra_mm=extra)
        emit_barrier(ctx)

    def ffn1():
        emit_ln(ctx, rowsum=True)
        zrow_fn, rsrow_fn = emit_zrow_staging(ctx, rowscr)

        def tail(og, acc, rdy):
            emit_ffn_tail(ctx, og, acc, rdy, c.s05, c.b05L1, dual_ev=False)
        emit_dense_fm(ctx, lambda og, j: wscr[og, j], D, tail,
                      rank1=(zrow_fn, rsrow_fn))
        emit_barrier(ctx)

    def attn(layer):
        nkv = 32 if layer == 0 else 8
        causal = (layer == 1)
        wqx = wq0 if layer == 0 else wq1
        wkx = wk0 if layer == 0 else wk1
        wvx = wv0 if layer == 0 else wv1
        wox = wo0 if layer == 0 else wo1
        ki = k0i if layer == 0 else k1i
        ko = k0o if layer == 0 else k1o
        vi = v0i if layer == 0 else v1i
        vo = v0o if layer == 0 else v1o
        emit_ln(ctx)
        if layer == 1:
            p.wait("gpsimd", s["sT"].at(), s["sV"].at())
            p.g("dma_start", out=c.scr.ap()[:, 0:1024], in_=cos_in,
                inc=[s["sDW"].inc(16)])
            p.g("dma_start", out=c.scr.ap()[:, 3072:4096], in_=sin_in,
                inc=[s["sDW"].inc(16)])
            p.wait("vector", s["sDW"].at())
        rope = (layer == 1)
        emit_dense_fm(ctx, lambda og, j: wkx[og, j], nkv * 128,
                      lambda og, acc, rdy: emit_qkv_tail(ctx, og, acc, rdy,
                                                         ki, rope=rope))
        p.wait("gpsimd", s["sB0"].at(), s["sB1"].at())
        p.g("collective_compute", "AllGather", ALU.bypass,
            replica_groups=rg, ins=[ki], outs=[ko], inc=[s["sG"].inc()])
        kg_done = s["sG"].at()
        emit_dense_fm(ctx, lambda og, j: wvx[og, j], nkv * 128,
                      lambda og, acc, rdy: emit_qkv_tail(ctx, og, acc, rdy,
                                                         vi, rope=False))
        p.wait("gpsimd", s["sB0"].at(), s["sB1"].at())
        p.g("collective_compute", "AllGather", ALU.bypass,
            replica_groups=rg, ins=[vi], outs=[vo], inc=[s["sG"].inc()])
        vg_done = s["sG"].at()
        emit_dense_fm(ctx, lambda og, j: wqx[og, j], D,
                      lambda og, acc, rdy: emit_qkv_tail(ctx, og, acc, rdy,
                                                         qb, rope=rope))
        p.wait("gpsimd", (s["sT"].h, s["sT"].n))
        if causal:
            p.g("dma_start", out=c.big.ap()[:, 28928:29440], in_=mw_in,
                inc=[s["sDW"].inc(16)])
        p.wait("gpsimd", kg_done, vg_done, s["sB0"].at(), s["sB1"].at())
        p.wait("vector", s["sDW"].at(), (s["sT"].h, s["sT"].n))
        p.v("memset", c.big.ap()[0:1, 30208:30336], 1.0, inc=[s["sV"].inc()])
        p.wait("tensor", s["sV"].at())
        emit_attn_core(ctx, qb, ko, vo, aout, 32, nkv, causal)
        emit_barrier(ctx)
        p.wait("sync", s["sO0"].at(), s["sO1"].at(), s["sV"].at())
        for g4 in range(8):
            sr = aout[g4 * 512:(g4 + 1) * 512, :].rearrange(
                "(e p) t -> p e t", p=128)
            dsx = c.big.ap().rearrange("p (e t) -> p e t", e=KC)[
                :, g4 * 4:(g4 + 1) * 4, :]
            p.y("dma_start", out=dsx, in_=sr, inc=[s["sDX"].inc(16)])
        p.wait("tensor", s["sDX"].at())
        emit_dense_tm(ctx, lambda og, j: wox[og, j], c.big.ap(), D)
        emit_barrier(ctx)

    ffn0()
    attn(0)
    ffn0()
    ffn1()
    attn(1)
    ffn1()
    emit_stream_store(ctx, x_out)
    with nc.Block() as block:
        p.replay(block)
    es.close()
    return nc


def fused_inputs(inputs):
    ident = np.eye(128, dtype=np.float32)
    st = ffn_l0_inputs(inputs)
    st = {"wbase": st["wbase"], "b_arr0": st["b_arr"], "wr": st["wr"],
          "A_in": st["A_in"], "B_in": st["B_in"], "E_in": st["E_in"],
          "id_in": ident}
    st["wq0"] = arrange_pieces(np.ascontiguousarray(inputs["L0_attn_w_q"].T))
    st["wk0"] = arrange_pieces(np.ascontiguousarray(inputs["L0_attn_w_k"].T))
    st["wv0"] = arrange_pieces(np.ascontiguousarray(inputs["L0_attn_w_v"].T))
    st["wo0"] = arrange_pieces(np.ascontiguousarray(inputs["L0_attn_w_o"].T),
                               opp=512, kpp=2)
    st["wq1"] = arrange_pieces(np.ascontiguousarray(inputs["L1_attn_Wq"]))
    st["wk1"] = arrange_pieces(np.ascontiguousarray(inputs["L1_attn_Wk"]))
    st["wv1"] = arrange_pieces(np.ascontiguousarray(inputs["L1_attn_Wv"]))
    st["wo1"] = arrange_pieces(np.ascontiguousarray(inputs["L1_attn_Wo"]),
                               opp=512, kpp=2)
    st["qw"] = np.ascontiguousarray(inputs["L1_ffn_qweight"])
    st["s_arr"] = cols128(np.ascontiguousarray(inputs["L1_ffn_weight_scale"][:, 0]))
    st["b_arr1"] = cols128(np.ascontiguousarray(inputs["L1_ffn_bias"]))
    st["z_arr"] = cols128(np.ascontiguousarray(inputs["L1_ffn_weight_zero_point"][:, 0]))
    return st


def kernel(**inputs):
    inputs = {k: np.asarray(v) for k, v in inputs.items()}
    x = inputs["L0_attn_x"]
    rows = [core_rows(c) for c in range(8)]
    xs = [np.ascontiguousarray(x[c // 4][rows[c]]) for c in range(8)]
    st = fused_inputs(inputs)
    cosf, sinf = rope_tables()
    nc = _NC_CACHE.get("fused")
    if nc is None:
        nc = build_fused()
        _NC_CACHE["fused"] = nc
    in_maps = [dict(st, x_in=xs[i],
                    cos_in=np.ascontiguousarray(cosf[:, rows[i]]),
                    sin_in=np.ascontiguousarray(sinf[:, rows[i]]),
                    mw_in=maskwin_host(i % 4)) for i in range(8)]
    res = run_bass_kernel_spmd(nc, in_maps, core_ids=list(range(8)))
    out = np.empty((2, 4096, 4096), dtype=np.float32)
    for cc in range(8):
        out[cc // 4][rows[cc]] = res.results[cc]["x_out"]
    return out




# version 12
# speedup vs baseline: 1.0112x; 1.0042x over previous
import numpy as np
import ml_dtypes
from contextlib import ExitStack

import concourse.bass as bass
import concourse.mybir as mybir
from concourse.bass_utils import run_bass_kernel_spmd

F32 = mybir.dt.float32
BF16 = mybir.dt.bfloat16
I32 = mybir.dt.int32
AF = mybir.ActivationFunctionType
ALU = mybir.AluOpType
AX = mybir.AxisListType.X
BF = ml_dtypes.bfloat16

D = 4096
TOK = 1024
NT = 8
KC = 32
SCL = float(1.0 / np.sqrt(128.0))
NEG = -1.0e30
KPP = 4
OPP = 256
NJ = KC // KPP

ENGINES = ("sync", "scalar", "vector", "tensor", "gpsimd")


class Sem:
    def __init__(self, h):
        self.h = h
        self.n = 0

    def inc(self, d=1):
        self.n += d
        return (self.h, d)

    def at(self):
        return (self.h, self.n)


class Prog:
    def __init__(self):
        self.ops = []
        self.last = {e: None for e in ENGINES}

    def op(self, eng, method, *args, wait=(), inc=(), **kwargs):
        rec = [eng, method, args, kwargs, list(wait), list(inc)]
        self.ops.append(rec)
        if method is not None:
            self.last[eng] = rec
        return rec

    def wait(self, eng, *wait):
        self.ops.append([eng, None, (), {}, list(wait), []])

    def fence(self, eng, sem):
        rec = self.last[eng]
        assert rec is not None
        rec[5].append(sem.inc())
        self.wait(eng, sem.at())

    def t(self, method, *a, **k):
        return self.op("tensor", method, *a, **k)

    def v(self, method, *a, **k):
        return self.op("vector", method, *a, **k)

    def s(self, method, *a, **k):
        return self.op("scalar", method, *a, **k)

    def g(self, method, *a, **k):
        return self.op("gpsimd", method, *a, **k)

    def y(self, method, *a, **k):
        return self.op("sync", method, *a, **k)

    def replay(self, block):
        def run(engobj, name):
            for eng, method, args, kwargs, waits, incs in self.ops:
                if eng != name:
                    continue
                for (h, val) in waits:
                    engobj.wait_ge(h, val)
                inst = None
                if method is not None:
                    inst = getattr(engobj, method)(*args, **kwargs)
                if incs:
                    if inst is None:
                        inst = engobj.nop()
                    inst.then_inc(*incs[0])
                    for (h, d) in incs[1:]:
                        engobj.nop().then_inc(h, d)

        block.sync(lambda e: run(e, "sync"))
        block.scalar(lambda e: run(e, "scalar"))
        block.vector(lambda e: run(e, "vector"))
        block.tensor(lambda e: run(e, "tensor"))
        block.gpsimd(lambda e: run(e, "gpsimd"))


class Ctx:
    def __init__(self, nc, es):
        self.nc = nc
        self.es = es
        self.p = Prog()
        self.sem = {}

    def sb(self, name, shape, dtype):
        return self.es.enter_context(self.nc.sbuf_tensor(name, shape, dtype))

    def psum(self, name, shape, dtype):
        return self.es.enter_context(self.nc.psum_tensor(name, shape, dtype))

    def mksem(self, *names):
        for n in names:
            h = self.es.enter_context(self.nc.semaphore(n))
            self.sem[n] = Sem(h)




def arrange_pieces(Wt, opp=OPP, kpp=KPP):
    K, O = Wt.shape
    a = Wt.reshape(K // (128 * kpp), kpp, 128, O // opp, opp)
    a = a.transpose(3, 0, 2, 1, 4)
    return np.ascontiguousarray(
        a.reshape(O // opp, K // (128 * kpp), 128, kpp * opp))


def cols128(v):
    n = v.shape[0] // 128
    return np.ascontiguousarray(v.reshape(n, 128).T)


def rope_tables(S=4096, hd=128, theta=10000.0):
    inv = 1.0 / theta ** (np.arange(0, hd, 2, dtype=np.float64) / hd)
    t = np.arange(S, dtype=np.float64)
    freqs = np.outer(t, inv)
    emb = np.concatenate([freqs, freqs], axis=1)
    cos = np.ascontiguousarray(np.cos(emb).astype(np.float32).T)
    sin = np.ascontiguousarray(np.sin(emb).astype(np.float32).T)
    return cos, sin


def tri_mask():
    m = np.where(np.tril(np.ones((128, 128), dtype=bool)), 0.0, NEG)
    return np.ascontiguousarray(m.astype(np.float32))


def e_matrix():
    E = np.zeros((8, 128), dtype=np.float32)
    for e in range(8):
        E[e, 16 * e:16 * (e + 1)] = 2.0
    return np.ascontiguousarray(E)




def common_tensors(ctx, stream=True, big=True):
    c = ctx
    if stream:
        c.sx = c.sb("sx", [128, NT * D], F32)
    if big:
        c.big = c.sb("big", [128, KC * TOK], BF16)
        c.xnT = c.big
    c.scr = c.sb("scr", [128, 4096], BF16)
    c.wbuf = c.sb("wbuf", [128, 2048], BF16)
    c.cst = c.sb("cst", [128, 512], F32)
    c.psA = c.psum("psA", [128, 4096], F32)
    c.mksem("sDW", "sDX", "sDS", "sT", "sV", "sS", "sG",
            "sW0", "sW1", "sB0", "sB1")
    c.stat = c.cst.ap()[:, 0:16]
    c.b05 = c.cst.ap()[:, 16:48]
    c.s05 = c.cst.ap()[:, 48:80]
    c.identf = c.cst.ap()[:, 80:208]
    c.l8 = c.cst.ap()[:, 208:216]
    c.r8 = c.cst.ap()[:, 216:232]
    c.msk = c.cst.ap()[:, 232:240]
    c.trimask = c.cst.ap()[:, 240:368]
    c.eps = c.cst.ap()[:, 368:369]
    cb = c.cst.ap().bitcast(BF16)
    c.zero128 = cb[:, 480:608]
    c.zs05 = c.cst.ap()[:, 304:336]
    c.rs8 = c.cst.ap()[:, 336:344]
    c.cb8 = cb[:, 744:752]
    c.sb_E = cb[0:8, 752:880]
    c.ident = cb[:, 880:1008]
    c.scrf = c.scr.ap().bitcast(F32)


def psb(ctx):
    return ctx.psA.ap().bitcast(BF16)


def pe_flush(ctx, region):
    return ctx.p.t("matmul", region, ctx.ident, ctx.zero128,
                   start=False, stop=True)


def emit_const_loads(ctx, consts, gp=()):
    p, s = ctx.p, ctx.sem
    for dst, src in consts:
        p.y("dma_start", out=dst, in_=src, inc=[s["sDX"].inc(16)])
    for dst, src in gp:
        p.g("dma_start", out=dst, in_=src, inc=[s["sDW"].inc(16)])


def emit_stream_load(ctx, x_ap):
    p, s = ctx.p, ctx.sem
    for t in range(NT):
        p.y("dma_start", out=ctx.sx[:, t * D:(t + 1) * D],
            in_=x_ap[t * 128:(t + 1) * 128, :], inc=[s["sDX"].inc(16)])


def emit_stream_store(ctx, out_ap):
    p, s = ctx.p, ctx.sem
    p.wait("sync", s["sV"].at(), s["sS"].at())
    for t in range(NT):
        p.y("dma_start", out=out_ap[t * 128:(t + 1) * 128, :],
            in_=ctx.sx[:, t * D:(t + 1) * D], inc=[s["sDS"].inc(16)])
    p.wait("sync", s["sDS"].at())


def emit_ln(ctx, rowsum=False):
    c, p, s = ctx, ctx.p, ctx.sem
    st = c.stat
    pb = psb(c)
    xn = c.scr.ap()[:, 0:4096]
    evac_done = {}
    for t in range(NT):
        xt = c.sx[:, t * D:(t + 1) * D]
        if t > 0:
            p.wait("scalar", s["sT"].at())
        else:
            p.wait("scalar", s["sV"].at())
        p.s("activation", xn, xt, AF.Square, inc=[s["sS"].inc()])
        p.v("reduce_sum", st[:, 0:1], xt, axis=AX)
        p.wait("vector", s["sS"].at())
        p.v("reduce_sum", st[:, 1:2], xn, axis=AX)
        p.fence("vector", s["sV"])
        p.v("tensor_scalar", st[:, 2:3], st[:, 0:1], 1.0 / D, None, ALU.mult)
        p.fence("vector", s["sV"])
        p.v("tensor_tensor", st[:, 3:4], st[:, 2:3], st[:, 2:3], ALU.mult)
        p.fence("vector", s["sV"])
        p.v("tensor_scalar", st[:, 4:5], st[:, 1:2], 1.0 / D, st[:, 3:4],
            ALU.mult, ALU.subtract, inc=[s["sV"].inc()])
        p.wait("scalar", s["sV"].at())
        p.s("activation", st[:, 5:6], st[:, 4:5], AF.Ln, bias=c.eps, scale=1.0)
        p.fence("scalar", s["sS"])
        p.s("activation", st[:, 6:7], st[:, 5:6], AF.Exp, scale=-0.5,
            inc=[s["sS"].inc()])
        p.wait("vector", s["sS"].at())
        p.v("tensor_scalar", st[:, 7:8], st[:, 2:3], st[:, 6:7], -1.0,
            ALU.mult, ALU.mult)
        p.fence("vector", s["sV"])
        p.v("tensor_scalar", xn, xt, st[:, 6:7], st[:, 7:8],
            ALU.mult, ALU.add, inc=[s["sV"].inc()])
        if rowsum:
            p.fence("vector", s["sV"])
            p.v("reduce_sum", c.rs8[:, t:t + 1], xn, axis=AX,
                inc=[s["sV"].inc()])
        p.wait("tensor", s["sV"].at())
        for g in range(4):
            bank = 6 + (g % 2)
            bview = pb[:, bank * 1024:(bank + 1) * 1024]
            key = (t * 4 + g) - 2
            if key >= 0:
                p.wait("tensor", (s["sV"].h, evac_done[key]))
            for kk in range(8):
                kc = g * 8 + kk
                p.t("transpose", bview[:, kk * 128:(kk + 1) * 128],
                    xn[:, kc * 128:(kc + 1) * 128], c.ident,
                    inc=([s["sT"].inc()] if kk == 7 else []))
            p.wait("vector", s["sT"].at())
            dst = c.xnT[:].rearrange("p (k c) -> p k c", k=KC)[
                :, g * 8:(g + 1) * 8, t * 128:(t + 1) * 128]
            src = bview.rearrange("p (k c) -> p k c", k=8)
            p.v("tensor_copy", dst, src, inc=[s["sV"].inc()])
            evac_done[t * 4 + g] = s["sV"].n
    p.wait("tensor", s["sV"].at())


def emit_dense_fm(ctx, pieces_ap, n_out, tail, extra_mm=None, rank1=None):
    c, p, s = ctx, ctx.p, ctx.sem
    n_og = n_out // OPP
    acc = c.psA.ap()[:, 0:2048]
    wslot = [c.wbuf.ap()[:, 0:KPP * OPP],
             c.wbuf.ap()[:, 1024:1024 + KPP * OPP]]
    slot_free = [s["sT"].n, s["sT"].n]
    dma_done = {}

    wsem = [s["sW0"], s["sW1"]]

    def issue_piece(idx):
        og_, j_ = idx // NJ, idx % NJ
        sl_ = 0 if SERIAL_W else idx % 2
        p.wait("gpsimd", (s["sT"].h, slot_free[sl_]))
        p.g("dma_start", out=wslot[sl_], in_=pieces_ap(og_, j_),
            inc=[wsem[sl_].inc(16)])
        dma_done[idx] = wsem[sl_].at()

    if not SERIAL_W:
        issue_piece(0)
        issue_piece(1)
    for og in range(n_og):
        for kc in range(KC):
            j, ks = kc // KPP, kc % KPP
            idx = og * NJ + j
            sl = 0 if SERIAL_W else idx % 2
            if ks == 0:
                if SERIAL_W:
                    issue_piece(idx)
                p.wait("tensor", dma_done[idx])
            if kc == TRAIL_AT and getattr(c, "tail_trail", None) is not None:
                c.tail_trail()
                c.tail_trail = None
            for oc2 in range(2):
                lhsT = wslot[sl][:, ks * OPP + oc2 * 128: ks * OPP + (oc2 + 1) * 128]
                a = acc[:, oc2 * 1024:(oc2 + 1) * 1024]
                last = (kc == KC - 1) and (extra_mm is None)
                p.t("matmul", a[:, 0:512], lhsT,
                    c.xnT[:, kc * TOK: kc * TOK + 512],
                    start=(kc == 0), stop=False)
                rec = p.t("matmul", a[:, 512:1024], lhsT,
                          c.xnT[:, kc * TOK + 512: (kc + 1) * TOK],
                          start=(kc == 0), stop=(last and oc2 == 1))
                if ks == KPP - 1 and oc2 == 1:
                    rec[5].append(s["sT"].inc())
                    slot_free[sl] = s["sT"].n
                    if not SERIAL_W and idx + 2 < n_og * NJ:
                        issue_piece(idx + 2)
        if extra_mm is not None:
            extra_mm(og)
        if rank1 is not None:
            urow_fn, vrow_fn = rank1
            for oc2 in range(2):
                oc = og * 2 + oc2
                a = acc[:, oc2 * 1024:(oc2 + 1) * 1024]
                vr = vrow_fn(oc)
                p.t("matmul", a[:, 0:512], urow_fn(oc),
                    vr[:, 0:512], start=False, stop=False)
                p.t("matmul", a[:, 512:1024], urow_fn(oc),
                    vr[:, 512:1024], start=False, stop=True)
        fl = pe_flush(c, acc[:, 0:128])
        fl[5].append(s["sT"].inc())
        tail(og, acc, s["sT"].at())
        if getattr(c, "tail_evac_done", None) is not None:
            p.wait("tensor", *c.tail_evac_done)
            c.tail_evac_done = None
    if getattr(c, "tail_trail", None) is not None:
        c.tail_trail()
        c.tail_trail = None


def emit_ffn_tail(ctx, og, acc, acc_ready, s05, b05, dual_ev=True):
    c, p, s = ctx, ctx.p, ctx.sem
    pb = psb(c)
    ev0 = c.scr.ap()[:, 1024:2048]
    ev1 = c.scr.ap()[:, 3072:4096] if dual_ev else ev0
    evs = [ev0, ev1]
    if not hasattr(c, "tail_add_done"):
        c.tail_add_done = {}
        c.tail_n = 0
        c.tail_ev_free = {}
    ev_done = []

    def evac(oc2):
        oc = og * 2 + oc2
        eng = "scalar" if (dual_ev and oc2 == 0) else "vector"
        p.wait(eng, acc_ready)
        key = (dual_ev, oc2)
        if c.tail_ev_free.get(key) is not None:
            p.wait(eng, (s["sT"].h, c.tail_ev_free[key]))
        sc = s05 if isinstance(s05, float) else s05[:, oc:oc + 1]
        if eng == "scalar":
            p.s("activation", evs[oc2], acc[:, oc2 * 1024:(oc2 + 1) * 1024],
                AF.Identity, bias=b05[:, oc:oc + 1], scale=sc,
                inc=[s["sS"].inc()])
            ev_done.append(("s", s["sS"].n))
        else:
            p.v("tensor_scalar", evs[oc2], acc[:, oc2 * 1024:(oc2 + 1) * 1024],
                sc, b05[:, oc:oc + 1], ALU.mult, ALU.add, inc=[s["sV"].inc()])
            ev_done.append(("v", s["sV"].n))

    def trail(oc2):
        oc = og * 2 + oc2
        ev = evs[oc2]
        key = (dual_ev, oc2)
        eng, val = ev_done[oc2]
        p.wait("tensor", (s["sS" if eng == "s" else "sV"].h, val))
        for t in range(NT):
            bank = 4 + (c.tail_n % 2)
            bv = pb[:, bank * 1024: bank * 1024 + 128]
            k2 = c.tail_n - 2
            if k2 >= 0:
                p.wait("tensor", (s["sV"].h, c.tail_add_done[k2]))
            p.t("transpose", bv, ev[:, t * 128:(t + 1) * 128], c.ident,
                inc=[s["sT"].inc()])
            if t == NT - 1:
                c.tail_ev_free[key] = s["sT"].n
                if not dual_ev:
                    c.tail_ev_free[(dual_ev, 1 - oc2)] = s["sT"].n
            p.wait("vector", s["sT"].at())
            dst = c.sx[:, t * D + oc * 128: t * D + (oc + 1) * 128]
            p.v("scalar_tensor_tensor", out=dst, in0=bv, scalar=1.0,
                in1=dst, op0=ALU.mult, op1=ALU.add, inc=[s["sV"].inc()])
            c.tail_add_done[c.tail_n] = s["sV"].n
            c.tail_n += 1

    if dual_ev:
        evac(0)
        evac(1)
        c.tail_evac_done = [s["sV"].at(), s["sS"].at()]

        def trail_all():
            trail(0)
            trail(1)
        if DEFER_TRAIL:
            assert getattr(c, "tail_trail", None) is None, "trail dropped"
            c.tail_trail = trail_all
        else:
            trail_all()
    else:
        evac(0)
        trail(0)
        evac(1)
        c.tail_evac_done = [s["sV"].at()]
        if DEFER_TRAIL:
            assert getattr(c, "tail_trail", None) is None, "trail dropped"
            c.tail_trail = lambda: trail(1)
        else:
            trail(1)


def emit_router_lora(ctx, wr_ap, A_ap, B_ap):
    c, p, s = ctx, ctx.p, ctx.sem
    hs = c.scr.ap()[:, 0:1024]
    lg = c.scr.ap()[0:8, 2048:3072]
    cT = c.scr.ap()[0:8, 3072:4096]
    hf = c.scrf[:, 512:1024]
    l8, r8, msk = c.l8, c.r8, c.msk
    wrb = c.wbuf.ap()[:, 0:256]
    p.g("dma_start", out=wrb, in_=wr_ap, inc=[s["sW0"].inc(16)])
    wr_done = s["sW0"].at()

    lg_ps = c.psA.ap()[0:8, 0:1024]
    p.wait("tensor", wr_done)
    for kc in range(KC):
        lhsT = wrb[:, kc * 8:(kc + 1) * 8]
        p.t("matmul", lg_ps[:, 0:512], lhsT, c.xnT[:, kc * TOK:kc * TOK + 512],
            start=(kc == 0), stop=False)
        p.t("matmul", lg_ps[:, 512:1024], lhsT,
            c.xnT[:, kc * TOK + 512:(kc + 1) * TOK],
            start=(kc == 0), stop=(kc == KC - 1))
    fl = pe_flush(c, c.psA.ap()[:, 0:128])
    fl[5].append(s["sT"].inc())
    p.wait("vector", s["sT"].at())
    p.v("tensor_copy", lg, lg_ps, inc=[s["sV"].inc()])

    p.wait("gpsimd", s["sV"].at())
    aslot = [c.wbuf.ap()[:, 0:1024], c.wbuf.ap()[:, 1024:2048]]
    a_done = {}
    a_free = {}
    h_ps = c.psA.ap()[:, 0:1024]

    asem = [s["sW0"], s["sW1"]]

    def issue_a(ch_):
        sl_ = ch_ % 2
        if ch_ in (2, 3):
            p.wait("gpsimd", (s["sT"].h, a_free[ch_ - 2]))
        p.g("dma_start", out=aslot[sl_], in_=A_ap[ch_],
            inc=[asem[sl_].inc(16)])
        a_done[ch_] = asem[sl_].at()

    issue_a(0)
    issue_a(1)
    for kc in range(KC):
        ch = kc // 8
        sl = ch % 2
        if kc % 8 == 0:
            p.wait("tensor", a_done[ch])
        lhsT = aslot[sl][:, (kc % 8) * 128:(kc % 8 + 1) * 128]
        p.t("matmul", h_ps[:, 0:512], lhsT, c.xnT[:, kc * TOK:kc * TOK + 512],
            start=(kc == 0), stop=False)
        rec = p.t("matmul", h_ps[:, 512:1024], lhsT,
                  c.xnT[:, kc * TOK + 512:(kc + 1) * TOK],
                  start=(kc == 0), stop=(kc == KC - 1))
        if kc % 8 == 7:
            rec[5].append(s["sT"].inc())
            a_free[ch] = s["sT"].n
            if ch + 2 < 4:
                issue_a(ch + 2)
    fl = pe_flush(c, h_ps[:, 0:128])
    fl[5].append(s["sT"].inc())
    h_done = s["sT"].at()

    lgT_ps = psb(c)[:, 3072:3080]
    cbT_ps = psb(c)[0:8, 3584:3712]
    for t in range(NT):
        p.wait("tensor", s["sV"].at())
        p.t("transpose", lgT_ps, lg[:, t * 128:(t + 1) * 128],
            c.ident[0:8, 0:8], inc=[s["sT"].inc()])
        p.wait("vector", s["sT"].at())
        p.v("tensor_copy", l8, lgT_ps)
        p.fence("vector", s["sV"])
        p.v("reduce_max", r8[:, 0:1], l8, axis=AX)
        p.fence("vector", s["sV"])
        p.v("tensor_scalar", r8[:, 2:3], r8[:, 0:1], -1.0, None, ALU.mult)
        p.v("tensor_single_scalar", r8[:, 8:16], l8, r8[:, 0:1], ALU.is_equal)
        p.fence("vector", s["sV"])
        p.v("scalar_tensor_tensor", out=r8[:, 8:16], in0=r8[:, 8:16],
            scalar=NEG, in1=l8, op0=ALU.mult, op1=ALU.add)
        p.fence("vector", s["sV"])
        p.v("reduce_max", r8[:, 1:2], r8[:, 8:16], axis=AX)
        p.fence("vector", s["sV"])
        p.v("tensor_tensor", r8[:, 3:4], r8[:, 1:2], r8[:, 0:1], ALU.subtract,
            inc=[s["sV"].inc()])
        p.wait("scalar", s["sV"].at())
        p.s("activation", r8[:, 4:5], r8[:, 3:4], AF.Exp, inc=[s["sS"].inc()])
        p.wait("vector", s["sS"].at())
        p.v("tensor_scalar", r8[:, 5:6], r8[:, 4:5], 1.0, None, ALU.add)
        p.fence("vector", s["sV"])
        p.v("reciprocal", r8[:, 6:7], r8[:, 5:6], inc=[s["sV"].inc()])
        p.wait("scalar", s["sV"].at())
        p.s("activation", r8[:, 8:16], l8, AF.Exp, bias=r8[:, 2:3],
            scale=1.0, inc=[s["sS"].inc()])
        p.wait("vector", s["sS"].at())
        p.v("tensor_single_scalar", msk, l8, r8[:, 1:2], ALU.is_ge)
        p.fence("vector", s["sV"])
        p.v("tensor_scalar", r8[:, 8:16], r8[:, 8:16], r8[:, 6:7], None,
            ALU.mult)
        p.fence("vector", s["sV"])
        p.v("tensor_tensor", c.cb8, r8[:, 8:16], msk, ALU.mult,
            inc=[s["sV"].inc()])
        if t == 0 and getattr(c, "dbg_cst", None) is not None:
            p.wait("sync", s["sV"].at())
            p.y("dma_start", out=c.dbg_cst, in_=c.cst[:, :],
                inc=[s["sDS"].inc(16), s["sG"].inc()])
            p.wait("vector", s["sG"].at())
        p.wait("tensor", s["sV"].at())
        p.t("transpose", cbT_ps, c.cb8, c.ident, inc=[s["sT"].inc()])
        p.wait("vector", s["sT"].at())
        p.v("tensor_copy", cT[:, t * 128:(t + 1) * 128], cbT_ps,
            inc=[s["sV"].inc()])
    p.wait("tensor", s["sV"].at())
    srep_ps = c.psA.ap()[:, 1024:1536]
    for half in range(2):
        p.t("matmul", srep_ps, c.sb_E, cT[:, half * 512:(half + 1) * 512],
            start=True, stop=True)
        fl = pe_flush(c, srep_ps[:, 0:128])
        fl[5].append(s["sT"].inc())
        sr_done = s["sT"].at()
        p.wait("scalar", h_done)
        p.s("activation", hf, h_ps[:, half * 512:(half + 1) * 512], AF.Copy,
            inc=[s["sS"].inc()])
        p.wait("vector", sr_done, s["sS"].at())
        p.v("tensor_tensor", hs[:, half * 512:(half + 1) * 512], hf, srep_ps,
            ALU.mult, inc=[s["sV"].inc()])
        p.wait("tensor", s["sV"].at())
        p.wait("scalar", s["sV"].at())

    bslot = [c.scr.ap()[:, 2048:2304], c.scr.ap()[:, 2304:2560]]
    bfree = [None, None]
    bdone = {}

    bsem = [s["sB0"], s["sB1"]]

    def pre_og(og):
        sl = og % 2
        gate = bfree[sl] if bfree[sl] is not None else s["sT"].n
        p.wait("gpsimd", (s["sT"].h, gate))
        p.g("dma_start", out=bslot[sl], in_=B_ap[og], inc=[bsem[sl].inc(16)])
        bdone[og] = bsem[sl].at()

    def extra_mm(og):
        sl = og % 2
        p.wait("tensor", bdone[og])
        for oc2 in range(2):
            lhsT = bslot[sl][:, oc2 * 128:(oc2 + 1) * 128]
            a = c.psA.ap()[:, oc2 * 1024:(oc2 + 1) * 1024]
            p.t("matmul", a[:, 0:512], lhsT, hs[:, 0:512],
                start=False, stop=False)
            rec = p.t("matmul", a[:, 512:1024], lhsT, hs[:, 512:1024],
                      start=False, stop=(oc2 == 1))
            if oc2 == 1:
                rec[5].append(s["sT"].inc())
                bfree[sl] = s["sT"].n
        return rec
    return pre_og, extra_mm




DEBUG_FFN = False
NO_LORA = False
SERIAL_W = False
DEFER_TRAIL = True
TRAIL_AT = 4


def build_ffn_l0():
    es = ExitStack()
    nc = bass.Bass("TRN2", target_bir_lowering=False)
    x_in = nc.dram_tensor("x_in", [TOK, D], F32, kind="ExternalInput").ap()
    x_out = nc.dram_tensor("x_out", [TOK, D], F32, kind="ExternalOutput").ap()
    wb = nc.dram_tensor("wbase", [16, NJ, 128, KPP * OPP], F32,
                        kind="ExternalInput").ap()
    b_arr = nc.dram_tensor("b_arr", [128, 32], F32, kind="ExternalInput").ap()
    wr = nc.dram_tensor("wr", [128, 256], F32, kind="ExternalInput").ap()
    A_in = nc.dram_tensor("A_in", [4, 128, 1024], F32, kind="ExternalInput").ap()
    B_in = nc.dram_tensor("B_in", [16, 128, 256], F32, kind="ExternalInput").ap()
    E_in = nc.dram_tensor("E_in", [8, 128], F32, kind="ExternalInput").ap()
    id_in = nc.dram_tensor("id_in", [128, 128], F32, kind="ExternalInput").ap()

    ctx = Ctx(nc, es)
    common_tensors(ctx)
    c, p, s = ctx, ctx.p, ctx.sem
    emit_const_loads(ctx, [(c.b05, b_arr)],
                     gp=[(c.ident, id_in), (c.sb_E, E_in)])
    emit_stream_load(ctx, x_in)
    p.wait("vector", s["sDX"].at(), s["sDW"].at())
    p.v("memset", c.eps, 1e-5)
    p.v("memset", c.zero128, 0.0)
    p.v("tensor_scalar", c.b05, c.b05, 0.5, None, ALU.mult,
        inc=[s["sV"].inc()])
    if DEBUG_FFN:
        ctx.dbg_cst = nc.dram_tensor("dbg_cst", [128, 512], F32,
                                     kind="ExternalOutput").ap()
        dbg_xnT = nc.dram_tensor("dbg_xnT", [128, KC * TOK], BF16,
                                 kind="ExternalOutput").ap()
        dbg_lg = nc.dram_tensor("dbg_lg", [8, 1024], BF16,
                                kind="ExternalOutput").ap()
        dbg_cT = nc.dram_tensor("dbg_cT", [8, 1024], BF16,
                                kind="ExternalOutput").ap()
        dbg_hs = nc.dram_tensor("dbg_hs", [128, 1024], BF16,
                                kind="ExternalOutput").ap()
    emit_ln(ctx)
    if DEBUG_FFN:
        p.wait("sync", s["sV"].at())
        p.y("dma_start", out=dbg_xnT, in_=c.xnT[:, :], inc=[s["sDS"].inc(16)])
    pre_og, extra = emit_router_lora(ctx, wr, A_in, B_in)
    if DEBUG_FFN:
        p.wait("sync", s["sV"].at())
        p.y("dma_start", out=dbg_lg, in_=c.scr.ap()[0:8, 2048:3072], inc=[s["sDS"].inc(16)])
        p.y("dma_start", out=dbg_cT, in_=c.scr.ap()[0:8, 3072:4096], inc=[s["sDS"].inc(16)])
        p.y("dma_start", out=dbg_hs, in_=c.scr.ap()[:, 0:1024], inc=[s["sDS"].inc(16)])
        p.op("sync", None, inc=[s["sG"].inc()])
        p.wait("gpsimd", s["sG"].at())

    def tail(og, acc, rdy):
        emit_ffn_tail(ctx, og, acc, rdy, 0.5, c.b05)

    done_og = set()

    def pieces_ap(og, j):
        if og not in done_og:
            done_og.add(og)
            pre_og(og)
        return wb[og, j]

    emit_dense_fm(ctx, pieces_ap, D, tail,
                  extra_mm=(None if NO_LORA else extra))
    emit_stream_store(ctx, x_out)

    with nc.Block() as block:
        p.replay(block)
    es.close()
    return nc


def ffn_l0_inputs(inputs):
    wbT = np.ascontiguousarray(inputs["L0_ffn_W_base"].T)
    wb = arrange_pieces(wbT)
    b_arr = cols128(np.asarray(inputs["L0_ffn_b_base"]))
    wr = np.ascontiguousarray(
        np.asarray(inputs["L0_ffn_W_router"]).T.reshape(32, 128, 8)
        .transpose(1, 0, 2).reshape(128, 256))
    A = np.ascontiguousarray(
        np.asarray(inputs["L0_ffn_A_stack"]).transpose(2, 0, 1)
        .reshape(32, 128, 128).transpose(1, 0, 2).reshape(128, 4096))
    A = np.ascontiguousarray(A.reshape(128, 4, 1024).transpose(1, 0, 2))
    B = np.ascontiguousarray(
        np.asarray(inputs["L0_ffn_B_stack"]).transpose(0, 2, 1)
        .reshape(128, 4096).reshape(128, 16, 256).transpose(1, 0, 2))
    return {
        "wbase": wb, "b_arr": b_arr, "wr": wr, "A_in": A, "B_in": B,
        "E_in": e_matrix(), "id_in": np.eye(128, dtype=np.float32),
    }


def run_ffn_l0(xs, statics):
    nc = build_ffn_l0()
    in_maps = [dict(statics, x_in=xs[i]) for i in range(8)]
    res = run_bass_kernel_spmd(nc, in_maps, core_ids=list(range(8)))
    return [r["x_out"] for r in res.results]




def emit_qkv_tail(ctx, og, acc, acc_ready, out_ap, rope=False):
    c, p, s = ctx, ctx.p, ctx.sem
    evs = [c.scr.ap()[:, 1024:2048], c.scr.ap()[:, 2048:3072]]
    sems = [s["sB0"], s["sB1"]]
    for oc2 in range(2):
        oc = og * 2 + oc2
        sl = oc % 2
        ev = evs[sl]
        if rope:
            p.wait("vector", acc_ready, sems[sl].at())
            a = acc[:, oc2 * 1024:(oc2 + 1) * 1024]
            t1 = c.psA.ap()[:, 2048:2560]
            cosb = c.scr.ap()[:, 0:1024]
            sinb = c.scr.ap()[:, 3072:4096]
            for h in range(2):
                ah = a[:, h * 512:(h + 1) * 512]
                ch = cosb[:, h * 512:(h + 1) * 512]
                sh = sinb[:, h * 512:(h + 1) * 512]
                evh = ev[:, h * 512:(h + 1) * 512]
                p.v("scalar_tensor_tensor", out=t1[0:64, :], in0=ah[64:128, :],
                    scalar=-1.0, in1=sh[0:64, :], op0=ALU.mult, op1=ALU.mult)
                p.v("scalar_tensor_tensor", out=t1[64:128, :], in0=ah[0:64, :],
                    scalar=1.0, in1=sh[64:128, :], op0=ALU.mult, op1=ALU.mult)
                p.v("tensor_tensor", evh, ah, ch, ALU.mult)
                p.fence("vector", s["sV"])
                p.v("tensor_tensor", evh, evh, t1, ALU.add,
                    inc=[s["sV"].inc()])
        else:
            p.wait("scalar", acc_ready, sems[sl].at(), s["sV"].at())
            p.s("activation", ev, acc[:, oc2 * 1024:(oc2 + 1) * 1024],
                AF.Copy, inc=[s["sS"].inc()])
        if rope:
            p.wait("sync", s["sV"].at())
        else:
            p.wait("sync", s["sS"].at())
        p.y("dma_start", out=out_ap[oc * 128:(oc + 1) * 128, :], in_=ev,
            inc=[sems[sl].inc(16)])
    if rope:
        p.wait("tensor", s["sV"].at())
    else:
        p.wait("tensor", s["sS"].at())


def build_qkv(layer):
    es = ExitStack()
    nc = bass.Bass("TRN2", target_bir_lowering=False)
    nkv = 32 if layer == 0 else 8
    x_in = nc.dram_tensor("x_in", [TOK, D], F32, kind="ExternalInput").ap()
    wq = nc.dram_tensor("wq", [16, NJ, 128, KPP * OPP], F32, kind="ExternalInput").ap()
    wk = nc.dram_tensor("wk", [nkv // 2, NJ, 128, KPP * OPP], F32, kind="ExternalInput").ap()
    wv = nc.dram_tensor("wv", [nkv // 2, NJ, 128, KPP * OPP], F32, kind="ExternalInput").ap()
    id_in = nc.dram_tensor("id_in", [128, 128], F32, kind="ExternalInput").ap()
    qT = nc.dram_tensor("qT", [4096, TOK], BF16, kind="ExternalOutput").ap()
    kT = nc.dram_tensor("kT", [nkv * 128, TOK], BF16, kind="ExternalOutput").ap()
    vT = nc.dram_tensor("vT", [nkv * 128, TOK], BF16, kind="ExternalOutput").ap()
    if layer == 1:
        cos_in = nc.dram_tensor("cos_in", [128, TOK], F32, kind="ExternalInput").ap()
        sin_in = nc.dram_tensor("sin_in", [128, TOK], F32, kind="ExternalInput").ap()

    ctx = Ctx(nc, es)
    common_tensors(ctx)
    c, p, s = ctx, ctx.p, ctx.sem
    emit_const_loads(ctx, [], gp=[(c.ident, id_in)])
    emit_stream_load(ctx, x_in)
    p.wait("vector", s["sDX"].at())
    p.wait("vector", s["sDW"].at())
    p.v("memset", c.eps, 1e-5)
    p.v("memset", c.zero128, 0.0, inc=[s["sV"].inc()])
    p.wait("tensor", s["sV"].at())
    emit_ln(ctx)
    if layer == 1:
        p.wait("gpsimd", s["sT"].at(), s["sV"].at())
        p.g("dma_start", out=c.scr.ap()[:, 0:1024], in_=cos_in,
            inc=[s["sDW"].inc(16)])
        p.g("dma_start", out=c.scr.ap()[:, 3072:4096], in_=sin_in,
            inc=[s["sDW"].inc(16)])
        p.wait("vector", s["sDW"].at())
    rope = (layer == 1)
    emit_dense_fm(ctx, lambda og, j: wq[og, j], D,
                  lambda og, acc, rdy: emit_qkv_tail(ctx, og, acc, rdy, qT, rope=rope))
    emit_dense_fm(ctx, lambda og, j: wk[og, j], nkv * 128,
                  lambda og, acc, rdy: emit_qkv_tail(ctx, og, acc, rdy, kT, rope=rope))
    emit_dense_fm(ctx, lambda og, j: wv[og, j], nkv * 128,
                  lambda og, acc, rdy: emit_qkv_tail(ctx, og, acc, rdy, vT, rope=False))
    p.wait("sync", s["sB0"].at(), s["sB1"].at())
    with nc.Block() as block:
        p.replay(block)
    es.close()
    return nc


def emit_attn_core(ctx, qT_ap, kT_ap, vT_ap, aout_ap, nh, nkv, causal):
    c, p, s = ctx, ctx.p, ctx.sem
    B = c.big.ap()
    pb = psb(c)
    Kh = [B[:, 0:4096], B[:, 4096:8192]]
    VTh = B[:, 8192:12288]
    expTf = B[:, 8192:12288]
    Vh = [B[:, 12288:16384], B[:, 16384:20480]]
    Qs = [B[:, 20480:21504], B[:, 21504:22528]]
    expS = B[:, 22528:26624]
    outT = [B[:, 26624:27648], B[:, 27648:28672]]
    mwin = B[:, 28928:29440]
    rrow = B[0:1, 29952:30080]
    rbcb = B[:, 30080:30208]
    onesr = B[0:1, 30208:30336]
    l_ = c.stat[:, 8:9]
    r_ = c.stat[:, 9:10]
    negm = c.stat[:, 10:11]
    m_ = c.stat[:, 11:12]
    sK = [s["sK0"], s["sK1"]]
    sU = s["sU0"]
    sQ = [s["sQ0"], s["sQ1"]]
    sO = [s["sO0"], s["sO1"]]
    kslot_pe = [0, 0]
    qslot_pe = [0, 0]
    vt_pe = 0
    group = nh // nkv
    nrank = kT_ap.shape[0] // (nkv * 128)
    for kv in range(nkv):
        ksl = kv % 2
        p.wait("gpsimd", (s["sT"].h, kslot_pe[ksl]))
        for r in range(nrank):
            p.g("dma_start", out=Kh[ksl][:, r * 1024:(r + 1) * 1024],
                in_=kT_ap[r * nkv * 128 + kv * 128:
                          r * nkv * 128 + (kv + 1) * 128, :],
                inc=[sK[ksl].inc(16)])
        k_done = sK[ksl].at()
        p.wait("gpsimd", (s["sT"].h, vt_pe))
        for r in range(nrank):
            p.g("dma_start", out=VTh[:, r * 1024:(r + 1) * 1024],
                in_=vT_ap[r * nkv * 128 + kv * 128:
                          r * nkv * 128 + (kv + 1) * 128, :],
                inc=[sU.inc(16)])
        vt_done = sU.at()
        vsl = kv % 2
        p.wait("tensor", vt_done, s["sV"].at())
        ev_done = {}
        for g in range(8):
            bank = g % 2
            bv = pb[:, bank * 1024:bank * 1024 + 512]
            if g >= 2:
                p.wait("tensor", (s["sV"].h, ev_done[g - 2]))
            for kk in range(4):
                kc = g * 4 + kk
                p.t("transpose", bv[:, kk * 128:(kk + 1) * 128],
                    VTh[:, kc * 128:(kc + 1) * 128], c.ident,
                    inc=([s["sT"].inc()] if kk == 3 else []))
            p.wait("vector", s["sT"].at())
            p.v("tensor_copy", Vh[vsl][:, g * 512:(g + 1) * 512], bv,
                inc=[s["sV"].inc()])
            ev_done[g] = s["sV"].n
        p.wait("tensor", s["sV"].at())
        for qg in range(group):
            qh = kv * group + qg
            qsl = qh % 2
            p.wait("gpsimd", (s["sT"].h, qslot_pe[qsl]))
            p.g("dma_start", out=Qs[qsl], in_=qT_ap[qh * 128:(qh + 1) * 128, :],
                inc=[sQ[qsl].inc(16)])
            q_done = sQ[qsl].at()
            osl = qh % 2
            p.wait("vector", sO[osl].at())
            for t in range(NT):
                if causal:
                    w = (t + 1) * 128
                    ranges = [(r * 1024, w) for r in range(4)]
                else:
                    w = 4096
                    ranges = [(0, 4096)]
                kwidth = sum(rr[1] for rr in ranges)
                p.wait("tensor", k_done, q_done, s["sV"].at())
                sview = c.psA.ap().rearrange("p (r q) -> p r q", r=4)
                mview = mwin.rearrange("p (r c) -> p r c", r=4)
                m4 = c.stat[:, 12:16]
                mh = c.stat[:, 1:3]
                sh_done = []
                for hf in range(2):
                    for col0, wd in ranges[2 * hf:2 * hf + 2] if causal else                             [(hf * 2048, 2048)]:
                        i = 0
                        while i < wd:
                            n = min(512, wd - i)
                            p.t("matmul", c.psA.ap()[:, col0 + i:col0 + i + n],
                                Qs[qsl][:, t * 128:(t + 1) * 128],
                                Kh[ksl][:, col0 + i:col0 + i + n],
                                start=True, stop=True)
                            i += n
                    fl = pe_flush(c, c.psA.ap()[:, hf * 2048:hf * 2048 + 128])
                    fl[5].append(s["sT"].inc())
                    sh_done.append(s["sT"].at())
                s_done = s["sT"].at()
                kslot_pe[ksl] = s["sT"].n
                qslot_pe[qsl] = s["sT"].n
                for hf in range(2):
                    p.wait("vector", sh_done[hf])
                    if causal:
                        dv = sview[:, 2 * hf:2 * hf + 2, t * 128:(t + 1) * 128]
                        mv = mview[:, 2 * hf:2 * hf + 2, :]
                        p.v("tensor_tensor", dv, dv, mv, ALU.add)
                        p.fence("vector", s["sV"])
                        ssub = c.psA.ap().rearrange(
                            "p (r a z) -> p r a z", r=4, z=8)[
                            :, 2 * hf:2 * hf + 2, 0:w // 8, 0]
                        p.v("reduce_max", m4[:, 2 * hf:2 * hf + 2], ssub,
                            axis=AX)
                        p.fence("vector", s["sV"])
                        p.v("reduce_max", mh[:, hf:hf + 1],
                            m4[:, 2 * hf:2 * hf + 2], axis=AX)
                    else:
                        ssub = c.psA.ap().rearrange(
                            "p (a z) -> p a z", z=8)[
                            :, hf * 256:(hf + 1) * 256, 0]
                        p.v("reduce_max", mh[:, hf:hf + 1], ssub, axis=AX)
                    p.fence("vector", s["sV"])
                p.v("reduce_max", m_, mh, axis=AX)
                p.fence("vector", s["sV"])
                p.v("tensor_scalar", negm, m_, -SCL, None, ALU.mult,
                    inc=[s["sV"].inc()])
                p.wait("scalar", s["sV"].at())
                l_b = c.stat[:, 0:1]
                if causal:
                    eviewA = expS[:, 0:2 * w].rearrange(
                        "p (r q) -> p r q", r=2)
                    eviewB = expS[:, 2 * w:4 * w].rearrange(
                        "p (r q) -> p r q", r=2)
                    p.s("activation", eviewA, sview[:, 0:2, 0:w],
                        AF.Exp, bias=negm, scale=SCL, accum_out=l_,
                        inc=[s["sS"].inc()])
                    p.s("activation", eviewB, sview[:, 2:4, 0:w],
                        AF.Exp, bias=negm, scale=SCL, accum_out=l_b,
                        inc=[s["sS"].inc()])
                else:
                    p.s("activation", expS[:, 0:2048], c.psA.ap()[:, 0:2048],
                        AF.Exp, bias=negm, scale=SCL, accum_out=l_,
                        inc=[s["sS"].inc()])
                    p.s("activation", expS[:, 2048:4096],
                        c.psA.ap()[:, 2048:4096],
                        AF.Exp, bias=negm, scale=SCL, accum_out=l_b,
                        inc=[s["sS"].inc()])
                exp_done = s["sS"].n
                p.wait("vector", s["sS"].at())
                p.v("tensor_tensor", l_, l_, l_b, ALU.add)
                p.fence("vector", s["sV"])
                p.v("reciprocal", r_, l_)
                p.fence("vector", s["sV"])
                p.v("tensor_copy", c.stat.bitcast(BF16)[:, 24:25], r_,
                    inc=[s["sV"].inc()])
                r_bf = c.stat.bitcast(BF16)[:, 24:25]
                rv_done = s["sV"].at()
                nkc = kwidth // 128
                out_ps = c.psA.ap()[:, 2048:2176]

                def vcol(cc):
                    if not causal:
                        return cc * 128
                    rr_, jj = cc // (w // 128), cc % (w // 128)
                    return rr_ * 1024 + jj * 128

                half = 8 * ((nkc // 2 + 7) // 8)
                if half >= nkc:
                    half = nkc
                p.wait("tensor", (s["sS"].h, exp_done - 1))
                ev_t = []
                for cc in range(nkc):
                    if cc == nkc // 2:
                        p.wait("tensor", (s["sS"].h, exp_done))
                    tslot = pb[:, cc * 128:(cc + 1) * 128]
                    rec = p.t("transpose", tslot,
                              expS[:, cc * 128:(cc + 1) * 128], c.ident)
                    if cc == half - 1 or cc == nkc - 1:
                        fl = p.t("transpose", tslot,
                                 expS[:, cc * 128:(cc + 1) * 128], c.ident)
                        fl[5].append(s["sT"].inc())
                        ev_t.append(s["sT"].n)
                p.wait("tensor", rv_done)
                rrow_ps = pb[0:1, 5 * 1024 + 512:5 * 1024 + 640]
                p.t("transpose", rrow_ps, r_bf, c.ident,
                    inc=[s["sT"].inc()])
                p.wait("vector", (s["sT"].h, ev_t[0]))
                p.v("tensor_copy", expTf[:, 0:half * 128],
                    pb[:, 0:half * 128], inc=[s["sV"].inc()])
                evA = s["sV"].n
                p.wait("vector", s["sT"].at())
                p.v("tensor_copy", rrow, rrow_ps)
                if nkc > half:
                    p.wait("vector", (s["sT"].h, ev_t[1]))
                    p.v("tensor_copy", expTf[:, half * 128:nkc * 128],
                        pb[:, half * 128:nkc * 128], inc=[s["sV"].inc()])
                evB = s["sV"].n
                p.op("vector", None, inc=[s["sV"].inc()])
                p.wait("tensor", (s["sV"].h, evA), (s["sS"].h, exp_done))
                for cc in range(half):
                    vc = vcol(cc)
                    p.t("matmul", out_ps, Vh[vsl][:, vc:vc + 128],
                        expTf[:, cc * 128:(cc + 1) * 128],
                        start=(cc == 0), stop=False)
                if nkc > half:
                    p.wait("tensor", (s["sV"].h, evB))
                for cc in range(half, nkc):
                    vc = vcol(cc)
                    p.t("matmul", out_ps, Vh[vsl][:, vc:vc + 128],
                        expTf[:, cc * 128:(cc + 1) * 128],
                        start=False, stop=(cc == nkc - 1))
                rbc_ps = c.psA.ap()[:, 2560:2688]
                p.wait("tensor", s["sV"].at())
                p.t("matmul", rbc_ps, onesr, rrow, start=True, stop=True)
                fl = pe_flush(c, rbc_ps)
                fl[5].append(s["sT"].inc())
                p.wait("vector", s["sT"].at())
                p.v("tensor_copy", rbcb, rbc_ps)
                p.fence("vector", s["sV"])
                p.v("tensor_tensor", outT[osl][:, t * 128:(t + 1) * 128],
                    out_ps, rbcb, ALU.mult, inc=[s["sV"].inc()])
            p.wait("sync", s["sV"].at())
            p.y("dma_start", out=aout_ap[qh * 128:(qh + 1) * 128, :],
                in_=outT[osl], inc=[sO[osl].inc(16)])
        vt_pe = s["sT"].n
    p.wait("tensor", s["sV"].at())


def emit_dense_tm(ctx, pieces_ap, attnT, n_out, opp_o=512, kpp_e=2):
    c, p, s = ctx, ctx.p, ctx.sem
    n_og = n_out // opp_o
    n_j = KC // kpp_e
    wslot = [c.wbuf.ap()[:, 0:kpp_e * opp_o],
             c.wbuf.ap()[:, 1024:1024 + kpp_e * opp_o]]
    wsem = [s["sW0"], s["sW1"]]
    slot_free = [s["sT"].n, s["sT"].n]
    dma_done = {}

    def issue_piece(idx):
        og_, j_ = idx // n_j, idx % n_j
        sl_ = idx % 2
        p.wait("gpsimd", (s["sT"].h, slot_free[sl_]))
        p.g("dma_start", out=wslot[sl_], in_=pieces_ap(og_, j_),
            inc=[wsem[sl_].inc(16)])
        dma_done[idx] = wsem[sl_].at()

    issue_piece(0)
    issue_piece(1)
    for og in range(n_og):
        for ec in range(KC):
            j, ks = ec // kpp_e, ec % kpp_e
            idx = og * n_j + j
            sl = idx % 2
            if ks == 0:
                p.wait("tensor", dma_done[idx])
            for t in range(NT):
                rec = p.t("matmul", c.psA.ap()[:, t * 512:(t + 1) * 512],
                          attnT[:, ec * TOK + t * 128: ec * TOK + (t + 1) * 128],
                          wslot[sl][:, ks * opp_o:(ks + 1) * opp_o],
                          start=(ec == 0), stop=(ec == KC - 1))
                if ks == kpp_e - 1 and t == NT - 1:
                    rec[5].append(s["sT"].inc())
                    slot_free[sl] = s["sT"].n
                    if idx + 2 < n_og * n_j:
                        issue_piece(idx + 2)
        fl = pe_flush(c, c.psA.ap()[:, 0:128])
        fl[5].append(s["sT"].inc())
        acc_ready = s["sT"].at()
        for t in range(NT):
            p.wait("vector", acc_ready)
            dst = c.sx[:, t * D + og * opp_o: t * D + (og + 1) * opp_o]
            p.v("tensor_tensor", dst, dst, c.psA.ap()[:, t * 512:(t + 1) * 512],
                ALU.add, inc=[s["sV"].inc()])
        p.wait("tensor", s["sV"].at())


def build_attnB(layer):
    es = ExitStack()
    nc = bass.Bass("TRN2", target_bir_lowering=False)
    nh = 32
    nkv = 32 if layer == 0 else 8
    causal = (layer == 1)
    x_in = nc.dram_tensor("x_in", [TOK, D], F32, kind="ExternalInput").ap()
    x_out = nc.dram_tensor("x_out", [TOK, D], F32, kind="ExternalOutput").ap()
    qT = nc.dram_tensor("qT", [4096, TOK], BF16, kind="ExternalInput").ap()
    kT = nc.dram_tensor("kT", [4 * nkv * 128, TOK], BF16, kind="ExternalInput").ap()
    vT = nc.dram_tensor("vT", [4 * nkv * 128, TOK], BF16, kind="ExternalInput").ap()
    wo = nc.dram_tensor("wo", [8, 16, 128, 1024], F32, kind="ExternalInput").ap()
    id_in = nc.dram_tensor("id_in", [128, 128], F32, kind="ExternalInput").ap()
    mw_in = nc.dram_tensor("mw_in", [128, 512], F32, kind="ExternalInput").ap()
    aout = nc.dram_tensor("aout", [4096, TOK], BF16, kind="Internal").ap()

    ctx = Ctx(nc, es)
    common_tensors(ctx)
    c, p, s = ctx, ctx.p, ctx.sem
    c.mksem("sK0", "sK1", "sU0", "sQ0", "sQ1", "sO0", "sO1")
    gp = [(c.ident, id_in)]
    if causal:
        gp.append((c.big.ap()[:, 28928:29440], mw_in))
    emit_const_loads(ctx, [], gp=gp)
    emit_stream_load(ctx, x_in)
    p.wait("vector", s["sDX"].at(), s["sDW"].at())
    p.v("memset", c.big.ap()[0:1, 30208:30336], 1.0)
    p.v("memset", c.zero128, 0.0, inc=[s["sV"].inc()])
    p.wait("tensor", s["sV"].at())
    emit_attn_core(ctx, qT, kT, vT, aout, nh, nkv, causal)
    p.wait("sync", s["sV"].at(), s["sO0"].at(), s["sO1"].at())
    for g4 in range(8):
        src = aout[g4 * 512:(g4 + 1) * 512, :].rearrange(
            "(e p) t -> p e t", p=128)
        dst = c.big.ap().rearrange("p (e t) -> p e t", e=KC)[
            :, g4 * 4:(g4 + 1) * 4, :]
        p.y("dma_start", out=dst, in_=src, inc=[s["sDX"].inc(16)])
    p.wait("tensor", s["sDX"].at())
    emit_dense_tm(ctx, lambda og, j: wo[og, j], c.big.ap(), D)
    emit_stream_store(ctx, x_out)
    with nc.Block() as block:
        p.replay(block)
    es.close()
    return nc


def emit_dequant(ctx, qw_ap, scratch_ap):
    c, p, s = ctx, ctx.p, ctx.sem
    B = c.big.ap()
    bi = c.big.ap().bitcast(I32)
    slab = B[:, 0:8192]
    u_int = bi[:, 4096:8192]
    u = B[:, 28672:32768]
    qsl = [bi[:, 8192:11264], bi[:, 11264:14336]]
    qsem = [s["sK0"], s["sK1"]]
    pb = psb(c)
    q_pe = [0, 0]
    for nt in range(32):
        sl = nt % 2
        p.wait("sync", (s["sV"].h, q_pe[sl]))
        p.y("dma_start", out=qsl[sl], in_=qw_ap[nt * 128:(nt + 1) * 128, :],
            inc=[qsem[sl].inc(16)])
        q_done = qsem[sl].at()
        qv = qsl[sl].rearrange("p (g r) -> p g r", r=3)
        uv = u_int.rearrange("p (g r) -> p g r", r=4)
        p.wait("vector", q_done, s["sT"].at())
        p.v("tensor_scalar", uv[:, :, 0], qv[:, :, 0], 63, None, ALU.bitwise_and)
        p.v("tensor_scalar", uv[:, :, 1], qv[:, :, 0], 6, 63,
            ALU.logical_shift_right, ALU.bitwise_and)
        p.v("tensor_scalar", uv[:, :, 2], qv[:, :, 1], 63, None, ALU.bitwise_and)
        p.v("tensor_scalar", uv[:, :, 3], qv[:, :, 2], 63, None, ALU.bitwise_and,
            inc=[s["sV"].inc()])
        p.fence("vector", s["sV"])
        p.v("tensor_copy", u, u_int, inc=[s["sV"].inc()])
        q_pe[sl] = s["sV"].n
        p.wait("tensor", s["sV"].at())
        half = nt % 2
        ev_done = {}
        for g in range(8):
            bank = 6 + g % 2
            bv = pb[:, bank * 1024:bank * 1024 + 512]
            if g >= 2:
                p.wait("tensor", (s["sV"].h, ev_done[g - 2]))
            for kk in range(4):
                kc = g * 4 + kk
                p.t("transpose", bv[:, kk * 128:(kk + 1) * 128],
                    u[:, kc * 128:(kc + 1) * 128], c.ident,
                    inc=([s["sT"].inc()] if kk == 3 else []))
            p.wait("vector", s["sT"].at())
            dst = slab.rearrange("p (j k h c) -> p j k h c", j=8, k=4, h=2)[
                :, g, :, half, :]
            src = bv.rearrange("p (k c) -> p k c", k=4)
            p.v("tensor_copy", dst, src, inc=[s["sV"].inc()])
            ev_done[g] = s["sV"].n
        if half == 1:
            og = nt // 2
            p.wait("sync", s["sV"].at())
            for j in range(8):
                p.y("dma_start", out=scratch_ap[og, j],
                    in_=slab[:, j * 1024:(j + 1) * 1024],
                    inc=[s["sDS"].inc(16)])
            p.wait("vector", s["sDS"].at())
    p.wait("tensor", s["sV"].at())
    p.wait("gpsimd", s["sDS"].at())


def build_ffn_l1():
    es = ExitStack()
    nc = bass.Bass("TRN2", target_bir_lowering=False)
    x_in = nc.dram_tensor("x_in", [TOK, D], F32, kind="ExternalInput").ap()
    x_out = nc.dram_tensor("x_out", [TOK, D], F32, kind="ExternalOutput").ap()
    qw = nc.dram_tensor("qw", [4096, 3072], I32, kind="ExternalInput").ap()
    s_arr = nc.dram_tensor("s_arr", [128, 32], F32, kind="ExternalInput").ap()
    b_arr = nc.dram_tensor("b_arr", [128, 32], F32, kind="ExternalInput").ap()
    z_arr = nc.dram_tensor("z_arr", [128, 32], F32, kind="ExternalInput").ap()
    id_in = nc.dram_tensor("id_in", [128, 128], F32, kind="ExternalInput").ap()
    scratch = nc.dram_tensor("wscr", [16, NJ, 128, KPP * OPP], BF16,
                             kind=("ExternalOutput" if DEBUG_FFN else "Internal")).ap()
    rowscr = nc.dram_tensor("rowscr", [40, 128], BF16, kind="Internal").ap()
    ctx = Ctx(nc, es)
    common_tensors(ctx)
    c, p, s = ctx, ctx.p, ctx.sem
    c.mksem("sK0", "sK1")
    emit_const_loads(ctx, [(c.s05, s_arr), (c.b05, b_arr),
                           (c.scrf[:, 0:32], z_arr)],
                     gp=[(c.ident, id_in)])
    p.wait("vector", s["sDX"].at(), s["sDW"].at())
    p.v("memset", c.eps, 1e-5)
    p.v("memset", c.zero128, 0.0)
    p.v("tensor_scalar", c.s05, c.s05, 0.5, None, ALU.mult)
    p.v("tensor_scalar", c.b05, c.b05, 0.5, None, ALU.mult)
    p.fence("vector", s["sV"])
    p.v("tensor_tensor", c.zs05, c.scrf[:, 0:32], c.s05, ALU.mult)
    p.fence("vector", s["sV"])
    p.v("tensor_scalar", c.zs05, c.zs05, -1.0, None, ALU.mult,
        inc=[s["sV"].inc()])
    emit_dequant(ctx, qw, scratch)
    emit_stream_load(ctx, x_in)
    p.wait("vector", s["sDX"].at())
    emit_ln(ctx, rowsum=True)
    zT_ps = psb(c)[0:32, 0:128]
    rT_ps = psb(c)[0:8, 1024:1152]
    zs_bf = c.scr.ap()[:, 256:288]
    rs_bf = c.scr.ap()[:, 288:296]
    p.v("tensor_copy", zs_bf, c.zs05)
    p.v("tensor_copy", rs_bf, c.rs8, inc=[s["sV"].inc()])
    p.wait("tensor", s["sV"].at())
    p.t("transpose", zT_ps, zs_bf, c.ident, inc=[s["sT"].inc()])
    p.t("transpose", rT_ps, rs_bf, c.ident)
    fl = pe_flush(c, c.psA.ap()[:, 1536:1664])
    fl[5].append(s["sT"].inc())
    p.wait("vector", s["sT"].at())
    p.v("tensor_copy", c.scr.ap()[0:32, 0:128], zT_ps)
    p.v("tensor_copy", c.scr.ap()[0:8, 128:256], rT_ps, inc=[s["sV"].inc()])
    p.wait("sync", s["sV"].at())
    p.y("dma_start", out=rowscr[0:32, :], in_=c.scr.ap()[0:32, 0:128],
        inc=[s["sDS"].inc(16)])
    p.y("dma_start", out=rowscr[32:40, :], in_=c.scr.ap()[0:8, 128:256],
        inc=[s["sDS"].inc(16)])
    p.wait("sync", s["sDS"].at())
    p.y("dma_start", out=c.scr.ap()[0:1, 2048:4096],
        in_=rowscr[0:16, :].rearrange("a b -> (a b)").unsqueeze(0),
        inc=[s["sDX"].inc(16)])
    p.y("dma_start", out=c.scr.ap()[32:33, 2048:4096],
        in_=rowscr[16:32, :].rearrange("a b -> (a b)").unsqueeze(0),
        inc=[s["sDX"].inc(16)])
    p.y("dma_start", out=c.scr.ap()[0:1, 0:1024],
        in_=rowscr[32:40, :].rearrange("a b -> (a b)").unsqueeze(0),
        inc=[s["sDX"].inc(16)])
    p.y("dma_start", out=c.scr.ap()[32:33, 0:1024],
        in_=rowscr[32:40, :].rearrange("a b -> (a b)").unsqueeze(0),
        inc=[s["sDX"].inc(16)])
    p.wait("tensor", s["sDX"].at())

    def zrow_fn(oc):
        part = 0 if oc < 16 else 32
        return c.scr.ap()[part:part + 1, 2048 + (oc % 16) * 128:
                          2048 + (oc % 16 + 1) * 128]

    def rsrow_fn(oc):
        part = 0 if oc < 16 else 32
        return c.scr.ap()[part:part + 1, 0:1024]

    def tail(og, acc, rdy):
        emit_ffn_tail(ctx, og, acc, rdy, c.s05, c.b05)
    if DEBUG_FFN:
        dbg_xnT = nc.dram_tensor("dbg_xnT", [128, KC * TOK], BF16,
                                 kind="ExternalOutput").ap()
        p.wait("sync", s["sV"].at())
        p.y("dma_start", out=dbg_xnT, in_=c.xnT[:, :], inc=[s["sDS"].inc(16)])
        p.wait("vector", s["sDS"].at())

    def tail(og, acc, rdy):
        emit_ffn_tail(ctx, og, acc, rdy, c.s05, c.b05, dbg=dbg_ev,
                      dbg2=dbg_ev2)

    emit_dense_fm(ctx, lambda og, j: scratch[og, j], D, tail,
                  rank1=(zrow_fn, rsrow_fn))
    emit_stream_store(ctx, x_out)
    with nc.Block() as block:
        p.replay(block)
    es.close()
    return nc



_NC_CACHE = {}


def _get_nc(name, builder):
    if name not in _NC_CACHE:
        _NC_CACHE[name] = builder()
    return _NC_CACHE[name]


def _run(name, builder, per_core):
    nc = _get_nc(name, builder)
    res = run_bass_kernel_spmd(nc, per_core, core_ids=list(range(8)))
    return res.results


def core_rows(c):
    g = c % 4
    idx = []
    for t in range(NT):
        base = (4 * t + g) * 128
        idx.extend(range(base, base + 128))
    return np.array(idx)


def maskwin_host(cg):
    m = np.full((128, 512), NEG, dtype=np.float32)
    for rr in range(128):
        m[rr, 0:cg * 128] = 0.0
        m[rr, cg * 128:cg * 128 + rr + 1] = 0.0
    return m


def _OLD_kernel(**inputs):
    inputs = {k: np.asarray(v) for k, v in inputs.items()}
    x = inputs["L0_attn_x"]
    rows = [core_rows(c) for c in range(8)]
    xs = [np.ascontiguousarray(x[c // 4][rows[c]]) for c in range(8)]
    ident = np.eye(128, dtype=np.float32)

    st_ffn0 = ffn_l0_inputs(inputs)
    st_qkv0 = {
        "wq": arrange_pieces(np.ascontiguousarray(inputs["L0_attn_w_q"].T)),
        "wk": arrange_pieces(np.ascontiguousarray(inputs["L0_attn_w_k"].T)),
        "wv": arrange_pieces(np.ascontiguousarray(inputs["L0_attn_w_v"].T)),
        "id_in": ident,
    }
    wo0 = arrange_pieces(np.ascontiguousarray(inputs["L0_attn_w_o"].T),
                         opp=512, kpp=2)
    st_qkv1 = {
        "wq": arrange_pieces(np.ascontiguousarray(inputs["L1_attn_Wq"])),
        "wk": arrange_pieces(np.ascontiguousarray(inputs["L1_attn_Wk"])),
        "wv": arrange_pieces(np.ascontiguousarray(inputs["L1_attn_Wv"])),
        "id_in": ident,
    }
    wo1 = arrange_pieces(np.ascontiguousarray(inputs["L1_attn_Wo"]),
                         opp=512, kpp=2)
    st_ffn1 = {
        "qw": np.ascontiguousarray(inputs["L1_ffn_qweight"]),
        "s_arr": cols128(np.ascontiguousarray(inputs["L1_ffn_weight_scale"][:, 0])),
        "b_arr": cols128(np.ascontiguousarray(inputs["L1_ffn_bias"])),
        "z_arr": cols128(np.ascontiguousarray(inputs["L1_ffn_weight_zero_point"][:, 0])),
        "id_in": ident,
    }
    cosf, sinf = rope_tables()
    cos_c = [np.ascontiguousarray(cosf[:, rows[c]]) for c in range(8)]
    sin_c = [np.ascontiguousarray(sinf[:, rows[c]]) for c in range(8)]

    def gather_kv(shards):
        full = [np.ascontiguousarray(np.concatenate(shards[g * 4:(g + 1) * 4], axis=0))
                for g in range(2)]
        return [full[c // 4] for c in range(8)]

    r = _run("ffn0", build_ffn_l0, [dict(st_ffn0, x_in=xs[i]) for i in range(8)])
    x1 = [r[i]["x_out"] for i in range(8)]
    r = _run("qkv0", lambda: build_qkv(0),
             [dict(st_qkv0, x_in=x1[i]) for i in range(8)])
    q0 = [r[i]["qT"] for i in range(8)]
    k0 = gather_kv([r[i]["kT"] for i in range(8)])
    v0 = gather_kv([r[i]["vT"] for i in range(8)])
    r = _run("attnB0", lambda: build_attnB(0),
             [dict(x_in=x1[i], qT=q0[i], kT=k0[i], vT=v0[i], wo=wo0,
                   id_in=ident, mw_in=np.zeros((128, 512), dtype=np.float32))
              for i in range(8)])
    x2 = [r[i]["x_out"] for i in range(8)]
    r = _run("ffn0", build_ffn_l0, [dict(st_ffn0, x_in=x2[i]) for i in range(8)])
    x3 = [r[i]["x_out"] for i in range(8)]
    r = _run("ffn1", build_ffn_l1, [dict(st_ffn1, x_in=x3[i]) for i in range(8)])
    x4 = [r[i]["x_out"] for i in range(8)]
    r = _run("qkv1", lambda: build_qkv(1),
             [dict(st_qkv1, x_in=x4[i], cos_in=cos_c[i], sin_in=sin_c[i])
              for i in range(8)])
    q1 = [r[i]["qT"] for i in range(8)]
    k1 = gather_kv([r[i]["kT"] for i in range(8)])
    v1 = gather_kv([r[i]["vT"] for i in range(8)])
    mws = [maskwin_host(c % 4) for c in range(8)]
    r = _run("attnB1", lambda: build_attnB(1),
             [dict(x_in=x4[i], qT=q1[i], kT=k1[i], vT=v1[i], wo=wo1,
                   id_in=ident, mw_in=mws[i]) for i in range(8)])
    x5 = [r[i]["x_out"] for i in range(8)]
    r = _run("ffn1", build_ffn_l1, [dict(st_ffn1, x_in=x5[i]) for i in range(8)])
    xo = [r[i]["x_out"] for i in range(8)]

    out = np.empty((2, 4096, 4096), dtype=np.float32)
    for c in range(8):
        out[c // 4][rows[c]] = xo[c]
    return out


def emit_barrier(ctx):
    p, s = ctx.p, ctx.sem
    tgt = [s["sT"].at(), s["sV"].at(), s["sS"].at()]
    for eng in ENGINES:
        p.wait(eng, *tgt)


def emit_zrow_staging(ctx, rowscr):
    c, p, s = ctx, ctx.p, ctx.sem
    zT_ps = psb(c)[0:32, 0:128]
    rT_ps = psb(c)[0:8, 1024:1152]
    zs_bf = c.scr.ap()[:, 256:288]
    rs_bf = c.scr.ap()[:, 288:296]
    p.wait("vector", s["sT"].at())
    p.v("tensor_copy", zs_bf, c.zs05)
    p.v("tensor_copy", rs_bf, c.rs8, inc=[s["sV"].inc()])
    p.wait("tensor", s["sV"].at())
    p.t("transpose", zT_ps, zs_bf, c.ident, inc=[s["sT"].inc()])
    p.t("transpose", rT_ps, rs_bf, c.ident)
    fl = pe_flush(c, c.psA.ap()[:, 1536:1664])
    fl[5].append(s["sT"].inc())
    p.wait("vector", s["sT"].at())
    p.v("tensor_copy", c.scr.ap()[0:32, 0:128], zT_ps)
    p.v("tensor_copy", c.scr.ap()[0:8, 128:256], rT_ps, inc=[s["sV"].inc()])
    p.wait("sync", s["sV"].at())
    p.y("dma_start", out=rowscr[0:32, :], in_=c.scr.ap()[0:32, 0:128],
        inc=[s["sDS"].inc(16)])
    p.y("dma_start", out=rowscr[32:40, :], in_=c.scr.ap()[0:8, 128:256],
        inc=[s["sDS"].inc(16)])
    p.wait("sync", s["sDS"].at())
    p.y("dma_start", out=c.scr.ap()[0:1, 2048:4096],
        in_=rowscr[0:16, :].rearrange("a b -> (a b)").unsqueeze(0),
        inc=[s["sDX"].inc(16)])
    p.y("dma_start", out=c.scr.ap()[32:33, 2048:4096],
        in_=rowscr[16:32, :].rearrange("a b -> (a b)").unsqueeze(0),
        inc=[s["sDX"].inc(16)])
    p.y("dma_start", out=c.scr.ap()[0:1, 0:1024],
        in_=rowscr[32:40, :].rearrange("a b -> (a b)").unsqueeze(0),
        inc=[s["sDX"].inc(16)])
    p.y("dma_start", out=c.scr.ap()[32:33, 0:1024],
        in_=rowscr[32:40, :].rearrange("a b -> (a b)").unsqueeze(0),
        inc=[s["sDX"].inc(16)])
    p.wait("tensor", s["sDX"].at())

    def zrow_fn(oc):
        part = 0 if oc < 16 else 32
        return c.scr.ap()[part:part + 1, 2048 + (oc % 16) * 128:
                          2048 + (oc % 16 + 1) * 128]

    def rsrow_fn(oc):
        part = 0 if oc < 16 else 32
        return c.scr.ap()[part:part + 1, 0:1024]
    return zrow_fn, rsrow_fn


def build_fused():
    es = ExitStack()
    nc = bass.Bass("TRN2", target_bir_lowering=False)
    dt_ = nc.dram_tensor
    x_in = dt_("x_in", [TOK, D], F32, kind="ExternalInput").ap()
    x_out = dt_("x_out", [TOK, D], F32, kind="ExternalOutput").ap()
    wb = dt_("wbase", [16, NJ, 128, KPP * OPP], F32, kind="ExternalInput").ap()
    b_arr0 = dt_("b_arr0", [128, 32], F32, kind="ExternalInput").ap()
    wr = dt_("wr", [128, 256], F32, kind="ExternalInput").ap()
    A_in = dt_("A_in", [4, 128, 1024], F32, kind="ExternalInput").ap()
    B_in = dt_("B_in", [16, 128, 256], F32, kind="ExternalInput").ap()
    E_in = dt_("E_in", [8, 128], F32, kind="ExternalInput").ap()
    id_in = dt_("id_in", [128, 128], F32, kind="ExternalInput").ap()
    wq0 = dt_("wq0", [16, NJ, 128, KPP * OPP], F32, kind="ExternalInput").ap()
    wk0 = dt_("wk0", [16, NJ, 128, KPP * OPP], F32, kind="ExternalInput").ap()
    wv0 = dt_("wv0", [16, NJ, 128, KPP * OPP], F32, kind="ExternalInput").ap()
    wo0 = dt_("wo0", [8, 16, 128, 1024], F32, kind="ExternalInput").ap()
    wq1 = dt_("wq1", [16, NJ, 128, KPP * OPP], F32, kind="ExternalInput").ap()
    wk1 = dt_("wk1", [4, NJ, 128, KPP * OPP], F32, kind="ExternalInput").ap()
    wv1 = dt_("wv1", [4, NJ, 128, KPP * OPP], F32, kind="ExternalInput").ap()
    wo1 = dt_("wo1", [8, 16, 128, 1024], F32, kind="ExternalInput").ap()
    qw = dt_("qw", [4096, 3072], I32, kind="ExternalInput").ap()
    s_arr = dt_("s_arr", [128, 32], F32, kind="ExternalInput").ap()
    b_arr1 = dt_("b_arr1", [128, 32], F32, kind="ExternalInput").ap()
    z_arr = dt_("z_arr", [128, 32], F32, kind="ExternalInput").ap()
    cos_in = dt_("cos_in", [128, TOK], F32, kind="ExternalInput").ap()
    sin_in = dt_("sin_in", [128, TOK], F32, kind="ExternalInput").ap()
    mw_in = dt_("mw_in", [128, 512], F32, kind="ExternalInput").ap()
    wscr = dt_("wscr", [16, NJ, 128, KPP * OPP], BF16, kind="Internal").ap()
    rowscr = dt_("rowscr", [40, 128], BF16, kind="Internal").ap()
    qb = dt_("qb", [4096, TOK], BF16, kind="Internal").ap()
    aout = dt_("aout", [4096, TOK], BF16, kind="Internal").ap()
    k0i = dt_("k0i", [4096, TOK], BF16, kind="Internal").ap()
    v0i = dt_("v0i", [4096, TOK], BF16, kind="Internal").ap()
    k0o = dt_("k0o", [16384, TOK], BF16, kind="Internal").ap()
    v0o = dt_("v0o", [16384, TOK], BF16, kind="Internal").ap()
    k1i = dt_("k1i", [1024, TOK], BF16, kind="Internal").ap()
    v1i = dt_("v1i", [1024, TOK], BF16, kind="Internal").ap()
    k1o = dt_("k1o", [4096, TOK], BF16, kind="Internal").ap()
    v1o = dt_("v1o", [4096, TOK], BF16, kind="Internal").ap()

    ctx = Ctx(nc, es)
    common_tensors(ctx)
    c, p, s = ctx, ctx.p, ctx.sem
    c.mksem("sK0", "sK1", "sU0", "sQ0", "sQ1", "sO0", "sO1")
    c.b05L1 = c.cst.ap()[:, 80:112]
    rg = [[0, 1, 2, 3], [4, 5, 6, 7]]

    emit_const_loads(ctx, [(c.b05, b_arr0), (c.s05, s_arr),
                           (c.b05L1, b_arr1), (c.scrf[:, 0:32], z_arr)],
                     gp=[(c.ident, id_in), (c.sb_E, E_in)])
    emit_stream_load(ctx, x_in)
    p.wait("vector", s["sDX"].at(), s["sDW"].at())
    p.v("memset", c.eps, 1e-5)
    p.v("memset", c.zero128, 0.0)
    p.v("tensor_scalar", c.b05, c.b05, 0.5, None, ALU.mult)
    p.v("tensor_scalar", c.s05, c.s05, 0.5, None, ALU.mult)
    p.v("tensor_scalar", c.b05L1, c.b05L1, 0.5, None, ALU.mult)
    p.fence("vector", s["sV"])
    p.v("tensor_tensor", c.zs05, c.scrf[:, 0:32], c.s05, ALU.mult)
    p.fence("vector", s["sV"])
    p.v("tensor_scalar", c.zs05, c.zs05, -1.0, None, ALU.mult,
        inc=[s["sV"].inc()])
    p.wait("tensor", s["sV"].at())
    emit_dequant(ctx, qw, wscr)
    emit_barrier(ctx)

    def ffn0():
        emit_ln(ctx)
        pre_og, extra = emit_router_lora(ctx, wr, A_in, B_in)
        done_og = set()

        def pieces_ap(og, j):
            if og not in done_og:
                done_og.add(og)
                pre_og(og)
            return wb[og, j]

        def tail(og, acc, rdy):
            emit_ffn_tail(ctx, og, acc, rdy, 0.5, c.b05)
        emit_dense_fm(ctx, pieces_ap, D, tail, extra_mm=extra)
        emit_barrier(ctx)

    def ffn1():
        emit_ln(ctx, rowsum=True)
        zrow_fn, rsrow_fn = emit_zrow_staging(ctx, rowscr)

        def tail(og, acc, rdy):
            emit_ffn_tail(ctx, og, acc, rdy, c.s05, c.b05L1, dual_ev=False)
        emit_dense_fm(ctx, lambda og, j: wscr[og, j], D, tail,
                      rank1=(zrow_fn, rsrow_fn))
        emit_barrier(ctx)

    def attn(layer):
        nkv = 32 if layer == 0 else 8
        causal = (layer == 1)
        wqx = wq0 if layer == 0 else wq1
        wkx = wk0 if layer == 0 else wk1
        wvx = wv0 if layer == 0 else wv1
        wox = wo0 if layer == 0 else wo1
        ki = k0i if layer == 0 else k1i
        ko = k0o if layer == 0 else k1o
        vi = v0i if layer == 0 else v1i
        vo = v0o if layer == 0 else v1o
        emit_ln(ctx)
        if layer == 1:
            p.wait("gpsimd", s["sT"].at(), s["sV"].at())
            p.g("dma_start", out=c.scr.ap()[:, 0:1024], in_=cos_in,
                inc=[s["sDW"].inc(16)])
            p.g("dma_start", out=c.scr.ap()[:, 3072:4096], in_=sin_in,
                inc=[s["sDW"].inc(16)])
            p.wait("vector", s["sDW"].at())
        rope = (layer == 1)
        emit_dense_fm(ctx, lambda og, j: wkx[og, j], nkv * 128,
                      lambda og, acc, rdy: emit_qkv_tail(ctx, og, acc, rdy,
                                                         ki, rope=rope))
        p.wait("gpsimd", s["sB0"].at(), s["sB1"].at())
        p.g("collective_compute", "AllGather", ALU.bypass,
            replica_groups=rg, ins=[ki], outs=[ko], inc=[s["sG"].inc()])
        kg_done = s["sG"].at()
        emit_dense_fm(ctx, lambda og, j: wvx[og, j], nkv * 128,
                      lambda og, acc, rdy: emit_qkv_tail(ctx, og, acc, rdy,
                                                         vi, rope=False))
        p.wait("gpsimd", s["sB0"].at(), s["sB1"].at())
        p.g("collective_compute", "AllGather", ALU.bypass,
            replica_groups=rg, ins=[vi], outs=[vo], inc=[s["sG"].inc()])
        vg_done = s["sG"].at()
        emit_dense_fm(ctx, lambda og, j: wqx[og, j], D,
                      lambda og, acc, rdy: emit_qkv_tail(ctx, og, acc, rdy,
                                                         qb, rope=rope))
        p.wait("gpsimd", (s["sT"].h, s["sT"].n))
        if causal:
            p.g("dma_start", out=c.big.ap()[:, 28928:29440], in_=mw_in,
                inc=[s["sDW"].inc(16)])
        p.wait("gpsimd", kg_done, vg_done, s["sB0"].at(), s["sB1"].at())
        p.wait("vector", s["sDW"].at(), (s["sT"].h, s["sT"].n))
        p.v("memset", c.big.ap()[0:1, 30208:30336], 1.0, inc=[s["sV"].inc()])
        p.wait("tensor", s["sV"].at())
        emit_attn_core(ctx, qb, ko, vo, aout, 32, nkv, causal)
        emit_barrier(ctx)
        p.wait("sync", s["sO0"].at(), s["sO1"].at(), s["sV"].at())
        for g4 in range(8):
            sr = aout[g4 * 512:(g4 + 1) * 512, :].rearrange(
                "(e p) t -> p e t", p=128)
            dsx = c.big.ap().rearrange("p (e t) -> p e t", e=KC)[
                :, g4 * 4:(g4 + 1) * 4, :]
            p.y("dma_start", out=dsx, in_=sr, inc=[s["sDX"].inc(16)])
        p.wait("tensor", s["sDX"].at())
        emit_dense_tm(ctx, lambda og, j: wox[og, j], c.big.ap(), D)
        emit_barrier(ctx)

    ffn0()
    attn(0)
    ffn0()
    ffn1()
    attn(1)
    ffn1()
    emit_stream_store(ctx, x_out)
    with nc.Block() as block:
        p.replay(block)
    es.close()
    return nc


def fused_inputs(inputs):
    ident = np.eye(128, dtype=np.float32)
    st = ffn_l0_inputs(inputs)
    st = {"wbase": st["wbase"], "b_arr0": st["b_arr"], "wr": st["wr"],
          "A_in": st["A_in"], "B_in": st["B_in"], "E_in": st["E_in"],
          "id_in": ident}
    st["wq0"] = arrange_pieces(np.ascontiguousarray(inputs["L0_attn_w_q"].T))
    st["wk0"] = arrange_pieces(np.ascontiguousarray(inputs["L0_attn_w_k"].T))
    st["wv0"] = arrange_pieces(np.ascontiguousarray(inputs["L0_attn_w_v"].T))
    st["wo0"] = arrange_pieces(np.ascontiguousarray(inputs["L0_attn_w_o"].T),
                               opp=512, kpp=2)
    st["wq1"] = arrange_pieces(np.ascontiguousarray(inputs["L1_attn_Wq"]))
    st["wk1"] = arrange_pieces(np.ascontiguousarray(inputs["L1_attn_Wk"]))
    st["wv1"] = arrange_pieces(np.ascontiguousarray(inputs["L1_attn_Wv"]))
    st["wo1"] = arrange_pieces(np.ascontiguousarray(inputs["L1_attn_Wo"]),
                               opp=512, kpp=2)
    st["qw"] = np.ascontiguousarray(inputs["L1_ffn_qweight"])
    st["s_arr"] = cols128(np.ascontiguousarray(inputs["L1_ffn_weight_scale"][:, 0]))
    st["b_arr1"] = cols128(np.ascontiguousarray(inputs["L1_ffn_bias"]))
    st["z_arr"] = cols128(np.ascontiguousarray(inputs["L1_ffn_weight_zero_point"][:, 0]))
    return st


def kernel(**inputs):
    inputs = {k: np.asarray(v) for k, v in inputs.items()}
    x = inputs["L0_attn_x"]
    rows = [core_rows(c) for c in range(8)]
    xs = [np.ascontiguousarray(x[c // 4][rows[c]]) for c in range(8)]
    st = fused_inputs(inputs)
    cosf, sinf = rope_tables()
    nc = _NC_CACHE.get("fused")
    if nc is None:
        nc = build_fused()
        _NC_CACHE["fused"] = nc
    in_maps = [dict(st, x_in=xs[i],
                    cos_in=np.ascontiguousarray(cosf[:, rows[i]]),
                    sin_in=np.ascontiguousarray(sinf[:, rows[i]]),
                    mw_in=maskwin_host(i % 4)) for i in range(8)]
    res = run_bass_kernel_spmd(nc, in_maps, core_ids=list(range(8)))
    out = np.empty((2, 4096, 4096), dtype=np.float32)
    for cc in range(8):
        out[cc // 4][rows[cc]] = res.results[cc]["x_out"]
    return out


